# version 03
# baseline (speedup 1.0000x reference)
import numpy as np
import ml_dtypes
import concourse.bass as bass
import concourse.mybir as mybir
from concourse.bass_utils import run_bass_kernel_spmd

F32 = mybir.dt.float32
BF16 = mybir.dt.bfloat16
AF = mybir.ActivationFunctionType
ALU = mybir.AluOpType
AX = mybir.AxisListType.X

NC = 8
D, N, DFF, E = 1024, 16, 4096, 8
R, TR, TS, DC, NF, Q = 4, 4096, 512, 8, 32, 1024


class Sched:
    def __init__(self):
        self.q = {k: [] for k in ("sync", "vector", "scalar", "tensor", "gpsimd")}
        self.cnt = {}
        self.gseq = 0
        self.cnt_log = []

    def inc(self, sem, n=1):
        self.cnt[sem] = self.cnt.get(sem, 0) + n
        return self.cnt[sem]

    def at(self, sem):
        return self.cnt.get(sem, 0)

    def on(self, eng, fn):
        self.q[eng].append((self.gseq, fn))
        self.gseq += 1
        self.cnt_log.append(dict(self.cnt))

    def snapshot(self):
        return (self.gseq, dict(self.cnt))


def build_nc(stage=None):
    nc = bass.Bass("TRN2", target_bir_lowering=False)

    d_x = nc.dram_tensor("x_my", [R, TS, D], F32, kind="ExternalInput").ap()
    d_wdtTk = nc.dram_tensor("wdtTk", [D, 128], F32, kind="ExternalInput").ap()
    d_wbcT = nc.dram_tensor("wbcT", [D, 2 * N], F32, kind="ExternalInput").ap()
    d_sel = nc.dram_tensor("sel", [D, 128], BF16, kind="ExternalInput").ap()
    d_selT = nc.dram_tensor("selT", [128, D], BF16, kind="ExternalInput").ap()
    d_identb = nc.dram_tensor("identb", [128, 128], BF16, kind="ExternalInput").ap()
    d_identf = nc.dram_tensor("identf", [128, 128], F32, kind="ExternalInput").ap()
    d_small = nc.dram_tensor("small", [128, 32], F32, kind="ExternalInput").ap()
    d_vecs = nc.dram_tensor("vecs", [1, 4 * D], F32, kind="ExternalInput").ap()
    d_b1c = nc.dram_tensor("b1c", [128, NF], F32, kind="ExternalInput").ap()
    d_sbupc = nc.dram_tensor("sbupc", [128, NF], F32, kind="ExternalInput").ap()
    d_upbc = nc.dram_tensor("upbc", [128, E * NF], F32, kind="ExternalInput").ap()
    d_downb = nc.dram_tensor("downb", [E, D], F32, kind="ExternalInput").ap()
    d_gatew = nc.dram_tensor("gatew", [D, E], F32, kind="ExternalInput").ap()
    d_w1c = nc.dram_tensor("w1c", [NF, D, 128], F32, kind="ExternalInput").ap()
    d_w2T = nc.dram_tensor("w2T", [DFF, D], F32, kind="ExternalInput").ap()
    d_swupc = nc.dram_tensor("swupc", [NF, D, 128], F32, kind="ExternalInput").ap()
    d_swdn = nc.dram_tensor("swdn", [DFF, D], F32, kind="ExternalInput").ap()
    d_upwc = nc.dram_tensor("upwc", [E, NF, D, 128], F32, kind="ExternalInput").ap()
    d_dnw = nc.dram_tensor("dnw", [E, DFF, D], F32, kind="ExternalInput").ap()
    d_out = nc.dram_tensor("out_my", [R, TS, D], F32, kind="ExternalOutput").ap()

    d_xa = nc.dram_tensor("xa_stash", [R, TS, D], F32, kind="Internal").ap()
    d_bvcd = nc.dram_tensor("bvcd", [2 * N, Q], F32, kind="Internal").ap()
    d_x1s = nc.dram_tensor("x1_stash", [R, TS, D], F32, kind="Internal").ap()
    cin1 = [nc.dram_tensor(f"cin1_{r}", [D, TS], BF16, kind="Internal").ap() for r in range(R)]
    cout1 = [nc.dram_tensor(f"cout1_{r}", [NC, D, TS], BF16, kind="Internal").ap() for r in range(R)]
    cin2 = [nc.dram_tensor(f"cin2_{r}", [TR, D], BF16, kind="Internal").ap() for r in range(R)]
    cout2 = [nc.dram_tensor(f"cout2_{r}", [TS, D], BF16, kind="Internal").ap() for r in range(R)]
    RG = [list(range(NC))]

    S = Sched()
    from contextlib import ExitStack
    ctx = ExitStack()

    def sbt(name, shape, dt):
        return ctx.enter_context(nc.sbuf_tensor(name, shape, dt))

    t_identb = sbt("t_identb", [128, 128], BF16)
    t_identf = sbt("t_identf", [128, 128], F32)
    t_sel = sbt("t_sel", [128, DC, 128], BF16)
    t_selT = sbt("t_selT", [128, D], BF16)
    t_small = sbt("t_small", [128, 32], F32)
    t_n1rep = sbt("t_n1rep", [128, D], BF16)
    t_n2rep = sbt("t_n2rep", [128, D], BF16)
    t_downb = sbt("t_downb", [E, D], BF16)
    t_onesb = sbt("t_onesb", [1, 128], BF16)
    t_onesf = sbt("t_onesf", [1, 128], F32)
    t_Arow = sbt("t_Arow", [1, N], F32)
    t_Acol = sbt("t_Acol", [N, 1], F32)
    t_invA = sbt("t_invA", [2 * N, 1], F32)
    t_Arep = sbt("t_Arep", [128, N], F32)
    t_b1c = sbt("t_b1c", [128, NF], F32)
    t_sbupc = sbt("t_sbupc", [128, NF], F32)
    t_upbc = sbt("t_upbc", [128, E * NF], F32)
    t_wdtTk = sbt("t_wdtTk", [128, DC, 128], BF16)
    t_wbcT = sbt("t_wbcT", [128, DC, 2 * N], BF16)
    t_gatew = sbt("t_gatew", [128, DC, E], BF16)
    t_stats = sbt("t_stats", [128, 8 * R], F32)
    t_state = sbt("t_state", [128, 2 * N], F32)
    t_wtok = sbt("t_wtok", [128, 16 * E], F32)
    t_col = sbt("t_col", [128, 8], F32)

    AW = 48800
    t_arena = ctx.enter_context(nc.sbuf_tensor("arena", [128, AW], F32))
    arena_bf = t_arena.bitcast(BF16)

    def af(off, w):
        assert off + w <= AW, (off, w)
        return t_arena[:, off:off + w]

    def ab(off, wbf):
        assert 2 * off + wbf <= 2 * AW, (off, wbf)
        return arena_bf[:, 2 * off:2 * off + wbf]

    o_xffm = 0
    o_wfm = DC * 1024
    o_wbf = o_wfm + 2048
    o_vrow = o_wbf + 1024
    o_vrowb = o_vrow + 4096
    P0_END = o_vrowb + 1024
    o_rhs = P0_END
    o_dt = o_rhs + 4096
    o_rdt = o_dt + Q
    o_g = o_rdt + Q
    o_xnk = o_g + Q
    o_dA = o_xnk + 512
    o_bsc = o_dA + Q
    o_G = o_bsc + Q
    o_h = o_G + Q
    o_tq = o_h + Q
    o_yacc = o_tq + Q
    o_bvc = o_yacc + Q
    o_ypl = o_bvc + 1024
    P2_END = o_ypl + 512
    p13 = P2_END
    o_xs = p13
    o_wk1 = p13 + 4096
    o_wk2 = p13 + 5120
    o_xntok = p13 + 5128
    o_xnfm = p13 + 5640
    o_ysb = p13
    o_x1 = p13 + 512
    o_xa1 = p13 + 2560
    o_xn2 = p13 + 3584
    o_xn2fm = p13 + 4096
    o_h1 = p13 + 6144
    o_wbuf = p13 + 14336
    o_w2buf = p13 + 15360
    o_xf = o_xn2
    o_x1e = o_h1
    o_wkA = o_h1 + 2048
    o_sc = o_h1 + 3072
    o_xs3 = o_h1 + 6144
    P3_END = p13 + 16384
    assert P3_END <= AW, P3_END
    p4 = P0_END
    o_hact = p4
    o_upb = p4 + 16384
    o_dnb = p4 + 17408
    o_comb = p4 + 18432
    o_fin = p4 + 26624
    o_xar = p4 + 27648
    P4_END = p4 + 29696
    assert P4_END <= AW, P4_END

    p_mm = [ctx.enter_context(nc.psum_tensor(f"p_mm{i}", [128, 512], F32)) for i in range(8)]
    p_trb = p_mm[6].bitcast(BF16)
    p_trf = p_mm[7]
    p_plb = p_mm[7].bitcast(BF16)

    semnames = ["ld0", "cast0", "setup", "sx", "svar", "srsq", "sxn", "strA", "strAe",
                "scin", "sxg1", "srhs", "swdt", "sdt", "sbvp", "sbv", "sselp", "sxnk",
                "srdt", "sg", "sorB", "sorC", "sGn", "sdA", "sbn", "sh", "stq", "syq",
                "sye", "spl", "sple", "sciny", "sxg2",
                "syld", "sx2", "sx1", "srsq2", "sxn2", "strC", "sxn2e", "sx1w",
                "sw1s", "sw1", "sh1", "sw2s", "sw2", "sy2", "sx1e", "sxaw",
                "srsq3", "sxf", "strD", "sxfe", "sgtp", "sgt", "swv", "swTp", "swT",
                "sups", "sup", "sha", "sdns", "sdn", "scmb", "sxar", "sfin", "sout", "sbvd", "sbcB", "sbcC", "sw1sA", "sw1sB", "sw2sA", "sw2sB", "supsA", "supsB", "sdnsA", "sdnsB", "scin0", "scin1", "scin2", "scin3", "sx2A", "sx2B", "sx1wA", "sx1wB", "sx3A", "sx3B", "sx1eA", "sx1eB", "sxaL", "sxarA", "sxarB"]
    sems = {}
    for nm in semnames:
        sems[nm] = ctx.enter_context(nc.semaphore(nm))

    blk = ctx.enter_context(nc.Block())

    def W(e, sem, val):
        if val > 0:
            e.wait_ge(sems[sem], val)

    N_LD0 = S.inc("ld0", 16 * 7)

    def su_sync(e):
        for src, dst in [(d_identb[:], t_identb[:]), (d_identf[:], t_identf[:]),
                         (d_small[:], t_small[:]),
                         (d_vecs[:], t_arena[0:1, o_vrow:o_vrow + 4 * D]),
                         (d_b1c[:], t_b1c[:]), (d_sbupc[:], t_sbupc[:]),
                         (d_upbc[:], t_upbc[:])]:
            e.dma_start(out=dst, in_=src).then_inc(sems["ld0"], 16)
    S.on("sync", su_sync)

    N_CAST0 = S.inc("cast0", 16 * 6)

    def su_gp(e):
        for src, dst in [
            (d_sel.rearrange("(c p) j -> p c j", p=128), t_sel[:, :, :]),
            (d_selT[:], t_selT[:]),
            (d_wdtTk.rearrange("(c p) j -> p c j", p=128), t_wdtTk[:, :, :]),
            (d_wbcT.rearrange("(c p) j -> p c j", p=128), t_wbcT[:, :, :]),
            (d_gatew.rearrange("(c p) j -> p c j", p=128), t_gatew[:, :, :]),
            (d_downb[:], t_downb[:]),
        ]:
            e.dma_start(out=dst, in_=src).then_inc(sems["cast0"], 16)
    S.on("gpsimd", su_gp)

    n1 = S.inc("setup")

    def su_s(e):
        W(e, "ld0", N_LD0)
        e.activation(out=t_Arow[:], in_=t_small[0:1, 9:9 + N], func=AF.Exp).then_inc(sems["setup"], 1)
    S.on("scalar", su_s)

    n2 = S.inc("setup")

    def su_v(e, n1=n1):
        W(e, "setup", n1)
        e.tensor_scalar_mul(out=t_Arow[:], in0=t_Arow[:], scalar1=-1.0)
        e.memset(t_onesb[:], 1.0)
        e.memset(t_onesf[:], 1.0)
        e.memset(t_state[:], 0.0)
        e.tensor_copy(out=ab(o_vrowb, 2 * D)[0:1, :],
                      in_=af(o_vrow + 2 * D, 2 * D)[0:1, :]).then_inc(sems["setup"], 1)
    S.on("vector", su_v)

    tb0 = S.at("setup")
    S.inc("setup", 6)

    def su_t(e, n2=n2):
        W(e, "setup", n2)
        W(e, "cast0", N_CAST0)
        e.matmul(p_trf[:, 0:N], t_onesf[:], t_Arow[:], start=True, stop=True).then_inc(sems["setup"], 1)
        e.transpose(p_trf[0:N, 128:129], t_Arow[:], t_identf[0:1, 0:1]).then_inc(sems["setup"], 1)
        for j in range(2):
            e.matmul(p_mm[2 * j][:], t_onesf[:],
                     af(o_vrow, 4 * D)[0:1, 512 * j:512 * (j + 1)],
                     start=True, stop=True).then_inc(sems["setup"], 1)
            e.matmul(p_mm[2 * j + 1][:], t_onesf[:],
                     af(o_vrow, 4 * D)[0:1, D + 512 * j:D + 512 * (j + 1)],
                     start=True, stop=True).then_inc(sems["setup"], 1)
    S.on("tensor", su_t)

    SETUP_DONE = S.inc("setup")

    def su_v2(e, tb0=tb0):
        W(e, "setup", tb0 + 2)
        e.tensor_copy(out=t_Arep[:], in_=p_trf[:, 0:N])
        W(e, "setup", tb0 + 2)
        e.tensor_copy(out=t_Acol[:], in_=p_trf[0:N, 128:129])
        e.memset(t_invA[:], 1.0)
        e.reciprocal(out=t_invA[0:N, :], in_=t_Acol[:])
        for j in range(2):
            W(e, "setup", tb0 + 3 + 2 * j)
            e.tensor_copy(out=t_n1rep[:, 512 * j:512 * (j + 1)], in_=p_mm[2 * j][:])
            W(e, "setup", tb0 + 4 + 2 * j)
            e.tensor_copy(out=t_n2rep[:, 512 * j:512 * (j + 1)], in_=p_mm[2 * j + 1][:])
        e.tensor_copy(out=t_col[:, 7:8], in_=t_n1rep[:, 0:1]).then_inc(sems["setup"], 1)
    S.on("vector", su_v2)

    for r in range(R):
        nxn_prev = 4 * r
        nsx_base = S.at("sx")
        S.inc("sx", 64)

        def f_xld(e, r=r, nxn_prev=nxn_prev):
            W(e, "sxn", nxn_prev)
            for i in range(4):
                e.dma_start(out=af(o_xs + 1024 * i, 1024),
                            in_=d_x[r, 128 * i:128 * (i + 1), :]).then_inc(sems["sx"], 16)
        S.on("sync", f_xld)

        for i in range(4):
            sxval = nsx_base + 64
            nvar = S.inc("svar")

            def f_v(e, r=r, i=i, sxval=sxval):
                if r == 0 and i == 0:
                    W(e, "setup", SETUP_DONE)
                W(e, "sx", sxval)
                xs = af(o_xs + 1024 * i, 1024)
                st = t_stats[:, 2 * (4 * r + i):2 * (4 * r + i) + 1]
                e.tensor_mul(out=af(o_wk1, 1024), in0=xs, in1=xs)
                e.reduce_sum(out=af(o_wk2, 1), in_=xs, axis=AX)
                e.reduce_sum(out=af(o_wk2 + 1, 1), in_=af(o_wk1, 1024), axis=AX)
                e.tensor_scalar_mul(out=st, in0=af(o_wk2, 1), scalar1=1.0 / D)
                e.tensor_mul(out=af(o_wk2 + 2, 1), in0=st, in1=st)
                e.scalar_tensor_tensor(out=af(o_wk2 + 3, 1), in0=af(o_wk2 + 1, 1),
                                       scalar=1.0 / D, in1=af(o_wk2 + 2, 1),
                                       op0=ALU.mult, op1=ALU.subtract).then_inc(sems["svar"], 1)
            S.on("vector", f_v)

            nrsq = S.inc("srsq")

            def f_s(e, r=r, i=i, nvar=nvar):
                W(e, "svar", nvar)
                var = af(o_wk2 + 3, 1)
                rst = t_stats[:, 2 * (4 * r + i) + 1:2 * (4 * r + i) + 2]
                e.activation(out=t_col[:, 0:1], in_=var, func=AF.Ln, bias=t_small[:, 6:7])
                e.activation(out=rst, in_=t_col[:, 0:1], func=AF.Exp, scale=-0.5)
                e.activation(out=t_col[:, 1:2], in_=var, func=AF.Ln, scale=1.000001,
                             bias=t_small[:, 7:8])
                e.activation(out=t_col[:, 2:3], in_=t_col[:, 1:2], func=AF.Exp,
                             scale=-0.5).then_inc(sems["srsq"], 1)
            S.on("scalar", f_s)

            nxn = S.inc("sxn")

            def f_v2(e, r=r, i=i, nrsq=nrsq):
                W(e, "srsq", nrsq)
                xs = af(o_xs + 1024 * i, 1024)
                st = t_stats[:, 2 * (4 * r + i):2 * (4 * r + i) + 1]
                e.tensor_scalar(out=af(o_wk1, 1024), in0=xs, scalar1=st,
                                scalar2=t_col[:, 2:3], op0=ALU.subtract, op1=ALU.mult)
                e.tensor_mul(out=ab(o_xntok, 1024), in0=af(o_wk1, 1024),
                             in1=t_n1rep[:]).then_inc(sems["sxn"], 1)
            S.on("vector", f_v2)

            for j in range(DC):
                ktr = S.inc("strA")

                def f_t(e, j=j, nxn=nxn, ktr=ktr):
                    if j == 0:
                        W(e, "sxn", nxn)
                    if ktr > 2:
                        W(e, "strAe", ktr - 2)
                    bk = p_trb if (ktr - 1) % 2 == 0 else p_plb
                    e.transpose(bk[:, 0:128],
                                ab(o_xntok, 1024)[:, 128 * j:128 * (j + 1)],
                                t_identb[:]).then_inc(sems["strA"], 1)
                S.on("tensor", f_t)

                S.inc("strAe")

                def f_ve(e, r=r, i=i, j=j, ktr=ktr):
                    W(e, "strA", ktr)
                    if i == 0 and j == 0 and r >= 1:
                        W(e, f"scin{r - 1}", 128)
                    bk = p_trb if (ktr - 1) % 2 == 0 else p_plb
                    e.tensor_copy(out=ab(o_xnfm + 256 * j, 512)[:, 128 * i:128 * (i + 1)],
                                  in_=bk[:, 0:128]).then_inc(sems["strAe"], 1)
                S.on("vector", f_ve)

        nevac = S.at("strAe")
        cin_sem = f"scin{r}"
        S.inc(cin_sem, 16 * 8)

        def f_cin(e, r=r, nevac=nevac, cin_sem=cin_sem):
            W(e, "strAe", nevac)
            for j in range(DC):
                e.dma_start(out=cin1[r][128 * j:128 * (j + 1), :],
                            in_=ab(o_xnfm + 256 * j, 512)).then_inc(sems[cin_sem], 16)
        S.on("sync", f_cin)

        S.inc("sxg1")

        def f_xg1(e, r=r, cin_sem=cin_sem):
            W(e, cin_sem, 128)
            e.collective_compute("AllGather", ALU.bypass, replica_groups=RG,
                                 ins=[cin1[r][:]], outs=[cout1[r][:]]).then_inc(sems["sxg1"], 1)
        S.on("gpsimd", f_xg1)

    SNAP_XN = S.snapshot()
    ROUND_SNAPS = []
    sdt_guard = [0, 0]
    SY2_GUARD = 0
    for r in range(R):
        ROUND_SNAPS.append(S.snapshot())
        xg_sh1 = S.at("sh1")
        xg_sy2 = S.at("sy2")
        xg_sgt = S.at("sgt")
        xg_swT = S.at("swT")
        xg_setup = SETUP_DONE if r == 0 else 0
        for qq in range(4):
            h_half = qq // 2
            rk0 = 4 * h_half + 2 * (qq % 2)
            g_wdt = S.at("swdt")
            g_bvp = S.at("sbvp")
            g_selp = S.at("sselp")
            nrhs = S.at("srhs") + 16 * DC * 2
            S.inc("srhs", 16 * DC * 2)

            def f_rhs(e, r=r, rk0=rk0, g_wdt=g_wdt, g_bvp=g_bvp, g_selp=g_selp):
                W(e, "sxg1", r + 1)
                W(e, "swdt", g_wdt)
                W(e, "sbvp", g_bvp)
                W(e, "sselp", g_selp)
                for j in range(DC):
                    for u in range(2):
                        e.dma_start(out=ab(o_rhs + 512 * j, 1024)[:, 512 * u:512 * (u + 1)],
                                    in_=cout1[r][rk0 + u, 128 * j:128 * (j + 1), :]
                                    ).then_inc(sems["srhs"], 16)
            S.on("sync", f_rhs)

            for tt in range(2):
                g_sdt = sdt_guard[tt]
                nwdt = S.inc("swdt")

                first_q = (qq == 0 and tt == 0)

                def f_wdt(e, tt=tt, nrhs=nrhs, g_sdt=g_sdt, first_q=first_q,
                          xg_sh1=xg_sh1, xg_setup=xg_setup):
                    W(e, "srhs", nrhs)
                    W(e, "sdt", g_sdt)
                    if first_q:
                        W(e, "setup", xg_setup)
                        W(e, "sh1", xg_sh1)
                    for j in range(DC - 1):
                        e.matmul(p_mm[tt][:], t_wdtTk[:, j, :],
                                 ab(o_rhs + 512 * j, 1024)[:, 512 * tt:512 * (tt + 1)],
                                 start=(j == 0), stop=False)
                    e.matmul(p_mm[tt][:], t_wdtTk[:, DC - 1, :],
                             ab(o_rhs + 512 * (DC - 1), 1024)[:, 512 * tt:512 * (tt + 1)],
                             start=False, stop=True).then_inc(sems["swdt"], 1)
                S.on("tensor", f_wdt)

                g_sdt_all = S.at("sdt")
                g_syq_all = S.at("syq")
                ndtv = S.inc("sdt")
                sdt_guard[tt] = ndtv

                def f_vsp(e, tt=tt, nwdt=nwdt, g_sdt_all=g_sdt_all, g_syq_all=g_syq_all):
                    W(e, "swdt", nwdt)
                    if tt == 0:
                        W(e, "sdt", g_sdt_all)
                        W(e, "syq", g_syq_all)
                    e.tensor_scalar(out=af(o_tq + 512 * tt, 512), in0=p_mm[tt][:],
                                    scalar1=t_small[:, 1:2], scalar2=80.0,
                                    op0=ALU.add, op1=ALU.min).then_inc(sems["sdt"], 1)
                S.on("vector", f_vsp)

            for tt in range(2):
                g_bve = S.at("sbv")
                nbvp = S.inc("sbvp")

                def f_bv(e, tt=tt, qq=qq, nrhs=nrhs, g_bve=g_bve, xg_sgt=xg_sgt,
                         xg_sy2=xg_sy2, xg_setup=xg_setup):
                    W(e, "srhs", nrhs)
                    W(e, "sbv", g_bve)
                    if qq == 0 and tt == 0:
                        W(e, "sgt", xg_sgt)
                        W(e, "sy2", xg_sy2)
                        W(e, "setup", xg_setup)
                    for j in range(DC - 1):
                        e.matmul(p_mm[2][0:32, :], t_wbcT[:, j, :],
                                 ab(o_rhs + 512 * j, 1024)[:, 512 * tt:512 * (tt + 1)],
                                 start=(j == 0), stop=False)
                    e.matmul(p_mm[2][0:32, :], t_wbcT[:, DC - 1, :],
                             ab(o_rhs + 512 * (DC - 1), 1024)[:, 512 * tt:512 * (tt + 1)],
                             start=False, stop=True).then_inc(sems["sbvp"], 1)
                S.on("tensor", f_bv)

                S.inc("sbv")

                def f_bve(e, tt=tt, nbvp=nbvp):
                    W(e, "sbvp", nbvp)
                    dst = af(o_bvc, 1024)
                    e.tensor_scalar(out=dst[0:32, 512 * tt:512 * (tt + 1)],
                                    in0=p_mm[2][0:32, :], scalar1=t_small[0:32, 4:5],
                                    scalar2=t_invA[:], op0=ALU.add,
                                    op1=ALU.mult).then_inc(sems["sbv"], 1)
                S.on("vector", f_bve)

            g_spl = S.at("spl")
            for tt in range(2):
                g_sg = S.at("sg")
                nselp = S.inc("sselp")

                g_xnk = S.at("sxnk")

                def f_selm(e, tt=tt, qq=qq, nrhs=nrhs, g_sg=g_sg, xg_sy2=xg_sy2,
                           xg_setup=xg_setup, g_xnk=g_xnk):
                    W(e, "srhs", nrhs)
                    W(e, "sg", g_sg)
                    W(e, "sxnk", g_xnk)
                    if qq == 0 and tt == 0:
                        W(e, "sy2", xg_sy2)
                        W(e, "setup", xg_setup)
                    for j in range(DC - 1):
                        e.matmul(p_mm[3][:], t_sel[:, j, :],
                                 ab(o_rhs + 512 * j, 1024)[:, 512 * tt:512 * (tt + 1)],
                                 start=(j == 0), stop=False)
                    e.matmul(p_mm[3][:], t_sel[:, DC - 1, :],
                             ab(o_rhs + 512 * (DC - 1), 1024)[:, 512 * tt:512 * (tt + 1)],
                             start=False, stop=True).then_inc(sems["sselp"], 1)
                S.on("tensor", f_selm)

                S.inc("sxnk")

                def f_sele(e, tt=tt, nselp=nselp, g_spl=g_spl):
                    W(e, "sselp", nselp)
                    W(e, "spl", g_spl)
                    e.tensor_copy(out=ab(o_xnk, 1024)[:, 512 * tt:512 * (tt + 1)],
                                  in_=p_mm[3][:]).then_inc(sems["sxnk"], 1)
                S.on("vector", f_sele)

            ndt_now = S.at("sdt")
            g_rdt_prev = S.at("srdt") - 1 if S.at("srdt") >= 1 else 0
            nsp = S.inc("sdt")

            def f_ssp(e, ndt_now=ndt_now, g_rdt_prev=g_rdt_prev):
                W(e, "sdt", ndt_now)
                W(e, "srdt", g_rdt_prev)
                e.activation(out=af(o_dt, Q), in_=af(o_tq, Q), func=AF.Exp)
                e.activation(out=af(o_dt, Q), in_=af(o_dt, Q), func=AF.Ln,
                             bias=1.0).then_inc(sems["sdt"], 1)
            S.on("scalar", f_ssp)

            nrdt1 = S.inc("srdt")

            def f_vmax(e, nsp=nsp):
                W(e, "sdt", nsp)
                e.tensor_scalar_max(out=af(o_rdt, Q), in0=af(o_dt, Q),
                                    scalar1=1.0e-20).then_inc(sems["srdt"], 1)
            S.on("vector", f_vmax)

            nrdt2 = S.inc("srdt")

            def f_srdt(e, nrdt1=nrdt1):
                W(e, "srdt", nrdt1)
                e.activation(out=af(o_rdt, Q), in_=af(o_rdt, Q), func=AF.Ln)
                e.activation(out=af(o_rdt, Q), in_=af(o_rdt, Q), func=AF.Exp,
                             scale=-1.0).then_inc(sems["srdt"], 1)
            S.on("scalar", f_srdt)

            nxnk_now = S.at("sxnk")
            ngm = S.inc("sg")
            ng = S.inc("sg")

            def f_vg(e, nrdt2=nrdt2, nxnk_now=nxnk_now):
                W(e, "srdt", nrdt2)
                W(e, "sxnk", nxnk_now)
                e.tensor_mul(out=af(o_g, Q), in0=ab(o_xnk, 1024),
                             in1=af(o_rdt, Q)).then_inc(sems["sg"], 1)
                e.memset(af(o_yacc, Q), 0.0).then_inc(sems["sg"], 1)
            S.on("vector", f_vg)

            nbv_now = S.at("sbv")
            g_bcB_prev = S.at("sbcB")
            g_bcC_prev = S.at("sbcC")
            nbvd = S.at("sbvd") + 16
            S.inc("sbvd", 16)

            def f_bvd(e, nbv_now=nbv_now, g_bcB_prev=g_bcB_prev, g_bcC_prev=g_bcC_prev):
                W(e, "sbv", nbv_now)
                W(e, "sbcB", g_bcB_prev)
                W(e, "sbcC", g_bcC_prev)
                e.dma_start(out=d_bvcd[:], in_=af(o_bvc, 1024)[0:32, :]).then_inc(sems["sbvd"], 16)
            S.on("sync", f_bvd)

            g_sye_prev = S.at("sye")
            for n_ in range(N):
                g_Gn = S.at("sGn")
                g_bn0 = S.at("sbn")
                nbcB = S.at("sbcB") + 16
                S.inc("sbcB", 16)

                def f_bcB(e, n_=n_, nbvd=nbvd, g_Gn=g_Gn, g_bn0=g_bn0):
                    W(e, "sbvd", nbvd)
                    W(e, "sGn", g_Gn)
                    W(e, "sbn", g_bn0)
                    e.dma_start(out=af(o_G, Q),
                                in_=d_bvcd[n_:n_ + 1, :].broadcast_to([128, Q])
                                ).then_inc(sems["sbcB"], 16)
                S.on("scalar", f_bcB)

                g_h = S.at("sh")
                ndA = S.inc("sdA")

                def f_sdA(e, n_=n_, nsp=nsp, g_h=g_h):
                    W(e, "sdt", nsp)
                    W(e, "sh", g_h)
                    e.activation(out=af(o_dA, Q), in_=af(o_dt, Q), func=AF.Exp,
                                 scale=t_Arep[:, n_:n_ + 1])
                    e.activation(out=af(o_bsc, Q), in_=af(o_dA, Q), func=AF.Copy,
                                 bias=-1.0).then_inc(sems["sdA"], 1)
                S.on("scalar", f_sdA)

                g_bn = S.at("sbn")
                nGn = S.inc("sGn")

                def f_vG(e, nbcB=nbcB, ngm=ngm, g_bn=g_bn):
                    W(e, "sbcB", nbcB)
                    W(e, "sg", ngm)
                    W(e, "sbn", g_bn)
                    e.tensor_mul(out=af(o_G, Q), in0=af(o_G, Q),
                                 in1=af(o_g, Q)).then_inc(sems["sGn"], 1)
                S.on("vector", f_vG)

                g_h2 = S.at("sh")
                nbn = S.inc("sbn")

                def f_gb(e, nGn=nGn, ndA=ndA, g_h2=g_h2):
                    W(e, "sGn", nGn)
                    W(e, "sdA", ndA)
                    W(e, "sh", g_h2)
                    e.tensor_mul(out=af(o_bsc, Q), in0=af(o_bsc, Q),
                                 in1=af(o_G, Q)).then_inc(sems["sbn"], 1)
                S.on("gpsimd", f_gb)

                g_tq2 = S.at("stq")
                nh = S.inc("sh")
                scol = 16 * h_half + n_

                def f_vscan(e, nbn=nbn, ndA=ndA, scol=scol, g_tq2=g_tq2):
                    W(e, "sbn", nbn)
                    W(e, "sdA", ndA)
                    W(e, "stq", g_tq2)
                    e.tensor_tensor_scan(out=af(o_h, Q), data0=af(o_dA, Q),
                                         data1=af(o_bsc, Q),
                                         initial=t_state[:, scol:scol + 1],
                                         op0=ALU.mult, op1=ALU.add)
                    e.tensor_copy(out=t_state[:, scol:scol + 1],
                                  in_=af(o_h + Q - 1, 1)).then_inc(sems["sh"], 1)
                S.on("vector", f_vscan)

                g_tq3 = S.at("stq")
                g_yq0 = S.at("syq")
                nbcC = S.at("sbcC") + 16
                S.inc("sbcC", 16)

                def f_bcC(e, n_=n_, nbvd=nbvd, g_tq3=g_tq3, g_yq0=g_yq0,
                          g_sye_prev=g_sye_prev):
                    W(e, "sbvd", nbvd)
                    W(e, "stq", g_tq3)
                    W(e, "syq", g_yq0)
                    if n_ == 0:
                        W(e, "sye", g_sye_prev)
                    e.dma_start(out=af(o_tq, Q),
                                in_=d_bvcd[N + n_:N + n_ + 1, :].broadcast_to([128, Q])
                                ).then_inc(sems["sbcC"], 16)
                S.on("scalar", f_bcC)

                g_yq = S.at("syq")
                ntq = S.inc("stq")

                def f_vt(e, nh=nh, nbcC=nbcC, g_yq=g_yq):
                    W(e, "sh", nh)
                    W(e, "sbcC", nbcC)
                    W(e, "syq", g_yq)
                    e.tensor_mul(out=af(o_tq, Q), in0=af(o_tq, Q),
                                 in1=af(o_h, Q)).then_inc(sems["stq"], 1)
                S.on("vector", f_vt)

                nyq = S.inc("syq")

                def f_gy(e, ntq=ntq, ng=ng):
                    W(e, "stq", ntq)
                    W(e, "sg", ng)
                    e.tensor_add(out=af(o_yacc, Q), in0=af(o_yacc, Q),
                                 in1=af(o_tq, Q)).then_inc(sems["syq"], 1)
                S.on("gpsimd", f_gy)

            nyq_all = S.at("syq")
            nye1 = S.inc("sye")
            nye = S.inc("sye")

            def f_vep(e, nyq_all=nyq_all):
                W(e, "syq", nyq_all)
                e.tensor_scalar_mul(out=af(o_tq, Q), in0=ab(o_xnk, 1024),
                                    scalar1=t_small[:, 2:3])
                e.tensor_add(out=af(o_yacc, Q), in0=af(o_yacc, Q), in1=af(o_tq, Q))
                e.tensor_scalar_mul(out=af(o_yacc, Q), in0=af(o_yacc, Q),
                                    scalar1=t_small[:, 3:4]).then_inc(sems["sye"], 1)
                e.tensor_copy(out=ab(o_xnk, 1024), in_=af(o_yacc, Q)).then_inc(sems["sye"], 1)
            S.on("vector", f_vep)

            for tb in range(8):
                tbh = (qq % 2) * 8 + tb
                owner = 4 * h_half + tbh // 4
                rowbase = 512 * owner + 128 * (tbh % 4)
                g_ple = S.at("sple")
                npl = S.inc("spl")

                def f_tpl(e, tb=tb, qq=qq, nye=nye, g_ple=g_ple,
                          xg_sy2=xg_sy2, xg_setup=xg_setup, xg_swT=xg_swT):
                    W(e, "sye", nye)
                    W(e, "sple", g_ple)
                    if qq == 0 and tb == 0:
                        W(e, "sy2", xg_sy2)
                        W(e, "swT", xg_swT)
                        W(e, "setup", xg_setup)
                    e.matmul(p_mm[6][:], ab(o_xnk, 1024)[:, 128 * tb:128 * (tb + 1)],
                             t_selT[:, 0:512], start=True, stop=True)
                    e.matmul(p_mm[7][:], ab(o_xnk, 1024)[:, 128 * tb:128 * (tb + 1)],
                             t_selT[:, 512:1024], start=True, stop=True).then_inc(sems["spl"], 1)
                S.on("tensor", f_tpl)

                g_cy = S.at("sciny")
                nple = S.inc("sple")

                def f_vpl(e, npl=npl, g_cy=g_cy):
                    W(e, "spl", npl)
                    W(e, "sciny", g_cy)
                    e.tensor_copy(out=ab(o_ypl, 1024)[:, 0:512], in_=p_mm[6][:])
                    e.tensor_copy(out=ab(o_ypl, 1024)[:, 512:1024],
                                  in_=p_mm[7][:]).then_inc(sems["sple"], 1)
                S.on("vector", f_vpl)

                nple2 = S.at("sple")
                S.inc("sciny", 16)
                firstpl = (qq == 0 and tb == 0)

                def f_dpl(e, r=r, rowbase=rowbase, nple2=nple2, firstpl=firstpl):
                    W(e, "sple", nple2)
                    if firstpl and r >= 1:
                        W(e, "sxg2", r)
                    e.dma_start(out=cin2[r][rowbase:rowbase + 128, :],
                                in_=ab(o_ypl, 1024)).then_inc(sems["sciny"], 16)
                S.on("sync", f_dpl)

        nciny_all = S.at("sciny")
        S.inc("sxg2")

        def f_xg2(e, r=r, nciny_all=nciny_all):
            W(e, "sciny", nciny_all)
            e.collective_compute("ReduceScatter", ALU.add, replica_groups=RG,
                                 ins=[cin2[r][:]], outs=[cout2[r][:]]).then_inc(sems["sxg2"], 1)
        S.on("gpsimd", f_xg2)

        for i in range(4):
            g_x1 = S.at("sx1")
            nyld = S.at("syld") + 16
            S.inc("syld", 16)

            def f_yld(e, r=r, i=i, g_x1=g_x1):
                W(e, "sxg2", r + 1)
                W(e, "sx1", g_x1)
                e.dma_start(out=ab(o_ysb, 1024),
                            in_=cout2[r][128 * i:128 * (i + 1), :]).then_inc(sems["syld"], 16)
            S.on("sync", f_yld)

            sl2 = "sx2A" if i % 2 == 0 else "sx2B"
            g_x1b = S.at("sx1") - 1 if S.at("sx1") >= 1 else 0
            nx2 = S.at(sl2) + 16
            S.inc(sl2, 16)

            def f_xrl(e, r=r, i=i, g_x1b=g_x1b, sl2=sl2):
                W(e, "sx1", g_x1b)
                e.dma_start(out=af(o_xs3 + 1024 * (i % 2), 1024),
                            in_=d_x[r, 128 * i:128 * (i + 1), :]).then_inc(sems[sl2], 16)
            S.on("sync", f_xrl)

            slw = "sx1wA" if i % 2 == 0 else "sx1wB"
            g_x1w = S.at(slw)
            nx1 = S.inc("sx1")

            def f_vx1(e, r=r, i=i, nyld=nyld, nx2=nx2, g_x1w=g_x1w, sl2=sl2, slw=slw):
                W(e, "syld", nyld)
                W(e, sl2, nx2)
                W(e, slw, g_x1w)
                st = t_stats[:, 2 * (4 * r + i):2 * (4 * r + i) + 2]
                x1s = af(o_x1 + 1024 * (i % 2), 1024)
                e.tensor_scalar(out=x1s, in0=af(o_xs3 + 1024 * (i % 2), 1024),
                                scalar1=st[:, 0:1], scalar2=st[:, 1:2],
                                op0=ALU.subtract, op1=ALU.mult)
                e.tensor_add(out=x1s, in0=x1s, in1=ab(o_ysb, 1024))
                e.tensor_mul(out=af(o_wk2, 1024) if False else af(o_xa1, 1024), in0=x1s, in1=x1s)
                e.reduce_sum(out=t_col[:, 3:4], in_=af(o_xa1, 1024), axis=AX).then_inc(sems["sx1"], 1)
            S.on("vector", f_vx1)

            nrsq2 = S.inc("srsq2")

            def f_srs2(e, nx1=nx1):
                W(e, "sx1", nx1)
                e.activation(out=t_col[:, 4:5], in_=t_col[:, 3:4], func=AF.Ln,
                             scale=1.0 / D, bias=t_small[:, 8:9])
                e.activation(out=t_col[:, 4:5], in_=t_col[:, 4:5], func=AF.Exp,
                             scale=-0.5).then_inc(sems["srsq2"], 1)
            S.on("scalar", f_srs2)

            nxn2 = S.inc("sxn2")

            def f_vxn2(e, i=i, nrsq2=nrsq2):
                W(e, "srsq2", nrsq2)
                x1s = af(o_x1 + 1024 * (i % 2), 1024)
                e.tensor_scalar_mul(out=af(o_xa1, 1024), in0=x1s, scalar1=t_col[:, 4:5])
                e.tensor_mul(out=ab(o_xn2, 1024), in0=af(o_xa1, 1024),
                             in1=t_n2rep[:]).then_inc(sems["sxn2"], 1)
            S.on("vector", f_vxn2)

            S.inc(slw, 16)

            def f_x1w(e, r=r, i=i, nxn2=nxn2, slw=slw):
                W(e, "sxn2", nxn2)
                e.dma_start(out=d_x1s[r, 128 * i:128 * (i + 1), :],
                            in_=af(o_x1 + 1024 * (i % 2), 1024)).then_inc(sems[slw], 16)
            S.on("sync", f_x1w)

            for j in range(DC):
                ktr = S.inc("strC")
                g_strAe = S.at("strAe")
                g_sple_r2 = S.at("sple")

                def f_t2(e, j=j, nxn2=nxn2, ktr=ktr, xg_sy2=xg_sy2, g_strAe=g_strAe,
                         g_sple_r2=g_sple_r2):
                    if j == 0:
                        W(e, "sxn2", nxn2)
                        if ktr == 1:
                            W(e, "strAe", g_strAe)
                    if ktr > 2:
                        W(e, "sxn2e", ktr - 2)
                    if ktr <= 8:
                        W(e, "sy2", xg_sy2)
                        W(e, "sple", g_sple_r2)
                    bk = p_trb if (ktr - 1) % 2 == 0 else p_plb
                    e.transpose(bk[:, 0:128],
                                ab(o_xn2, 1024)[:, 128 * j:128 * (j + 1)],
                                t_identb[:]).then_inc(sems["strC"], 1)
                S.on("tensor", f_t2)

                S.inc("sxn2e")

                def f_ve2(e, i=i, j=j, ktr=ktr):
                    W(e, "strC", ktr)
                    bk = p_trb if (ktr - 1) % 2 == 0 else p_plb
                    e.tensor_copy(out=ab(o_xn2fm + 256 * j, 512)[:, 128 * i:128 * (i + 1)],
                                  in_=bk[:, 0:128]).then_inc(sems["sxn2e"], 1)
                S.on("vector", f_ve2)

        nxn2e_all = S.at("sxn2e")
        for dffc in range(NF):
            slot_sem = "sw1sA" if dffc % 2 == 0 else "sw1sB"
            nw1s = S.at(slot_sem) + 16
            S.inc(slot_sem, 16)
            g_w1r = S.at("sw1") - 1 if S.at("sw1") >= 1 else 0

            def f_w1s(e, r=r, dffc=dffc, g_w1r=g_w1r, slot_sem=slot_sem):
                W(e, "sw1", g_w1r)
                e.dma_start(out=ab(o_wbuf + 512 * (dffc % 2), 1024)
                            .rearrange("p (c j) -> p c j", c=DC),
                            in_=d_w1c[dffc].rearrange("(c p) j -> p c j", p=128)
                            ).then_inc(sems[slot_sem], 16)
            S.on("gpsimd", f_w1s)

            g_h1 = S.at("sh1") - 1 if S.at("sh1") >= 1 else 0
            g_sdtA = S.at("sdt")
            nw1 = S.inc("sw1")

            def f_w1m(e, dffc=dffc, nw1s=nw1s, nxn2e_all=nxn2e_all, g_h1=g_h1,
                      g_sdtA=g_sdtA, slot_sem=slot_sem):
                W(e, slot_sem, nw1s)
                if dffc == 0:
                    W(e, "sxn2e", nxn2e_all)
                    W(e, "sdt", g_sdtA)
                W(e, "sh1", g_h1)
                wb = ab(o_wbuf + 512 * (dffc % 2), 1024)
                for j in range(DC - 1):
                    e.matmul(p_mm[dffc % 2][:], wb[:, 128 * j:128 * (j + 1)],
                             ab(o_xn2fm + 256 * j, 512), start=(j == 0), stop=False)
                e.matmul(p_mm[dffc % 2][:], wb[:, 128 * (DC - 1):128 * DC],
                         ab(o_xn2fm + 256 * (DC - 1), 512),
                         start=False, stop=True).then_inc(sems["sw1"], 1)
            S.on("tensor", f_w1m)

            g_sy2W1 = S.at("sy2")
            nh1 = S.inc("sh1")

            def f_sh1(e, dffc=dffc, nw1=nw1, g_sy2W1=g_sy2W1):
                W(e, "sw1", nw1)
                if dffc == 0:
                    W(e, "sy2", g_sy2W1)
                e.activation(out=ab(o_h1 + 256 * dffc, 512), in_=p_mm[dffc % 2][:],
                             func=AF.Gelu_apprx_tanh,
                             bias=t_b1c[:, dffc:dffc + 1]).then_inc(sems["sh1"], 1)
            S.on("scalar", f_sh1)

        nh1_all = S.at("sh1")
        g_ple_all = S.at("sple")
        g_sbv_all = S.at("sbv")
        g_sxnk_all = S.at("sxnk")
        g_sGn_all = S.at("sGn")
        g_stq_all = S.at("stq")
        g_sy2_prev = SY2_GUARD
        for dffc in range(NF):
            slot2 = "sw2sA" if dffc % 2 == 0 else "sw2sB"
            nw2s = S.at(slot2) + 16
            S.inc(slot2, 16)
            g_w2r = S.at("sw2")

            def f_w2s(e, dffc=dffc, g_w2r=g_w2r, slot2=slot2):
                if g_w2r >= 2:
                    W(e, "sw2", g_w2r - 1)
                e.dma_start(out=ab(o_w2buf + 512 * (dffc % 2), 1024),
                            in_=d_w2T[128 * dffc:128 * (dffc + 1), :]).then_inc(sems[slot2], 16)
            S.on("gpsimd", f_w2s)

            nw2 = S.inc("sw2")

            def f_w2m(e, r=r, dffc=dffc, nw2s=nw2s, nh1_all=nh1_all,
                      g_ple_all=g_ple_all, g_sbv_all=g_sbv_all, g_sxnk_all=g_sxnk_all,
                      g_sGn_all=g_sGn_all, g_stq_all=g_stq_all, g_sy2_prev=g_sy2_prev,
                      slot2=slot2):
                W(e, slot2, nw2s)
                if dffc == 0:
                    W(e, "sh1", nh1_all)
                    W(e, "sple", g_ple_all)
                    W(e, "sbv", g_sbv_all)
                    W(e, "sxnk", g_sxnk_all)
                    W(e, "sGn", g_sGn_all)
                    W(e, "stq", g_stq_all)
                    W(e, "sy2", g_sy2_prev)
                    for tb in range(4):
                        for dh in range(2):
                            e.matmul(p_mm[2 * tb + dh][:], t_onesb[:],
                                     ab(o_vrowb, 2 * D)[0:1, 512 * dh:512 * (dh + 1)],
                                     start=True, stop=False)
                wb = ab(o_w2buf + 512 * (dffc % 2), 1024)
                last = (dffc == NF - 1)
                for tb in range(4):
                    for dh in range(2):
                        mm = e.matmul(p_mm[2 * tb + dh][:],
                                      ab(o_h1 + 256 * dffc, 512)[:, 128 * tb:128 * (tb + 1)],
                                      wb[:, 512 * dh:512 * (dh + 1)],
                                      start=False, stop=last)
                        if tb == 3 and dh == 1:
                            mm.then_inc(sems["sw2"], 1)
            S.on("tensor", f_w2m)
        nw2_all = S.at("sw2")

        nx1_all = S.at("sx1")
        ny2_list = []
        for tb in range(4):
            sl3 = "sx3A" if tb % 2 == 0 else "sx3B"
            sle = "sx1eA" if tb % 2 == 0 else "sx1eB"
            g_y2a = S.at("sy2") - 2 if tb >= 2 else 0
            nx2b = S.at(sl3) + 16
            S.inc(sl3, 16)
            nx1e = S.at(sle) + 16
            S.inc(sle, 16)

            g_w1A = S.at("sx1wA")
            g_w1B = S.at("sx1wB")

            def f_rl(e, r=r, tb=tb, g_y2a=g_y2a, nx1_all=nx1_all, nw2_all=nw2_all,
                     sl3=sl3, sle=sle, g_w1A=g_w1A, g_w1B=g_w1B):
                W(e, "sx1", nx1_all)
                W(e, "sy2", g_y2a)
                W(e, "sw2", nw2_all)
                W(e, "sx1wA", g_w1A)
                W(e, "sx1wB", g_w1B)
                e.dma_start(out=af(o_xs3 + 1024 * (tb % 2), 1024),
                            in_=d_x[r, 128 * tb:128 * (tb + 1), :]).then_inc(sems[sl3], 16)
                e.dma_start(out=af(o_x1e + 1024 * (tb % 2), 1024),
                            in_=d_x1s[r, 128 * tb:128 * (tb + 1), :]).then_inc(sems[sle], 16)
            S.on("sync", f_rl)

            g_xaw = S.at("sxaw")
            ny2 = S.inc("sy2")
            ny2_list.append(ny2)

            def f_vy2(e, tb=tb, nw2_all=nw2_all, nx2b=nx2b, nx1e=nx1e, g_xaw=g_xaw,
                      sl3=sl3, sle=sle):
                W(e, "sw2", nw2_all)
                W(e, sl3, nx2b)
                W(e, sle, nx1e)
                W(e, "sxaw", g_xaw)
                for dh in range(2):
                    e.tensor_add(out=af(o_xa1 + 512 * dh, 512),
                                 in0=p_mm[2 * tb + dh][:],
                                 in1=af(o_x1e + 1024 * (tb % 2) + 512 * dh, 512))
                e.tensor_add(out=af(o_xa1, 1024), in0=af(o_xa1, 1024),
                             in1=af(o_xs3 + 1024 * (tb % 2), 1024)).then_inc(sems["sy2"], 1)
            S.on("vector", f_vy2)
            SY2_GUARD = ny2

            S.inc("sxaw", 16)

            def f_xaw(e, r=r, tb=tb, ny2=ny2):
                W(e, "sy2", ny2)
                e.dma_start(out=d_xa[r, 128 * tb:128 * (tb + 1), :],
                            in_=af(o_xa1, 1024)).then_inc(sems["sxaw"], 16)
            S.on("sync", f_xaw)

        ny2_all_r = S.at("sy2")
        for tb in range(4):
            nxar2 = S.at("sxaL") + 16
            S.inc("sxaL", 16)
            g_xf_prev = S.at("sxf")

            g_xaw_all = S.at("sxaw")

            def f_xarl(e, r=r, tb=tb, g_xf_prev=g_xf_prev, g_xaw_all=g_xaw_all):
                W(e, "sxf", g_xf_prev)
                W(e, "sxaw", g_xaw_all)
                e.dma_start(out=af(o_xa1, 1024),
                            in_=d_xa[r, 128 * tb:128 * (tb + 1), :]).then_inc(sems["sxaL"], 16)
            S.on("sync", f_xarl)

            nv3 = S.inc("svar")

            def f_vln1(e, nxar2=nxar2):
                W(e, "sxaL", nxar2)
                e.tensor_mul(out=af(o_wkA, 1024), in0=af(o_xa1, 1024), in1=af(o_xa1, 1024))
                e.reduce_sum(out=t_col[:, 5:6], in_=af(o_xa1, 1024), axis=AX)
                e.reduce_sum(out=t_col[:, 6:7], in_=af(o_wkA, 1024), axis=AX)
                e.tensor_scalar_mul(out=t_col[:, 5:6], in0=t_col[:, 5:6], scalar1=1.0 / D)
                e.tensor_mul(out=t_col[:, 2:3], in0=t_col[:, 5:6], in1=t_col[:, 5:6])
                e.scalar_tensor_tensor(out=t_col[:, 3:4], in0=t_col[:, 6:7],
                                       scalar=1.0 / D, in1=t_col[:, 2:3],
                                       op0=ALU.mult, op1=ALU.subtract).then_inc(sems["svar"], 1)
            S.on("vector", f_vln1)

            nrsq3 = S.inc("srsq3")

            def f_sln1(e, nv3=nv3):
                W(e, "svar", nv3)
                e.activation(out=t_col[:, 4:5], in_=t_col[:, 3:4], func=AF.Ln,
                             bias=t_small[:, 6:7])
                e.activation(out=t_col[:, 4:5], in_=t_col[:, 4:5], func=AF.Exp,
                             scale=-0.5).then_inc(sems["srsq3"], 1)
            S.on("scalar", f_sln1)

            g_strC = S.at("strC")
            nxf = S.inc("sxf")

            def f_vxf(e, nrsq3=nrsq3, g_strC=g_strC):
                W(e, "srsq3", nrsq3)
                W(e, "strC", g_strC)
                e.tensor_scalar(out=ab(o_xf, 1024), in0=af(o_xa1, 1024),
                                scalar1=t_col[:, 5:6], scalar2=t_col[:, 4:5],
                                op0=ALU.subtract, op1=ALU.mult).then_inc(sems["sxf"], 1)
            S.on("vector", f_vxf)

            for j in range(DC):
                ktr = S.inc("strD")

                def f_t3(e, j=j, nxf=nxf, ktr=ktr, ny2_all_r=ny2_all_r):
                    if j == 0:
                        W(e, "sxf", nxf)
                        W(e, "sy2", ny2_all_r)
                    if ktr > 2:
                        W(e, "sxfe", ktr - 2)
                    bk = p_trb if (ktr - 1) % 2 == 0 else p_plb
                    e.transpose(bk[:, 0:128],
                                ab(o_xf, 1024)[:, 128 * j:128 * (j + 1)],
                                t_identb[:]).then_inc(sems["strD"], 1)
                S.on("tensor", f_t3)

                S.inc("sxfe")

                def f_ve3(e, r=r, tb=tb, j=j, ktr=ktr):
                    W(e, "strD", ktr)
                    bk = p_trb if (ktr - 1) % 2 == 0 else p_plb
                    e.tensor_copy(out=ab(o_xffm + 1024 * j, 2048)[:, 512 * r + 128 * tb:
                                                                  512 * r + 128 * (tb + 1)],
                                  in_=bk[:, 0:128]).then_inc(sems["sxfe"], 1)
                S.on("vector", f_ve3)

        nxfe_all = S.at("sxfe")
        ngtp = S.inc("sgtp")

        def f_gt(e, r=r, nxfe_all=nxfe_all):
            W(e, "sxfe", nxfe_all)
            for j in range(DC - 1):
                e.matmul(p_mm[2][0:8, :], t_gatew[:, j, :],
                         ab(o_xffm + 1024 * j, 2048)[:, 512 * r:512 * (r + 1)],
                         start=(j == 0), stop=False)
            e.matmul(p_mm[2][0:8, :], t_gatew[:, DC - 1, :],
                     ab(o_xffm + 1024 * (DC - 1), 2048)[:, 512 * r:512 * (r + 1)],
                     start=False, stop=True).then_inc(sems["sgtp"], 1)
        S.on("tensor", f_gt)

        ngt = S.inc("sgt")

        def f_sgt(e, ngtp=ngtp):
            W(e, "sgtp", ngtp)
            e.activation(out=af(o_sc, 512)[0:8, :], in_=p_mm[2][0:8, :], func=AF.Sigmoid,
                         bias=t_small[0:8, 25:26]).then_inc(sems["sgt"], 1)
        S.on("scalar", f_sgt)

        nwv = S.inc("swv")

        def f_vtop(e, r=r, ngt=ngt):
            W(e, "sgt", ngt)
            s = [af(o_sc + 512 * k, 512)[0:32, :] for k in range(6)]
            rot = lambda sh: [(i // 8) * 8 + ((i % 8) + sh) % 8 for i in range(32)]
            e.stream_shuffle(out=s[2], in_=s[0], mask=rot(4))
            e.tensor_max(out=s[1], in0=s[0], in1=s[2])
            e.stream_shuffle(out=s[2], in_=s[1], mask=rot(2))
            e.tensor_max(out=s[1], in0=s[1], in1=s[2])
            e.stream_shuffle(out=s[2], in_=s[1], mask=rot(1))
            e.tensor_max(out=s[1], in0=s[1], in1=s[2])
            e.tensor_tensor(out=s[2], in0=s[0], in1=s[1], op=ALU.is_equal)
            e.tensor_scalar(out=s[2], in0=s[2], scalar1=t_small[0:32, 5:6], scalar2=99.0,
                            op0=ALU.mult, op1=ALU.add)
            e.stream_shuffle(out=s[3], in_=s[2], mask=rot(4))
            e.tensor_tensor(out=s[2], in0=s[2], in1=s[3], op=ALU.min)
            e.stream_shuffle(out=s[3], in_=s[2], mask=rot(2))
            e.tensor_tensor(out=s[2], in0=s[2], in1=s[3], op=ALU.min)
            e.stream_shuffle(out=s[3], in_=s[2], mask=rot(1))
            e.tensor_tensor(out=s[2], in0=s[2], in1=s[3], op=ALU.min)
            e.tensor_scalar(out=s[3], in0=s[2], scalar1=t_small[0:32, 0:1], scalar2=None,
                            op0=ALU.is_equal, op1=ALU.bypass)
            e.scalar_tensor_tensor(out=s[2], in0=s[3], scalar=-1.0e30, in1=s[0],
                                   op0=ALU.mult, op1=ALU.add)
            e.stream_shuffle(out=s[5], in_=s[2], mask=rot(4))
            e.tensor_max(out=s[4], in0=s[2], in1=s[5])
            e.stream_shuffle(out=s[5], in_=s[4], mask=rot(2))
            e.tensor_max(out=s[4], in0=s[4], in1=s[5])
            e.stream_shuffle(out=s[5], in_=s[4], mask=rot(1))
            e.tensor_max(out=s[4], in0=s[4], in1=s[5])
            e.tensor_tensor(out=s[5], in0=s[2], in1=s[4], op=ALU.is_equal)
            e.tensor_scalar(out=s[5], in0=s[5], scalar1=t_small[0:32, 5:6], scalar2=99.0,
                            op0=ALU.mult, op1=ALU.add)
            e.stream_shuffle(out=s[2], in_=s[5], mask=rot(4))
            e.tensor_tensor(out=s[5], in0=s[5], in1=s[2], op=ALU.min)
            e.stream_shuffle(out=s[2], in_=s[5], mask=rot(2))
            e.tensor_tensor(out=s[5], in0=s[5], in1=s[2], op=ALU.min)
            e.stream_shuffle(out=s[2], in_=s[5], mask=rot(1))
            e.tensor_tensor(out=s[5], in0=s[5], in1=s[2], op=ALU.min)
            e.tensor_scalar(out=s[2], in0=s[5], scalar1=t_small[0:32, 0:1], scalar2=None,
                            op0=ALU.is_equal, op1=ALU.bypass)
            e.tensor_add(out=s[5], in0=s[1], in1=s[4])
            e.tensor_scalar(out=s[5], in0=s[5], scalar1=1.0e-6, scalar2=10.0,
                            op0=ALU.add, op1=ALU.mult)
            e.reciprocal(out=s[5], in_=s[5])
            e.tensor_mul(out=s[1], in0=s[3], in1=s[1])
            e.tensor_mul(out=s[4], in0=s[2], in1=s[4])
            e.tensor_add(out=s[1], in0=s[1], in1=s[4])
            e.tensor_mul(out=af(o_wfm, 2048)[0:32, 512 * r:512 * (r + 1)],
                         in0=s[1], in1=s[5])
            e.tensor_copy(out=ab(o_wbf, 2048)[0:32, 512 * r:512 * (r + 1)],
                          in_=af(o_wfm, 2048)[0:32, 512 * r:512 * (r + 1)]).then_inc(sems["swv"], 1)
        S.on("vector", f_vtop)

        for i in range(4):
            nwtp = S.inc("swTp")

            g_sple_r = S.at("sple")

            def f_twt(e, r=r, i=i, nwv=nwv, nwtp=nwtp, g_sple_r=g_sple_r):
                if i == 0:
                    W(e, "swv", nwv)
                    W(e, "sple", g_sple_r)
                if nwtp > 1:
                    W(e, "swT", nwtp - 1)
                e.transpose(p_trf[:, 0:8],
                            af(o_wfm, 2048)[0:8, 512 * r + 128 * i:512 * r + 128 * (i + 1)],
                            t_identf[0:8, 0:8]).then_inc(sems["swTp"], 1)
            S.on("tensor", f_twt)

            S.inc("swT")

            def f_vwt(e, r=r, i=i, nwtp=nwtp):
                W(e, "swTp", nwtp)
                e.tensor_copy(out=t_wtok[:, (4 * r + i) * 8:(4 * r + i) * 8 + 8],
                              in_=p_trf[:, 0:8]).then_inc(sems["swT"], 1)
            S.on("vector", f_vwt)

    SNAP_ROUNDS = S.snapshot()
    BAR_WAITS = [(nm, S.at(nm)) for nm in
                 ["sciny", "sxaw", "sx1wA", "sx1wB", "sy2", "sxfe", "swT",
                  "sdt", "srdt", "sg", "sh", "syq", "sye", "sple", "sh1",
                  "sw1sA", "sw1sB", "sw2sA", "sw2sB", "sx2A", "sx2B",
                  "sx3A", "sx3B", "sx1eA", "sx1eB", "sxaL", "sgt", "swv",
                  "sxn2e", "sbn", "sGn", "stq", "sbv", "sxnk", "srhs",
                  "sx", "sxn", "sbcB", "sbcC", "sbvd", "sx1", "sxn2",
                  "srsq3", "sxf", "sxg2"]]
    NBAR = S.inc("setup")

    def f_bar(e):
        for nm, v in BAR_WAITS:
            W(e, nm, v)
        e.sem_inc(sems["setup"], 1)
    S.on("gpsimd", f_bar)

    first_onB = {"sync": True, "vector": True, "scalar": True, "tensor": True,
                 "gpsimd": True}

    def bwait(eng, e):
        if first_onB[eng]:
            W(e, "setup", NBAR)
            first_onB[eng] = False

    scmb_guard = 0
    for hb in range(2):
        for e_ in range(E + 1):
            g_dn_prev = S.at("sdn")
            g_cmb_prev = scmb_guard
            for dffc in range(NF):
                slotu = "supsA" if dffc % 2 == 0 else "supsB"
                nups = S.at(slotu) + 16
                S.inc(slotu, 16)
                g_upr = S.at("sup")

                def f_ups(e, hb=hb, e_=e_, dffc=dffc, g_upr=g_upr, slotu=slotu):
                    bwait("gpsimd", e)
                    if g_upr >= 2:
                        W(e, "sup", g_upr - 1)
                    src = d_swupc[dffc] if e_ == 0 else d_upwc[e_ - 1, dffc]
                    e.dma_start(out=ab(o_upb + 512 * (dffc % 2), 1024)
                                .rearrange("p (c j) -> p c j", c=DC),
                                in_=src.rearrange("(c p) j -> p c j", p=128)
                                ).then_inc(sems[slotu], 16)
                S.on("gpsimd", f_ups)

                g_ha = S.at("sha")
                nup = S.inc("sup")

                def f_upm(e, hb=hb, e_=e_, dffc=dffc, nups=nups, g_ha=g_ha,
                          g_cmb_prev=g_cmb_prev, slotu=slotu):
                    bwait("tensor", e)
                    W(e, slotu, nups)
                    W(e, "sha", g_ha)
                    if dffc <= 1:
                        W(e, "scmb", g_cmb_prev)
                    wb = ab(o_upb + 512 * (dffc % 2), 1024)
                    for tt in range(2):
                        for j in range(DC):
                            mm = e.matmul(p_mm[4 + tt][:], wb[:, 128 * j:128 * (j + 1)],
                                          ab(o_xffm + 1024 * j, 2048)[:, 1024 * hb + 512 * tt:
                                                                      1024 * hb + 512 * (tt + 1)],
                                          start=(j == 0), stop=(j == DC - 1))
                            if tt == 1 and j == DC - 1:
                                mm.then_inc(sems["sup"], 1)
                S.on("tensor", f_upm)

                nha = S.inc("sha")

                def f_sup(e, e_=e_, dffc=dffc, nup=nup, g_dn_prev=g_dn_prev):
                    bwait("scalar", e)
                    W(e, "sup", nup)
                    W(e, "sdn", g_dn_prev)
                    bc = t_sbupc[:, dffc:dffc + 1] if e_ == 0 else \
                        t_upbc[:, (e_ - 1) * NF + dffc:(e_ - 1) * NF + dffc + 1]
                    for tt in range(2):
                        act = e.activation(out=ab(o_hact + 512 * dffc, 1024)[:, 512 * tt:512 * (tt + 1)],
                                           in_=p_mm[4 + tt][:], func=AF.Gelu, bias=bc)
                        if tt == 1:
                            act.then_inc(sems["sha"], 1)
                S.on("scalar", f_sup)

            nha_all = S.at("sha")

            for dh in range(2):
                for dffc in range(NF):
                    slotd = "sdnsA" if dffc % 2 == 0 else "sdnsB"
                    ndns = S.at(slotd) + 16
                    S.inc(slotd, 16)
                    g_dnr = S.at("sdn")

                    def f_dns(e, e_=e_, dffc=dffc, dh=dh, g_dnr=g_dnr, slotd=slotd):
                        if g_dnr >= 2:
                            W(e, "sdn", g_dnr - 1)
                        src = d_swdn[128 * dffc:128 * (dffc + 1), 512 * dh:512 * (dh + 1)] \
                            if e_ == 0 else \
                            d_dnw[e_ - 1, 128 * dffc:128 * (dffc + 1), 512 * dh:512 * (dh + 1)]
                        e.dma_start(out=ab(o_dnb + 512 * (dffc % 2), 1024)[:, 0:512],
                                    in_=src).then_inc(sems[slotd], 16)
                    S.on("gpsimd", f_dns)

                    ndn = S.inc("sdn")

                    def f_dnm(e, e_=e_, hb=hb, dffc=dffc, dh=dh, ndns=ndns,
                              nha_all=nha_all, scmb_g=scmb_guard, slotd=slotd):
                        W(e, slotd, ndns)
                        if dffc == 0:
                            W(e, "sha", nha_all)
                            W(e, "scmb", scmb_g)
                            for tb in range(8):
                                if e_ == 0:
                                    e.matmul(p_mm[tb][:], t_onesb[:],
                                             ab(o_vrowb, 2 * D)[0:1, D + 512 * dh:D + 512 * (dh + 1)],
                                             start=True, stop=False)
                                else:
                                    e.matmul(p_mm[tb][:],
                                             ab(o_wbf, 2048)[0:8, 1024 * hb + 128 * tb:
                                                             1024 * hb + 128 * (tb + 1)],
                                             t_downb[:, 512 * dh:512 * (dh + 1)],
                                             start=True, stop=False)
                        wb = ab(o_dnb + 512 * (dffc % 2), 1024)[:, 0:512]
                        last = (dffc == NF - 1)
                        for tb in range(8):
                            mm = e.matmul(p_mm[tb][:],
                                          ab(o_hact + 512 * dffc, 1024)[:, 128 * tb:128 * (tb + 1)],
                                          wb, start=False, stop=last)
                            if tb == 7:
                                mm.then_inc(sems["sdn"], 1)
                    S.on("tensor", f_dnm)

                ndn_all = S.at("sdn")
                ncmb = S.inc("scmb")

                def f_vcmb(e, e_=e_, hb=hb, dh=dh, ndn_all=ndn_all):
                    bwait("vector", e)
                    W(e, "sdn", ndn_all)
                    for tb in range(8):
                        dst = af(o_comb + 1024 * tb + 512 * dh, 512)
                        if e_ == 0:
                            e.tensor_scalar_mul(out=dst, in0=p_mm[tb][:], scalar1=0.1)
                        else:
                            wcol = t_wtok[:, (8 * hb + tb) * 8 + (e_ - 1):
                                          (8 * hb + tb) * 8 + e_]
                            e.scalar_tensor_tensor(out=dst, in0=p_mm[tb][:], scalar=wcol,
                                                   in1=dst, op0=ALU.mult, op1=ALU.add)
                    e.tensor_copy(out=t_col[:, 7:8], in_=t_col[:, 7:8]).then_inc(sems["scmb"], 1)
                S.on("vector", f_vcmb)
                scmb_guard = ncmb

        for tb in range(8):
            r_ = 2 * hb + tb // 4
            i_ = tb % 4
            slx = "sxarA" if tb % 2 == 0 else "sxarB"
            nxar = S.at(slx) + 16
            S.inc(slx, 16)
            g_fin = S.at("sfin") - 1 if S.at("sfin") >= 1 else 0

            def f_xar(e, r_=r_, i_=i_, tb=tb, g_fin=g_fin, slx=slx):
                bwait("sync", e)
                if tb >= 2:
                    W(e, "sfin", g_fin)
                e.dma_start(out=af(o_xar + 1024 * (tb % 2), 1024),
                            in_=d_xa[r_, 128 * i_:128 * (i_ + 1), :]).then_inc(sems[slx], 16)
            S.on("sync", f_xar)

            ncmb_all = S.at("scmb")
            nfin = S.inc("sfin")

            g_out_prev = S.at("sout")

            def f_vfin(e, tb=tb, nxar=nxar, ncmb_all=ncmb_all, slx=slx,
                       g_out_prev=g_out_prev):
                W(e, "scmb", ncmb_all)
                W(e, slx, nxar)
                W(e, "sout", g_out_prev)
                cmb = af(o_comb + 1024 * tb, 1024)
                e.reduce_max(out=t_col[:, 0:1], in_=cmb, axis=AX, apply_absolute_value=True)
                e.tensor_scalar_add(out=t_col[:, 0:1], in0=t_col[:, 0:1], scalar1=1.0e-6)
                e.reciprocal(out=t_col[:, 1:2], in_=t_col[:, 0:1])
                e.tensor_scalar_mul(out=af(o_fin, 1024), in0=cmb, scalar1=t_col[:, 1:2])
                e.tensor_add(out=af(o_fin, 1024), in0=af(o_fin, 1024),
                             in1=af(o_xar + 1024 * (tb % 2), 1024)).then_inc(sems["sfin"], 1)
            S.on("vector", f_vfin)

            S.inc("sout", 16)

            def f_out(e, r_=r_, i_=i_, nfin=nfin):
                W(e, "sfin", nfin)
                e.dma_start(out=d_out[r_, 128 * i_:128 * (i_ + 1), :],
                            in_=af(o_fin, 1024)).then_inc(sems["sout"], 16)
            S.on("sync", f_out)

    SNAP_FULL = S.snapshot()
    import os as _os
    if _os.environ.get("PRINT_SNAPS"):
        print("SNAP_XN", SNAP_XN[0], "ROUNDS", [r[0] for r in ROUND_SNAPS],
              "SNAP_ROUNDS", SNAP_ROUNDS[0], "FULL", SNAP_FULL[0])
    if stage is not None:
        snaps = {"xn": SNAP_XN, "r1": ROUND_SNAPS[1] if len(ROUND_SNAPS) > 1 else SNAP_ROUNDS,
                 "r2": ROUND_SNAPS[2] if len(ROUND_SNAPS) > 2 else SNAP_ROUNDS,
                 "r3": ROUND_SNAPS[3] if len(ROUND_SNAPS) > 3 else SNAP_ROUNDS,
                 "rounds": SNAP_ROUNDS, "full": SNAP_FULL}
        if isinstance(stage, str) and stage.startswith("cut:"):
            gcut = int(stage.split(":")[1])
            cnts = S.cnt_log[gcut - 1] if 0 < gcut <= len(S.cnt_log) else (
                dict(S.cnt) if gcut > 0 else {})
        else:
            gcut, cnts = snaps[stage]
        for k in S.q:
            S.q[k] = [(g, f) for (g, f) in S.q[k] if g < gcut]

        def f_endS(e, cnts=cnts):
            for nm, v in cnts.items():
                W(e, nm, v)
        S.on("gpsimd", f_endS)
    else:
        NOUT = S.at("sout")

        def f_end(e):
            W(e, "sout", NOUT)
        S.on("gpsimd", f_end)

    class DrainProxy:
        _PASS = {"wait_ge", "sem_inc", "drain", "nop"}

        def __init__(self, eng):
            self._e = eng

        def __getattr__(self, name):
            fn = getattr(self._e, name)
            if name in DrainProxy._PASS or not callable(fn):
                return fn

            def wrapped(*a, **k):
                self._e.drain()
                return fn(*a, **k)
            return wrapped

    @blk.sync
    def _(e):
        for _, fn in S.q["sync"]:
            fn(e)

    @blk.vector
    def _(e):
        p = DrainProxy(e)
        for _, fn in S.q["vector"]:
            fn(p)

    @blk.scalar
    def _(e):
        p = DrainProxy(e)
        for _, fn in S.q["scalar"]:
            fn(p)

    @blk.tensor
    def _(e):
        for _, fn in S.q["tensor"]:
            fn(e)

    @blk.gpsimd
    def _(e):
        p = DrainProxy(e)
        for _, fn in S.q["gpsimd"]:
            fn(p)

    ctx.close()
    return nc


_NC_CACHE = [None]


def kernel(L0_attn_x, L0_attn_norm1_w, L0_attn_norm2_w, L0_attn_A_log, L0_attn_D_param,
           L0_attn_scale, L0_attn_Wdt, L0_attn_bdt, L0_attn_WB, L0_attn_bB, L0_attn_WC,
           L0_attn_bC, L0_attn_W1, L0_attn_b1, L0_attn_W2, L0_attn_b2, L0_ffn_gate_w,
           L0_ffn_bias, L0_ffn_up_w, L0_ffn_up_b, L0_ffn_down_w, L0_ffn_down_b,
           L0_ffn_sw_up, L0_ffn_sb_up, L0_ffn_sw_down, L0_ffn_sb_down):
    x = np.asarray(L0_attn_x, dtype=np.float32)
    WdtT = np.ascontiguousarray(np.asarray(L0_attn_Wdt, np.float32).T)
    wbcT = np.ascontiguousarray(
        np.concatenate([np.asarray(L0_attn_WB, np.float32).T,
                        np.asarray(L0_attn_WC, np.float32).T], axis=1))
    w1c = np.ascontiguousarray(
        np.asarray(L0_attn_W1, np.float32).T.reshape(D, NF, 128).transpose(1, 0, 2))
    w2T = np.ascontiguousarray(np.asarray(L0_attn_W2, np.float32).T)
    swupc = np.ascontiguousarray(
        np.asarray(L0_ffn_sw_up, np.float32).reshape(D, NF, 128).transpose(1, 0, 2))
    swdn = np.ascontiguousarray(np.asarray(L0_ffn_sw_down, np.float32))
    upwc = np.ascontiguousarray(
        np.asarray(L0_ffn_up_w, np.float32).reshape(E, D, NF, 128).transpose(0, 2, 1, 3))
    dnw = np.ascontiguousarray(np.asarray(L0_ffn_down_w, np.float32))
    gatew = np.ascontiguousarray(np.asarray(L0_ffn_gate_w, np.float32))
    vecs = np.concatenate([np.asarray(L0_attn_norm1_w, np.float32),
                           np.asarray(L0_attn_norm2_w, np.float32),
                           np.asarray(L0_attn_b2, np.float32),
                           np.asarray(L0_ffn_sb_down, np.float32)])[None, :]
    b1cols = np.ascontiguousarray(np.asarray(L0_attn_b1, np.float32).reshape(NF, 128).T)
    sbupcols = np.ascontiguousarray(np.asarray(L0_ffn_sb_up, np.float32).reshape(NF, 128).T)
    upbcols = np.ascontiguousarray(
        np.asarray(L0_ffn_up_b, np.float32).reshape(E, NF, 128).transpose(2, 0, 1)
        .reshape(128, E * NF))
    downb = np.ascontiguousarray(np.asarray(L0_ffn_down_b, np.float32))
    identb = np.eye(128).astype(ml_dtypes.bfloat16)
    identf = np.eye(128, dtype=np.float32)

    if _NC_CACHE[0] is None:
        _NC_CACHE[0] = build_nc()
    nc = _NC_CACHE[0]

    in_maps = []
    for c in range(NC):
        s_, q_ = c // 4, c % 4
        x_my = np.stack([x[s_, 2048 * r + TS * q_:2048 * r + TS * (q_ + 1), :]
                         for r in range(R)])
        small = np.zeros((128, 32), np.float32)
        small[0, 9:25] = np.asarray(L0_attn_A_log, np.float32)
        small[0:32, 0] = np.arange(32, dtype=np.float32)
        small[0:8, 25] = np.asarray(L0_ffn_bias, np.float32)
        small[:, 1] = np.asarray(L0_attn_bdt, np.float32)[128 * c:128 * (c + 1)]
        small[:, 2] = np.asarray(L0_attn_D_param, np.float32)[128 * c:128 * (c + 1)]
        small[:, 3] = np.asarray(L0_attn_scale, np.float32)[128 * c:128 * (c + 1)]
        small[0:16, 4] = np.asarray(L0_attn_bB, np.float32)
        small[16:32, 4] = np.asarray(L0_attn_bC, np.float32)
        small[0:32, 5] = np.arange(32, dtype=np.float32) - 99.0
        small[:, 6] = 1.0e-5
        small[:, 7] = 1.0e-11
        small[:, 8] = 1.0e-6
        sel = np.zeros((D, 128), np.float32)
        sel[128 * c:128 * (c + 1), :] = np.eye(128, dtype=np.float32)
        selT = np.zeros((128, D), np.float32)
        selT[:, 128 * c:128 * (c + 1)] = np.eye(128, dtype=np.float32)
        in_maps.append({
            "x_my": np.ascontiguousarray(x_my),
            "wdtTk": np.ascontiguousarray(WdtT[:, 128 * c:128 * (c + 1)]),
            "wbcT": wbcT,
            "sel": sel.astype(ml_dtypes.bfloat16),
            "selT": selT.astype(ml_dtypes.bfloat16),
            "identb": identb, "identf": identf,
            "small": small, "vecs": vecs,
            "b1c": b1cols, "sbupc": sbupcols, "upbc": upbcols, "downb": downb,
            "gatew": gatew, "w1c": w1c, "w2T": w2T, "swupc": swupc, "swdn": swdn,
            "upwc": upwc, "dnw": dnw,
        })

    res = run_bass_kernel_spmd(nc, in_maps, core_ids=list(range(NC)))
    out = np.zeros((2, 8192, D), np.float32)
    for c in range(NC):
        s_, q_ = c // 4, c % 4
        om = res.results[c]["out_my"].reshape(R, TS, D)
        for r in range(R):
            out[s_, 2048 * r + TS * q_:2048 * r + TS * (q_ + 1), :] = om[r]
    return out


# version 04 (final)
# speedup vs baseline: 1.0510x; 1.0510x over previous
import numpy as np
import ml_dtypes
import concourse.bass as bass
import concourse.mybir as mybir
from concourse.bass_utils import run_bass_kernel_spmd

F32 = mybir.dt.float32
BF16 = mybir.dt.bfloat16
AF = mybir.ActivationFunctionType
ALU = mybir.AluOpType
AX = mybir.AxisListType.X

NC = 8
D, N, DFF, E = 1024, 16, 4096, 8
R, TR, TS, DC, NF, Q = 4, 4096, 512, 8, 32, 1024


class Sched:
    def __init__(self):
        self.q = {k: [] for k in ("sync", "vector", "scalar", "tensor", "gpsimd")}
        self.cnt = {}
        self.gseq = 0
        self.cnt_log = []

    def inc(self, sem, n=1):
        self.cnt[sem] = self.cnt.get(sem, 0) + n
        return self.cnt[sem]

    def at(self, sem):
        return self.cnt.get(sem, 0)

    def on(self, eng, fn):
        self.q[eng].append((self.gseq, fn))
        self.gseq += 1
        self.cnt_log.append(dict(self.cnt))

    def snapshot(self):
        return (self.gseq, dict(self.cnt))


def build_nc(stage=None):
    nc = bass.Bass("TRN2", target_bir_lowering=False)

    d_x = nc.dram_tensor("x_my", [R, TS, D], F32, kind="ExternalInput").ap()
    d_wdtTk = nc.dram_tensor("wdtTk", [D, 128], F32, kind="ExternalInput").ap()
    d_wbcT = nc.dram_tensor("wbcT", [D, 2 * N], F32, kind="ExternalInput").ap()
    d_sel = nc.dram_tensor("sel", [D, 128], BF16, kind="ExternalInput").ap()
    d_selT = nc.dram_tensor("selT", [128, D], BF16, kind="ExternalInput").ap()
    d_identb = nc.dram_tensor("identb", [128, 128], BF16, kind="ExternalInput").ap()
    d_identf = nc.dram_tensor("identf", [128, 128], F32, kind="ExternalInput").ap()
    d_small = nc.dram_tensor("small", [128, 32], F32, kind="ExternalInput").ap()
    d_vecs = nc.dram_tensor("vecs", [1, 4 * D], F32, kind="ExternalInput").ap()
    d_b1c = nc.dram_tensor("b1c", [128, NF], F32, kind="ExternalInput").ap()
    d_sbupc = nc.dram_tensor("sbupc", [128, NF], F32, kind="ExternalInput").ap()
    d_upbc = nc.dram_tensor("upbc", [128, E * NF], F32, kind="ExternalInput").ap()
    d_downb = nc.dram_tensor("downb", [E, D], F32, kind="ExternalInput").ap()
    d_gatew = nc.dram_tensor("gatew", [D, E], F32, kind="ExternalInput").ap()
    d_w1c = nc.dram_tensor("w1c", [NF, D, 128], F32, kind="ExternalInput").ap()
    d_w2T = nc.dram_tensor("w2T", [DFF, D], F32, kind="ExternalInput").ap()
    d_swupc = nc.dram_tensor("swupc", [NF, D, 128], F32, kind="ExternalInput").ap()
    d_swdn = nc.dram_tensor("swdn", [DFF, D], F32, kind="ExternalInput").ap()
    d_upwc = nc.dram_tensor("upwc", [E, NF, D, 128], F32, kind="ExternalInput").ap()
    d_dnw = nc.dram_tensor("dnw", [E, DFF, D], F32, kind="ExternalInput").ap()
    d_out = nc.dram_tensor("out_my", [R, TS, D], F32, kind="ExternalOutput").ap()

    d_xa = nc.dram_tensor("xa_stash", [R, TS, D], F32, kind="Internal").ap()
    d_bvcd = nc.dram_tensor("bvcd", [2 * N, Q], F32, kind="Internal").ap()
    d_x1s = nc.dram_tensor("x1_stash", [R, TS, D], F32, kind="Internal").ap()
    cin1 = [nc.dram_tensor(f"cin1_{r}", [D, TS], BF16, kind="Internal").ap() for r in range(R)]
    cout1 = [nc.dram_tensor(f"cout1_{r}", [NC, D, TS], BF16, kind="Internal").ap() for r in range(R)]
    cin2 = [nc.dram_tensor(f"cin2_{r}", [TR, D], BF16, kind="Internal").ap() for r in range(R)]
    cout2 = [nc.dram_tensor(f"cout2_{r}", [TS, D], BF16, kind="Internal").ap() for r in range(R)]
    RG = [list(range(NC))]

    S = Sched()
    from contextlib import ExitStack
    ctx = ExitStack()

    def sbt(name, shape, dt):
        return ctx.enter_context(nc.sbuf_tensor(name, shape, dt))

    t_identb = sbt("t_identb", [128, 128], BF16)
    t_identf = sbt("t_identf", [128, 128], F32)
    t_sel = sbt("t_sel", [128, DC, 128], BF16)
    t_selT = sbt("t_selT", [128, D], BF16)
    t_small = sbt("t_small", [128, 32], F32)
    t_n1rep = sbt("t_n1rep", [128, D], BF16)
    t_n2rep = sbt("t_n2rep", [128, D], BF16)
    t_downb = sbt("t_downb", [E, D], BF16)
    t_onesb = sbt("t_onesb", [1, 128], BF16)
    t_onesf = sbt("t_onesf", [1, 128], F32)
    t_Arow = sbt("t_Arow", [1, N], F32)
    t_Acol = sbt("t_Acol", [N, 1], F32)
    t_invA = sbt("t_invA", [2 * N, 1], F32)
    t_Arep = sbt("t_Arep", [128, N], F32)
    t_b1c = sbt("t_b1c", [128, NF], F32)
    t_sbupc = sbt("t_sbupc", [128, NF], F32)
    t_upbc = sbt("t_upbc", [128, E * NF], F32)
    t_wdtTk = sbt("t_wdtTk", [128, DC, 128], BF16)
    t_wbcT = sbt("t_wbcT", [128, DC, 2 * N], BF16)
    t_gatew = sbt("t_gatew", [128, DC, E], BF16)
    t_stats = sbt("t_stats", [128, 8 * R], F32)
    t_state = sbt("t_state", [128, 2 * N], F32)
    t_wtok = sbt("t_wtok", [128, 16 * E], F32)
    t_col = sbt("t_col", [128, 8], F32)

    AW = 48800
    t_arena = ctx.enter_context(nc.sbuf_tensor("arena", [128, AW], F32))
    arena_bf = t_arena.bitcast(BF16)

    def af(off, w):
        assert off + w <= AW, (off, w)
        return t_arena[:, off:off + w]

    def ab(off, wbf):
        assert 2 * off + wbf <= 2 * AW, (off, wbf)
        return arena_bf[:, 2 * off:2 * off + wbf]

    o_xffm = 0
    o_wfm = DC * 1024
    o_wbf = o_wfm + 2048
    o_vrow = o_wbf + 1024
    o_vrowb = o_vrow + 4096
    P0_END = o_vrowb + 1024
    o_rhs = P0_END
    o_dt = o_rhs + 4096
    o_rdt = o_dt + Q
    o_g = o_rdt + Q
    o_xnk = o_g + Q
    o_dA = o_xnk + 512
    o_bsc = o_dA + Q
    o_G = o_bsc + Q
    o_h = o_G + Q
    o_tq = o_h + Q
    o_yacc = o_tq + Q
    o_bvc = o_yacc + Q
    o_ypl = o_bvc + 1024
    P2_END = o_ypl + 512
    p13 = P2_END
    o_xs = p13
    o_wk1 = p13 + 4096
    o_wk2 = p13 + 5120
    o_xntok = p13 + 5128
    o_xnfm = p13 + 5640
    o_ysb = p13
    o_x1 = p13 + 512
    o_xa1 = p13 + 2560
    o_xn2 = p13 + 3584
    o_xn2fm = p13 + 4096
    o_h1 = p13 + 6144
    o_wbuf = p13 + 14336
    o_w2buf = p13 + 15360
    o_xf = o_xn2
    o_x1e = o_h1
    o_wkA = o_h1 + 2048
    o_sc = o_h1 + 3072
    o_xs3 = o_h1 + 6144
    P3_END = p13 + 16384
    assert P3_END <= AW, P3_END
    p4 = P0_END
    o_hact = p4
    o_upb = p4 + 16384
    o_dnb = p4 + 17408
    o_comb = p4 + 18432
    o_fin = p4 + 26624
    o_xar = p4 + 27648
    P4_END = p4 + 29696
    assert P4_END <= AW, P4_END

    p_mm = [ctx.enter_context(nc.psum_tensor(f"p_mm{i}", [128, 512], F32)) for i in range(8)]
    p_trb = p_mm[6].bitcast(BF16)
    p_trf = p_mm[7]
    p_plb = p_mm[7].bitcast(BF16)

    semnames = ["ld0", "cast0", "setup", "sx", "svar", "srsq", "sxn", "strA", "strAe",
                "scin", "sxg1", "srhs", "swdt", "sdt", "sbvp", "sbv", "sselp", "sxnk",
                "srdt", "sg", "sorB", "sorC", "sGn", "sdA", "sbn", "sh", "stq", "syq",
                "sye", "spl", "sple", "sciny", "sxg2",
                "syld", "sx2", "sx1", "srsq2", "sxn2", "strC", "sxn2e", "sx1w",
                "sw1s", "sw1", "sh1", "sw2s", "sw2", "sy2", "sx1e", "sxaw",
                "srsq3", "sxf", "strD", "sxfe", "sgtp", "sgt", "swv", "swTp", "swT",
                "sups", "sup", "sha", "sdns", "sdn", "scmb", "sxar", "sfin", "sout", "sbvd", "sbcB", "sbcC", "sw1sA", "sw1sB", "sw2sA", "sw2sB", "supsA", "supsB", "sdnsA", "sdnsB", "scin0", "scin1", "scin2", "scin3", "sx2A", "sx2B", "sx1wA", "sx1wB", "sx3A", "sx3B", "sx1eA", "sx1eB", "sxaL", "sxarA", "sxarB"]
    sems = {}
    for nm in semnames:
        sems[nm] = ctx.enter_context(nc.semaphore(nm))

    blk = ctx.enter_context(nc.Block())

    def W(e, sem, val):
        if val > 0:
            e.wait_ge(sems[sem], val)

    N_LD0 = S.inc("ld0", 16 * 7)

    def su_sync(e):
        for src, dst in [(d_identb[:], t_identb[:]), (d_identf[:], t_identf[:]),
                         (d_small[:], t_small[:]),
                         (d_vecs[:], t_arena[0:1, o_vrow:o_vrow + 4 * D]),
                         (d_b1c[:], t_b1c[:]), (d_sbupc[:], t_sbupc[:]),
                         (d_upbc[:], t_upbc[:])]:
            e.dma_start(out=dst, in_=src).then_inc(sems["ld0"], 16)
    S.on("sync", su_sync)

    N_CAST0 = S.inc("cast0", 16 * 6)

    def su_gp(e):
        for src, dst in [
            (d_sel.rearrange("(c p) j -> p c j", p=128), t_sel[:, :, :]),
            (d_selT[:], t_selT[:]),
            (d_wdtTk.rearrange("(c p) j -> p c j", p=128), t_wdtTk[:, :, :]),
            (d_wbcT.rearrange("(c p) j -> p c j", p=128), t_wbcT[:, :, :]),
            (d_gatew.rearrange("(c p) j -> p c j", p=128), t_gatew[:, :, :]),
            (d_downb[:], t_downb[:]),
        ]:
            e.dma_start(out=dst, in_=src).then_inc(sems["cast0"], 16)
    S.on("gpsimd", su_gp)

    n1 = S.inc("setup")

    def su_s(e):
        W(e, "ld0", N_LD0)
        e.activation(out=t_Arow[:], in_=t_small[0:1, 9:9 + N], func=AF.Exp).then_inc(sems["setup"], 1)
    S.on("scalar", su_s)

    n2 = S.inc("setup")

    def su_v(e, n1=n1):
        W(e, "setup", n1)
        e.tensor_scalar_mul(out=t_Arow[:], in0=t_Arow[:], scalar1=-1.0)
        e.memset(t_onesb[:], 1.0)
        e.memset(t_onesf[:], 1.0)
        e.memset(t_state[:], 0.0)
        e.tensor_copy(out=ab(o_vrowb, 2 * D)[0:1, :],
                      in_=af(o_vrow + 2 * D, 2 * D)[0:1, :]).then_inc(sems["setup"], 1)
    S.on("vector", su_v)

    tb0 = S.at("setup")
    S.inc("setup", 6)

    def su_t(e, n2=n2):
        W(e, "setup", n2)
        W(e, "cast0", N_CAST0)
        e.matmul(p_trf[:, 0:N], t_onesf[:], t_Arow[:], start=True, stop=True).then_inc(sems["setup"], 1)
        e.transpose(p_trf[0:N, 128:129], t_Arow[:], t_identf[0:1, 0:1]).then_inc(sems["setup"], 1)
        for j in range(2):
            e.matmul(p_mm[2 * j][:], t_onesf[:],
                     af(o_vrow, 4 * D)[0:1, 512 * j:512 * (j + 1)],
                     start=True, stop=True).then_inc(sems["setup"], 1)
            e.matmul(p_mm[2 * j + 1][:], t_onesf[:],
                     af(o_vrow, 4 * D)[0:1, D + 512 * j:D + 512 * (j + 1)],
                     start=True, stop=True).then_inc(sems["setup"], 1)
    S.on("tensor", su_t)

    SETUP_DONE = S.inc("setup")

    def su_v2(e, tb0=tb0):
        W(e, "setup", tb0 + 2)
        e.tensor_copy(out=t_Arep[:], in_=p_trf[:, 0:N])
        W(e, "setup", tb0 + 2)
        e.tensor_copy(out=t_Acol[:], in_=p_trf[0:N, 128:129])
        e.memset(t_invA[:], 1.0)
        e.reciprocal(out=t_invA[0:N, :], in_=t_Acol[:])
        for j in range(2):
            W(e, "setup", tb0 + 3 + 2 * j)
            e.tensor_copy(out=t_n1rep[:, 512 * j:512 * (j + 1)], in_=p_mm[2 * j][:])
            W(e, "setup", tb0 + 4 + 2 * j)
            e.tensor_copy(out=t_n2rep[:, 512 * j:512 * (j + 1)], in_=p_mm[2 * j + 1][:])
        e.tensor_copy(out=t_col[:, 7:8], in_=t_n1rep[:, 0:1]).then_inc(sems["setup"], 1)
    S.on("vector", su_v2)

    for r in range(R):
        nxn_prev = 4 * r
        nsx_base = S.at("sx")
        S.inc("sx", 64)

        def f_xld(e, r=r, nxn_prev=nxn_prev):
            W(e, "sxn", nxn_prev)
            for i in range(4):
                e.dma_start(out=af(o_xs + 1024 * i, 1024),
                            in_=d_x[r, 128 * i:128 * (i + 1), :]).then_inc(sems["sx"], 16)
        S.on("sync", f_xld)

        for i in range(4):
            sxval = nsx_base + 64
            nvar = S.inc("svar")

            def f_v(e, r=r, i=i, sxval=sxval):
                if r == 0 and i == 0:
                    W(e, "setup", SETUP_DONE)
                W(e, "sx", sxval)
                xs = af(o_xs + 1024 * i, 1024)
                st = t_stats[:, 2 * (4 * r + i):2 * (4 * r + i) + 1]
                e.tensor_mul(out=af(o_wk1, 1024), in0=xs, in1=xs)
                e.reduce_sum(out=af(o_wk2, 1), in_=xs, axis=AX)
                e.reduce_sum(out=af(o_wk2 + 1, 1), in_=af(o_wk1, 1024), axis=AX)
                e.tensor_scalar_mul(out=st, in0=af(o_wk2, 1), scalar1=1.0 / D)
                e.tensor_mul(out=af(o_wk2 + 2, 1), in0=st, in1=st)
                e.scalar_tensor_tensor(out=af(o_wk2 + 3, 1), in0=af(o_wk2 + 1, 1),
                                       scalar=1.0 / D, in1=af(o_wk2 + 2, 1),
                                       op0=ALU.mult, op1=ALU.subtract).then_inc(sems["svar"], 1)
            S.on("vector", f_v)

            nrsq = S.inc("srsq")

            def f_s(e, r=r, i=i, nvar=nvar):
                W(e, "svar", nvar)
                var = af(o_wk2 + 3, 1)
                rst = t_stats[:, 2 * (4 * r + i) + 1:2 * (4 * r + i) + 2]
                e.activation(out=t_col[:, 0:1], in_=var, func=AF.Ln, bias=t_small[:, 6:7])
                e.activation(out=rst, in_=t_col[:, 0:1], func=AF.Exp, scale=-0.5)
                e.activation(out=t_col[:, 1:2], in_=var, func=AF.Ln, scale=1.000001,
                             bias=t_small[:, 7:8])
                e.activation(out=t_col[:, 2:3], in_=t_col[:, 1:2], func=AF.Exp,
                             scale=-0.5).then_inc(sems["srsq"], 1)
            S.on("scalar", f_s)

            nxn = S.inc("sxn")

            def f_v2(e, r=r, i=i, nrsq=nrsq):
                W(e, "srsq", nrsq)
                xs = af(o_xs + 1024 * i, 1024)
                st = t_stats[:, 2 * (4 * r + i):2 * (4 * r + i) + 1]
                e.tensor_scalar(out=af(o_wk1, 1024), in0=xs, scalar1=st,
                                scalar2=t_col[:, 2:3], op0=ALU.subtract, op1=ALU.mult)
                e.tensor_mul(out=ab(o_xntok, 1024), in0=af(o_wk1, 1024),
                             in1=t_n1rep[:]).then_inc(sems["sxn"], 1)
            S.on("vector", f_v2)

            for j in range(DC):
                ktr = S.inc("strA")

                def f_t(e, j=j, nxn=nxn, ktr=ktr):
                    if j == 0:
                        W(e, "sxn", nxn)
                    if ktr > 2:
                        W(e, "strAe", ktr - 2)
                    bk = p_trb if (ktr - 1) % 2 == 0 else p_plb
                    e.transpose(bk[:, 0:128],
                                ab(o_xntok, 1024)[:, 128 * j:128 * (j + 1)],
                                t_identb[:]).then_inc(sems["strA"], 1)
                S.on("tensor", f_t)

                S.inc("strAe")

                def f_ve(e, r=r, i=i, j=j, ktr=ktr):
                    W(e, "strA", ktr)
                    if i == 0 and j == 0 and r >= 1:
                        W(e, f"scin{r - 1}", 128)
                    bk = p_trb if (ktr - 1) % 2 == 0 else p_plb
                    e.tensor_copy(out=ab(o_xnfm + 256 * j, 512)[:, 128 * i:128 * (i + 1)],
                                  in_=bk[:, 0:128]).then_inc(sems["strAe"], 1)
                S.on("vector", f_ve)

        nevac = S.at("strAe")
        cin_sem = f"scin{r}"
        S.inc(cin_sem, 16 * 8)

        def f_cin(e, r=r, nevac=nevac, cin_sem=cin_sem):
            W(e, "strAe", nevac)
            for j in range(DC):
                e.dma_start(out=cin1[r][128 * j:128 * (j + 1), :],
                            in_=ab(o_xnfm + 256 * j, 512)).then_inc(sems[cin_sem], 16)
        S.on("sync", f_cin)

        S.inc("sxg1")

        def f_xg1(e, r=r, cin_sem=cin_sem):
            W(e, cin_sem, 128)
            e.collective_compute("AllGather", ALU.bypass, replica_groups=RG,
                                 ins=[cin1[r][:]], outs=[cout1[r][:]]).then_inc(sems["sxg1"], 1)
        S.on("gpsimd", f_xg1)

    SNAP_XN = S.snapshot()
    ROUND_SNAPS = []
    sdt_guard = [0, 0]
    SY2_GUARD = 0
    for r in range(R):
        ROUND_SNAPS.append(S.snapshot())
        xg_sh1 = S.at("sh1")
        xg_sy2 = S.at("sy2")
        xg_sgt = S.at("sgt")
        xg_swT = S.at("swT")
        xg_setup = SETUP_DONE if r == 0 else 0
        for qq in range(4):
            h_half = qq // 2
            rk0 = 4 * h_half + 2 * (qq % 2)
            g_wdt = S.at("swdt")
            g_bvp = S.at("sbvp")
            g_selp = S.at("sselp")
            nrhs = S.at("srhs") + 16 * DC * 2
            S.inc("srhs", 16 * DC * 2)

            def f_rhs(e, r=r, rk0=rk0, g_wdt=g_wdt, g_bvp=g_bvp, g_selp=g_selp):
                W(e, "sxg1", r + 1)
                W(e, "swdt", g_wdt)
                W(e, "sbvp", g_bvp)
                W(e, "sselp", g_selp)
                for j in range(DC):
                    for u in range(2):
                        e.dma_start(out=ab(o_rhs + 512 * j, 1024)[:, 512 * u:512 * (u + 1)],
                                    in_=cout1[r][rk0 + u, 128 * j:128 * (j + 1), :]
                                    ).then_inc(sems["srhs"], 16)
            S.on("sync", f_rhs)

            for tt in range(2):
                g_sdt = sdt_guard[tt]
                nwdt = S.inc("swdt")

                first_q = (qq == 0 and tt == 0)

                def f_wdt(e, tt=tt, nrhs=nrhs, g_sdt=g_sdt, first_q=first_q,
                          xg_sh1=xg_sh1, xg_setup=xg_setup):
                    W(e, "srhs", nrhs)
                    W(e, "sdt", g_sdt)
                    if first_q:
                        W(e, "setup", xg_setup)
                        W(e, "sh1", xg_sh1)
                    for j in range(DC - 1):
                        e.matmul(p_mm[tt][:], t_wdtTk[:, j, :],
                                 ab(o_rhs + 512 * j, 1024)[:, 512 * tt:512 * (tt + 1)],
                                 start=(j == 0), stop=False)
                    e.matmul(p_mm[tt][:], t_wdtTk[:, DC - 1, :],
                             ab(o_rhs + 512 * (DC - 1), 1024)[:, 512 * tt:512 * (tt + 1)],
                             start=False, stop=True).then_inc(sems["swdt"], 1)
                S.on("tensor", f_wdt)

                g_sdt_all = S.at("sdt")
                g_syq_all = S.at("syq")
                ndtv = S.inc("sdt")
                sdt_guard[tt] = ndtv

                def f_vsp(e, tt=tt, nwdt=nwdt, g_sdt_all=g_sdt_all, g_syq_all=g_syq_all):
                    W(e, "swdt", nwdt)
                    if tt == 0:
                        W(e, "sdt", g_sdt_all)
                        W(e, "syq", g_syq_all)
                    e.tensor_scalar(out=af(o_tq + 512 * tt, 512), in0=p_mm[tt][:],
                                    scalar1=t_small[:, 1:2], scalar2=80.0,
                                    op0=ALU.add, op1=ALU.min).then_inc(sems["sdt"], 1)
                S.on("vector", f_vsp)

            for tt in range(2):
                g_bve = S.at("sbv")
                nbvp = S.inc("sbvp")

                def f_bv(e, tt=tt, qq=qq, nrhs=nrhs, g_bve=g_bve, xg_sgt=xg_sgt,
                         xg_sy2=xg_sy2, xg_setup=xg_setup):
                    W(e, "srhs", nrhs)
                    W(e, "sbv", g_bve)
                    if qq == 0 and tt == 0:
                        W(e, "sgt", xg_sgt)
                        W(e, "sy2", xg_sy2)
                        W(e, "setup", xg_setup)
                    for j in range(DC - 1):
                        e.matmul(p_mm[2][0:32, :], t_wbcT[:, j, :],
                                 ab(o_rhs + 512 * j, 1024)[:, 512 * tt:512 * (tt + 1)],
                                 start=(j == 0), stop=False)
                    e.matmul(p_mm[2][0:32, :], t_wbcT[:, DC - 1, :],
                             ab(o_rhs + 512 * (DC - 1), 1024)[:, 512 * tt:512 * (tt + 1)],
                             start=False, stop=True).then_inc(sems["sbvp"], 1)
                S.on("tensor", f_bv)

                S.inc("sbv")

                def f_bve(e, tt=tt, nbvp=nbvp):
                    W(e, "sbvp", nbvp)
                    dst = af(o_bvc, 1024)
                    e.tensor_scalar(out=dst[0:32, 512 * tt:512 * (tt + 1)],
                                    in0=p_mm[2][0:32, :], scalar1=t_small[0:32, 4:5],
                                    scalar2=t_invA[:], op0=ALU.add,
                                    op1=ALU.mult).then_inc(sems["sbv"], 1)
                S.on("vector", f_bve)

            g_spl = S.at("spl")
            for tt in range(2):
                g_sg = S.at("sg")
                nselp = S.inc("sselp")

                g_xnk = S.at("sxnk")

                def f_selm(e, tt=tt, qq=qq, nrhs=nrhs, g_sg=g_sg, xg_sy2=xg_sy2,
                           xg_setup=xg_setup, g_xnk=g_xnk):
                    W(e, "srhs", nrhs)
                    W(e, "sg", g_sg)
                    W(e, "sxnk", g_xnk)
                    if qq == 0 and tt == 0:
                        W(e, "sy2", xg_sy2)
                        W(e, "setup", xg_setup)
                    for j in range(DC - 1):
                        e.matmul(p_mm[3][:], t_sel[:, j, :],
                                 ab(o_rhs + 512 * j, 1024)[:, 512 * tt:512 * (tt + 1)],
                                 start=(j == 0), stop=False)
                    e.matmul(p_mm[3][:], t_sel[:, DC - 1, :],
                             ab(o_rhs + 512 * (DC - 1), 1024)[:, 512 * tt:512 * (tt + 1)],
                             start=False, stop=True).then_inc(sems["sselp"], 1)
                S.on("tensor", f_selm)

                S.inc("sxnk")

                def f_sele(e, tt=tt, nselp=nselp, g_spl=g_spl):
                    W(e, "sselp", nselp)
                    W(e, "spl", g_spl)
                    e.tensor_copy(out=ab(o_xnk, 1024)[:, 512 * tt:512 * (tt + 1)],
                                  in_=p_mm[3][:]).then_inc(sems["sxnk"], 1)
                S.on("vector", f_sele)

            ndt_now = S.at("sdt")
            g_rdt_prev = S.at("srdt") - 1 if S.at("srdt") >= 1 else 0
            nsp = S.inc("sdt")

            def f_ssp(e, ndt_now=ndt_now, g_rdt_prev=g_rdt_prev):
                W(e, "sdt", ndt_now)
                W(e, "srdt", g_rdt_prev)
                e.activation(out=af(o_dt, Q), in_=af(o_tq, Q), func=AF.Exp)
                e.activation(out=af(o_dt, Q), in_=af(o_dt, Q), func=AF.Ln,
                             bias=1.0).then_inc(sems["sdt"], 1)
            S.on("scalar", f_ssp)

            nrdt1 = S.inc("srdt")

            def f_vmax(e, nsp=nsp):
                W(e, "sdt", nsp)
                e.tensor_scalar_max(out=af(o_rdt, Q), in0=af(o_dt, Q),
                                    scalar1=1.0e-20).then_inc(sems["srdt"], 1)
            S.on("vector", f_vmax)

            nrdt2 = S.inc("srdt")

            def f_srdt(e, nrdt1=nrdt1):
                W(e, "srdt", nrdt1)
                e.activation(out=af(o_rdt, Q), in_=af(o_rdt, Q), func=AF.Ln)
                e.activation(out=af(o_rdt, Q), in_=af(o_rdt, Q), func=AF.Exp,
                             scale=-1.0).then_inc(sems["srdt"], 1)
            S.on("scalar", f_srdt)

            nxnk_now = S.at("sxnk")
            ngm = S.inc("sg")
            ng = S.inc("sg")

            def f_vg(e, nrdt2=nrdt2, nxnk_now=nxnk_now):
                W(e, "srdt", nrdt2)
                W(e, "sxnk", nxnk_now)
                e.tensor_mul(out=af(o_g, Q), in0=ab(o_xnk, 1024),
                             in1=af(o_rdt, Q)).then_inc(sems["sg"], 1)
                e.memset(af(o_yacc, Q), 0.0).then_inc(sems["sg"], 1)
            S.on("vector", f_vg)

            nbv_now = S.at("sbv")
            g_bcB_prev = S.at("sbcB")
            g_bcC_prev = S.at("sbcC")
            nbvd = S.at("sbvd") + 16
            S.inc("sbvd", 16)

            def f_bvd(e, nbv_now=nbv_now, g_bcB_prev=g_bcB_prev, g_bcC_prev=g_bcC_prev):
                W(e, "sbv", nbv_now)
                W(e, "sbcB", g_bcB_prev)
                W(e, "sbcC", g_bcC_prev)
                e.dma_start(out=d_bvcd[:], in_=af(o_bvc, 1024)[0:32, :]).then_inc(sems["sbvd"], 16)
            S.on("sync", f_bvd)

            g_sye_prev = S.at("sye")
            for n_ in range(N):
                g_Gn = S.at("sGn")
                g_bn0 = S.at("sbn")
                nbcB = S.at("sbcB") + 16
                S.inc("sbcB", 16)

                def f_bcB(e, n_=n_, nbvd=nbvd, g_Gn=g_Gn, g_bn0=g_bn0):
                    W(e, "sbvd", nbvd)
                    W(e, "sGn", g_Gn)
                    W(e, "sbn", g_bn0)
                    e.dma_start(out=af(o_G, Q),
                                in_=d_bvcd[n_:n_ + 1, :].broadcast_to([128, Q])
                                ).then_inc(sems["sbcB"], 16)
                S.on("scalar", f_bcB)

                g_h = S.at("sh")
                ndA = S.inc("sdA")

                def f_sdA(e, n_=n_, nsp=nsp, g_h=g_h):
                    W(e, "sdt", nsp)
                    W(e, "sh", g_h)
                    e.activation(out=af(o_dA, Q), in_=af(o_dt, Q), func=AF.Exp,
                                 scale=t_Arep[:, n_:n_ + 1])
                    e.activation(out=af(o_bsc, Q), in_=af(o_dA, Q), func=AF.Copy,
                                 bias=-1.0).then_inc(sems["sdA"], 1)
                S.on("scalar", f_sdA)

                g_bn = S.at("sbn")
                nGn = S.inc("sGn")

                def f_vG(e, nbcB=nbcB, ngm=ngm, g_bn=g_bn):
                    W(e, "sbcB", nbcB)
                    W(e, "sg", ngm)
                    W(e, "sbn", g_bn)
                    e.tensor_mul(out=af(o_G, Q), in0=af(o_G, Q),
                                 in1=af(o_g, Q)).then_inc(sems["sGn"], 1)
                S.on("vector", f_vG)

                g_h2 = S.at("sh")
                nbn = S.inc("sbn")

                def f_gb(e, nGn=nGn, ndA=ndA, g_h2=g_h2):
                    W(e, "sGn", nGn)
                    W(e, "sdA", ndA)
                    W(e, "sh", g_h2)
                    e.tensor_mul(out=af(o_bsc, Q), in0=af(o_bsc, Q),
                                 in1=af(o_G, Q)).then_inc(sems["sbn"], 1)
                S.on("gpsimd", f_gb)

                g_tq2 = S.at("stq")
                nh = S.inc("sh")
                scol = 16 * h_half + n_

                def f_vscan(e, nbn=nbn, ndA=ndA, scol=scol, g_tq2=g_tq2):
                    W(e, "sbn", nbn)
                    W(e, "sdA", ndA)
                    W(e, "stq", g_tq2)
                    e.tensor_tensor_scan(out=af(o_h, Q), data0=af(o_dA, Q),
                                         data1=af(o_bsc, Q),
                                         initial=t_state[:, scol:scol + 1],
                                         op0=ALU.mult, op1=ALU.add)
                    e.tensor_copy(out=t_state[:, scol:scol + 1],
                                  in_=af(o_h + Q - 1, 1)).then_inc(sems["sh"], 1)
                S.on("vector", f_vscan)

                g_tq3 = S.at("stq")
                g_yq0 = S.at("syq")
                nbcC = S.at("sbcC") + 16
                S.inc("sbcC", 16)

                def f_bcC(e, n_=n_, nbvd=nbvd, g_tq3=g_tq3, g_yq0=g_yq0,
                          g_sye_prev=g_sye_prev):
                    W(e, "sbvd", nbvd)
                    W(e, "stq", g_tq3)
                    W(e, "syq", g_yq0)
                    if n_ == 0:
                        W(e, "sye", g_sye_prev)
                    e.dma_start(out=af(o_tq, Q),
                                in_=d_bvcd[N + n_:N + n_ + 1, :].broadcast_to([128, Q])
                                ).then_inc(sems["sbcC"], 16)
                S.on("scalar", f_bcC)

                g_yq = S.at("syq")
                ntq = S.inc("stq")

                def f_vt(e, nh=nh, nbcC=nbcC, g_yq=g_yq):
                    W(e, "sh", nh)
                    W(e, "sbcC", nbcC)
                    W(e, "syq", g_yq)
                    e.tensor_mul(out=af(o_tq, Q), in0=af(o_tq, Q),
                                 in1=af(o_h, Q)).then_inc(sems["stq"], 1)
                S.on("vector", f_vt)

                nyq = S.inc("syq")

                def f_gy(e, ntq=ntq, ng=ng):
                    W(e, "stq", ntq)
                    W(e, "sg", ng)
                    e.tensor_add(out=af(o_yacc, Q), in0=af(o_yacc, Q),
                                 in1=af(o_tq, Q)).then_inc(sems["syq"], 1)
                S.on("gpsimd", f_gy)

            nyq_all = S.at("syq")
            nye1 = S.inc("sye")
            nye = S.inc("sye")

            def f_vep(e, nyq_all=nyq_all):
                W(e, "syq", nyq_all)
                e.tensor_scalar_mul(out=af(o_tq, Q), in0=ab(o_xnk, 1024),
                                    scalar1=t_small[:, 2:3])
                e.tensor_add(out=af(o_yacc, Q), in0=af(o_yacc, Q), in1=af(o_tq, Q))
                e.tensor_scalar_mul(out=af(o_yacc, Q), in0=af(o_yacc, Q),
                                    scalar1=t_small[:, 3:4]).then_inc(sems["sye"], 1)
                e.tensor_copy(out=ab(o_xnk, 1024), in_=af(o_yacc, Q)).then_inc(sems["sye"], 1)
            S.on("vector", f_vep)

            for tb in range(8):
                tbh = (qq % 2) * 8 + tb
                owner = 4 * h_half + tbh // 4
                rowbase = 512 * owner + 128 * (tbh % 4)
                g_ple = S.at("sple")
                npl = S.inc("spl")

                def f_tpl(e, tb=tb, qq=qq, nye=nye, g_ple=g_ple,
                          xg_sy2=xg_sy2, xg_setup=xg_setup, xg_swT=xg_swT):
                    W(e, "sye", nye)
                    W(e, "sple", g_ple)
                    if qq == 0 and tb == 0:
                        W(e, "sy2", xg_sy2)
                        W(e, "swT", xg_swT)
                        W(e, "setup", xg_setup)
                    e.matmul(p_mm[6][:], ab(o_xnk, 1024)[:, 128 * tb:128 * (tb + 1)],
                             t_selT[:, 0:512], start=True, stop=True)
                    e.matmul(p_mm[7][:], ab(o_xnk, 1024)[:, 128 * tb:128 * (tb + 1)],
                             t_selT[:, 512:1024], start=True, stop=True).then_inc(sems["spl"], 1)
                S.on("tensor", f_tpl)

                g_cy = S.at("sciny")
                nple = S.inc("sple")

                def f_vpl(e, npl=npl, g_cy=g_cy):
                    W(e, "spl", npl)
                    W(e, "sciny", g_cy)
                    e.tensor_copy(out=ab(o_ypl, 1024)[:, 0:512], in_=p_mm[6][:])
                    e.tensor_copy(out=ab(o_ypl, 1024)[:, 512:1024],
                                  in_=p_mm[7][:]).then_inc(sems["sple"], 1)
                S.on("vector", f_vpl)

                nple2 = S.at("sple")
                S.inc("sciny", 16)
                firstpl = (qq == 0 and tb == 0)

                def f_dpl(e, r=r, rowbase=rowbase, nple2=nple2, firstpl=firstpl):
                    W(e, "sple", nple2)
                    if firstpl and r >= 1:
                        W(e, "sxg2", r)
                    e.dma_start(out=cin2[r][rowbase:rowbase + 128, :],
                                in_=ab(o_ypl, 1024)).then_inc(sems["sciny"], 16)
                S.on("sync", f_dpl)

        nciny_all = S.at("sciny")
        S.inc("sxg2")

        def f_xg2(e, r=r, nciny_all=nciny_all):
            W(e, "sciny", nciny_all)
            e.collective_compute("ReduceScatter", ALU.add, replica_groups=RG,
                                 ins=[cin2[r][:]], outs=[cout2[r][:]]).then_inc(sems["sxg2"], 1)
        S.on("gpsimd", f_xg2)

        for i in range(4):
            g_x1 = S.at("sx1")
            nyld = S.at("syld") + 16
            S.inc("syld", 16)

            def f_yld(e, r=r, i=i, g_x1=g_x1):
                W(e, "sxg2", r + 1)
                W(e, "sx1", g_x1)
                e.dma_start(out=ab(o_ysb, 1024),
                            in_=cout2[r][128 * i:128 * (i + 1), :]).then_inc(sems["syld"], 16)
            S.on("sync", f_yld)

            sl2 = "sx2A" if i % 2 == 0 else "sx2B"
            g_x1b = S.at("sx1") - 1 if S.at("sx1") >= 1 else 0
            nx2 = S.at(sl2) + 16
            S.inc(sl2, 16)

            def f_xrl(e, r=r, i=i, g_x1b=g_x1b, sl2=sl2):
                W(e, "sx1", g_x1b)
                e.dma_start(out=af(o_xs3 + 1024 * (i % 2), 1024),
                            in_=d_x[r, 128 * i:128 * (i + 1), :]).then_inc(sems[sl2], 16)
            S.on("sync", f_xrl)

            slw = "sx1wA" if i % 2 == 0 else "sx1wB"
            g_x1w = S.at(slw)
            nx1 = S.inc("sx1")

            def f_vx1(e, r=r, i=i, nyld=nyld, nx2=nx2, g_x1w=g_x1w, sl2=sl2, slw=slw):
                W(e, "syld", nyld)
                W(e, sl2, nx2)
                W(e, slw, g_x1w)
                st = t_stats[:, 2 * (4 * r + i):2 * (4 * r + i) + 2]
                x1s = af(o_x1 + 1024 * (i % 2), 1024)
                e.tensor_scalar(out=x1s, in0=af(o_xs3 + 1024 * (i % 2), 1024),
                                scalar1=st[:, 0:1], scalar2=st[:, 1:2],
                                op0=ALU.subtract, op1=ALU.mult)
                e.tensor_add(out=x1s, in0=x1s, in1=ab(o_ysb, 1024))
                e.tensor_mul(out=af(o_wk2, 1024) if False else af(o_xa1, 1024), in0=x1s, in1=x1s)
                e.reduce_sum(out=t_col[:, 3:4], in_=af(o_xa1, 1024), axis=AX).then_inc(sems["sx1"], 1)
            S.on("vector", f_vx1)

            nrsq2 = S.inc("srsq2")

            def f_srs2(e, nx1=nx1):
                W(e, "sx1", nx1)
                e.activation(out=t_col[:, 4:5], in_=t_col[:, 3:4], func=AF.Ln,
                             scale=1.0 / D, bias=t_small[:, 8:9])
                e.activation(out=t_col[:, 4:5], in_=t_col[:, 4:5], func=AF.Exp,
                             scale=-0.5).then_inc(sems["srsq2"], 1)
            S.on("scalar", f_srs2)

            nxn2 = S.inc("sxn2")

            def f_vxn2(e, i=i, nrsq2=nrsq2):
                W(e, "srsq2", nrsq2)
                x1s = af(o_x1 + 1024 * (i % 2), 1024)
                e.tensor_scalar_mul(out=af(o_xa1, 1024), in0=x1s, scalar1=t_col[:, 4:5])
                e.tensor_mul(out=ab(o_xn2, 1024), in0=af(o_xa1, 1024),
                             in1=t_n2rep[:]).then_inc(sems["sxn2"], 1)
            S.on("vector", f_vxn2)

            S.inc(slw, 16)

            def f_x1w(e, r=r, i=i, nxn2=nxn2, slw=slw):
                W(e, "sxn2", nxn2)
                e.dma_start(out=d_x1s[r, 128 * i:128 * (i + 1), :],
                            in_=af(o_x1 + 1024 * (i % 2), 1024)).then_inc(sems[slw], 16)
            S.on("sync", f_x1w)

            for j in range(DC):
                ktr = S.inc("strC")
                g_strAe = S.at("strAe")
                g_sple_r2 = S.at("sple")

                def f_t2(e, j=j, nxn2=nxn2, ktr=ktr, xg_sy2=xg_sy2, g_strAe=g_strAe,
                         g_sple_r2=g_sple_r2):
                    if j == 0:
                        W(e, "sxn2", nxn2)
                        if ktr == 1:
                            W(e, "strAe", g_strAe)
                    if ktr > 2:
                        W(e, "sxn2e", ktr - 2)
                    if ktr <= 8:
                        W(e, "sy2", xg_sy2)
                        W(e, "sple", g_sple_r2)
                    bk = p_trb if (ktr - 1) % 2 == 0 else p_plb
                    e.transpose(bk[:, 0:128],
                                ab(o_xn2, 1024)[:, 128 * j:128 * (j + 1)],
                                t_identb[:]).then_inc(sems["strC"], 1)
                S.on("tensor", f_t2)

                S.inc("sxn2e")

                def f_ve2(e, i=i, j=j, ktr=ktr):
                    W(e, "strC", ktr)
                    bk = p_trb if (ktr - 1) % 2 == 0 else p_plb
                    e.tensor_copy(out=ab(o_xn2fm + 256 * j, 512)[:, 128 * i:128 * (i + 1)],
                                  in_=bk[:, 0:128]).then_inc(sems["sxn2e"], 1)
                S.on("vector", f_ve2)

        nxn2e_all = S.at("sxn2e")
        for dffc in range(NF):
            slot_sem = "sw1sA" if dffc % 2 == 0 else "sw1sB"
            nw1s = S.at(slot_sem) + 16
            S.inc(slot_sem, 16)
            g_w1r = S.at("sw1") - 1 if S.at("sw1") >= 1 else 0

            def f_w1s(e, r=r, dffc=dffc, g_w1r=g_w1r, slot_sem=slot_sem):
                W(e, "sw1", g_w1r)
                e.dma_start(out=ab(o_wbuf + 512 * (dffc % 2), 1024)
                            .rearrange("p (c j) -> p c j", c=DC),
                            in_=d_w1c[dffc].rearrange("(c p) j -> p c j", p=128)
                            ).then_inc(sems[slot_sem], 16)
            S.on("gpsimd", f_w1s)

            g_h1 = S.at("sh1") - 1 if S.at("sh1") >= 1 else 0
            g_sdtA = S.at("sdt")
            nw1 = S.inc("sw1")

            def f_w1m(e, dffc=dffc, nw1s=nw1s, nxn2e_all=nxn2e_all, g_h1=g_h1,
                      g_sdtA=g_sdtA, slot_sem=slot_sem):
                W(e, slot_sem, nw1s)
                if dffc == 0:
                    W(e, "sxn2e", nxn2e_all)
                    W(e, "sdt", g_sdtA)
                W(e, "sh1", g_h1)
                wb = ab(o_wbuf + 512 * (dffc % 2), 1024)
                for j in range(DC - 1):
                    e.matmul(p_mm[dffc % 2][:], wb[:, 128 * j:128 * (j + 1)],
                             ab(o_xn2fm + 256 * j, 512), start=(j == 0), stop=False)
                e.matmul(p_mm[dffc % 2][:], wb[:, 128 * (DC - 1):128 * DC],
                         ab(o_xn2fm + 256 * (DC - 1), 512),
                         start=False, stop=True).then_inc(sems["sw1"], 1)
            S.on("tensor", f_w1m)

            g_sy2W1 = S.at("sy2")
            nh1 = S.inc("sh1")

            def f_sh1(e, dffc=dffc, nw1=nw1, g_sy2W1=g_sy2W1):
                W(e, "sw1", nw1)
                if dffc == 0:
                    W(e, "sy2", g_sy2W1)
                e.activation(out=ab(o_h1 + 256 * dffc, 512), in_=p_mm[dffc % 2][:],
                             func=AF.Gelu_apprx_tanh,
                             bias=t_b1c[:, dffc:dffc + 1]).then_inc(sems["sh1"], 1)
            S.on("scalar", f_sh1)

        nh1_all = S.at("sh1")
        g_ple_all = S.at("sple")
        g_sbv_all = S.at("sbv")
        g_sxnk_all = S.at("sxnk")
        g_sGn_all = S.at("sGn")
        g_stq_all = S.at("stq")
        g_sy2_prev = SY2_GUARD
        for dffc in range(NF):
            slot2 = "sw2sA" if dffc % 2 == 0 else "sw2sB"
            nw2s = S.at(slot2) + 16
            S.inc(slot2, 16)
            g_w2r = S.at("sw2")

            def f_w2s(e, dffc=dffc, g_w2r=g_w2r, slot2=slot2):
                if g_w2r >= 2:
                    W(e, "sw2", g_w2r - 1)
                e.dma_start(out=ab(o_w2buf + 512 * (dffc % 2), 1024),
                            in_=d_w2T[128 * dffc:128 * (dffc + 1), :]).then_inc(sems[slot2], 16)
            S.on("gpsimd", f_w2s)

            nw2 = S.inc("sw2")

            def f_w2m(e, r=r, dffc=dffc, nw2s=nw2s, nh1_all=nh1_all,
                      g_ple_all=g_ple_all, g_sbv_all=g_sbv_all, g_sxnk_all=g_sxnk_all,
                      g_sGn_all=g_sGn_all, g_stq_all=g_stq_all, g_sy2_prev=g_sy2_prev,
                      slot2=slot2):
                W(e, slot2, nw2s)
                if dffc == 0:
                    W(e, "sh1", nh1_all)
                    W(e, "sple", g_ple_all)
                    W(e, "sbv", g_sbv_all)
                    W(e, "sxnk", g_sxnk_all)
                    W(e, "sGn", g_sGn_all)
                    W(e, "stq", g_stq_all)
                    W(e, "sy2", g_sy2_prev)
                    for tb in range(4):
                        for dh in range(2):
                            e.matmul(p_mm[2 * tb + dh][:], t_onesb[:],
                                     ab(o_vrowb, 2 * D)[0:1, 512 * dh:512 * (dh + 1)],
                                     start=True, stop=False)
                wb = ab(o_w2buf + 512 * (dffc % 2), 1024)
                last = (dffc == NF - 1)
                for tb in range(4):
                    for dh in range(2):
                        mm = e.matmul(p_mm[2 * tb + dh][:],
                                      ab(o_h1 + 256 * dffc, 512)[:, 128 * tb:128 * (tb + 1)],
                                      wb[:, 512 * dh:512 * (dh + 1)],
                                      start=False, stop=last)
                        if tb == 3 and dh == 1:
                            mm.then_inc(sems["sw2"], 1)
            S.on("tensor", f_w2m)
        nw2_all = S.at("sw2")

        nx1_all = S.at("sx1")
        ny2_list = []
        for tb in range(4):
            sl3 = "sx3A" if tb % 2 == 0 else "sx3B"
            sle = "sx1eA" if tb % 2 == 0 else "sx1eB"
            g_y2a = S.at("sy2") - 2 if tb >= 2 else 0
            nx2b = S.at(sl3) + 16
            S.inc(sl3, 16)
            nx1e = S.at(sle) + 16
            S.inc(sle, 16)

            g_w1A = S.at("sx1wA")
            g_w1B = S.at("sx1wB")

            def f_rl(e, r=r, tb=tb, g_y2a=g_y2a, nx1_all=nx1_all, nw2_all=nw2_all,
                     sl3=sl3, sle=sle, g_w1A=g_w1A, g_w1B=g_w1B):
                W(e, "sx1", nx1_all)
                W(e, "sy2", g_y2a)
                W(e, "sw2", nw2_all)
                W(e, "sx1wA", g_w1A)
                W(e, "sx1wB", g_w1B)
                e.dma_start(out=af(o_xs3 + 1024 * (tb % 2), 1024),
                            in_=d_x[r, 128 * tb:128 * (tb + 1), :]).then_inc(sems[sl3], 16)
                e.dma_start(out=af(o_x1e + 1024 * (tb % 2), 1024),
                            in_=d_x1s[r, 128 * tb:128 * (tb + 1), :]).then_inc(sems[sle], 16)
            S.on("sync", f_rl)

            g_xaw = S.at("sxaw")
            ny2 = S.inc("sy2")
            ny2_list.append(ny2)

            def f_vy2(e, tb=tb, nw2_all=nw2_all, nx2b=nx2b, nx1e=nx1e, g_xaw=g_xaw,
                      sl3=sl3, sle=sle):
                W(e, "sw2", nw2_all)
                W(e, sl3, nx2b)
                W(e, sle, nx1e)
                W(e, "sxaw", g_xaw)
                for dh in range(2):
                    e.tensor_add(out=af(o_xa1 + 512 * dh, 512),
                                 in0=p_mm[2 * tb + dh][:],
                                 in1=af(o_x1e + 1024 * (tb % 2) + 512 * dh, 512))
                e.tensor_add(out=af(o_xa1, 1024), in0=af(o_xa1, 1024),
                             in1=af(o_xs3 + 1024 * (tb % 2), 1024)).then_inc(sems["sy2"], 1)
            S.on("vector", f_vy2)
            SY2_GUARD = ny2

            S.inc("sxaw", 16)

            def f_xaw(e, r=r, tb=tb, ny2=ny2):
                W(e, "sy2", ny2)
                e.dma_start(out=d_xa[r, 128 * tb:128 * (tb + 1), :],
                            in_=af(o_xa1, 1024)).then_inc(sems["sxaw"], 16)
            S.on("sync", f_xaw)

        ny2_all_r = S.at("sy2")
        for tb in range(4):
            nxar2 = S.at("sxaL") + 16
            S.inc("sxaL", 16)
            g_xf_prev = S.at("sxf")

            g_xaw_all = S.at("sxaw")

            def f_xarl(e, r=r, tb=tb, g_xf_prev=g_xf_prev, g_xaw_all=g_xaw_all):
                W(e, "sxf", g_xf_prev)
                W(e, "sxaw", g_xaw_all)
                e.dma_start(out=af(o_xa1, 1024),
                            in_=d_xa[r, 128 * tb:128 * (tb + 1), :]).then_inc(sems["sxaL"], 16)
            S.on("sync", f_xarl)

            nv3 = S.inc("svar")

            def f_vln1(e, nxar2=nxar2):
                W(e, "sxaL", nxar2)
                e.tensor_mul(out=af(o_wkA, 1024), in0=af(o_xa1, 1024), in1=af(o_xa1, 1024))
                e.reduce_sum(out=t_col[:, 5:6], in_=af(o_xa1, 1024), axis=AX)
                e.reduce_sum(out=t_col[:, 6:7], in_=af(o_wkA, 1024), axis=AX)
                e.tensor_scalar_mul(out=t_col[:, 5:6], in0=t_col[:, 5:6], scalar1=1.0 / D)
                e.tensor_mul(out=t_col[:, 2:3], in0=t_col[:, 5:6], in1=t_col[:, 5:6])
                e.scalar_tensor_tensor(out=t_col[:, 3:4], in0=t_col[:, 6:7],
                                       scalar=1.0 / D, in1=t_col[:, 2:3],
                                       op0=ALU.mult, op1=ALU.subtract).then_inc(sems["svar"], 1)
            S.on("vector", f_vln1)

            nrsq3 = S.inc("srsq3")

            def f_sln1(e, nv3=nv3):
                W(e, "svar", nv3)
                e.activation(out=t_col[:, 4:5], in_=t_col[:, 3:4], func=AF.Ln,
                             bias=t_small[:, 6:7])
                e.activation(out=t_col[:, 4:5], in_=t_col[:, 4:5], func=AF.Exp,
                             scale=-0.5).then_inc(sems["srsq3"], 1)
            S.on("scalar", f_sln1)

            g_strC = S.at("strC")
            nxf = S.inc("sxf")

            def f_vxf(e, nrsq3=nrsq3, g_strC=g_strC):
                W(e, "srsq3", nrsq3)
                W(e, "strC", g_strC)
                e.tensor_scalar(out=ab(o_xf, 1024), in0=af(o_xa1, 1024),
                                scalar1=t_col[:, 5:6], scalar2=t_col[:, 4:5],
                                op0=ALU.subtract, op1=ALU.mult).then_inc(sems["sxf"], 1)
            S.on("vector", f_vxf)

            for j in range(DC):
                ktr = S.inc("strD")

                def f_t3(e, j=j, nxf=nxf, ktr=ktr, ny2_all_r=ny2_all_r):
                    if j == 0:
                        W(e, "sxf", nxf)
                        W(e, "sy2", ny2_all_r)
                    if ktr > 2:
                        W(e, "sxfe", ktr - 2)
                    bk = p_trb if (ktr - 1) % 2 == 0 else p_plb
                    e.transpose(bk[:, 0:128],
                                ab(o_xf, 1024)[:, 128 * j:128 * (j + 1)],
                                t_identb[:]).then_inc(sems["strD"], 1)
                S.on("tensor", f_t3)

                S.inc("sxfe")

                def f_ve3(e, r=r, tb=tb, j=j, ktr=ktr):
                    W(e, "strD", ktr)
                    bk = p_trb if (ktr - 1) % 2 == 0 else p_plb
                    e.tensor_copy(out=ab(o_xffm + 1024 * j, 2048)[:, 512 * r + 128 * tb:
                                                                  512 * r + 128 * (tb + 1)],
                                  in_=bk[:, 0:128]).then_inc(sems["sxfe"], 1)
                S.on("vector", f_ve3)

        nxfe_all = S.at("sxfe")
        ngtp = S.inc("sgtp")

        def f_gt(e, r=r, nxfe_all=nxfe_all):
            W(e, "sxfe", nxfe_all)
            for j in range(DC - 1):
                e.matmul(p_mm[2][0:8, :], t_gatew[:, j, :],
                         ab(o_xffm + 1024 * j, 2048)[:, 512 * r:512 * (r + 1)],
                         start=(j == 0), stop=False)
            e.matmul(p_mm[2][0:8, :], t_gatew[:, DC - 1, :],
                     ab(o_xffm + 1024 * (DC - 1), 2048)[:, 512 * r:512 * (r + 1)],
                     start=False, stop=True).then_inc(sems["sgtp"], 1)
        S.on("tensor", f_gt)

        ngt = S.inc("sgt")

        def f_sgt(e, ngtp=ngtp):
            W(e, "sgtp", ngtp)
            e.activation(out=af(o_sc, 512)[0:8, :], in_=p_mm[2][0:8, :], func=AF.Sigmoid,
                         bias=t_small[0:8, 25:26]).then_inc(sems["sgt"], 1)
        S.on("scalar", f_sgt)

        nwv = S.inc("swv")

        def f_vtop(e, r=r, ngt=ngt):
            W(e, "sgt", ngt)
            s = [af(o_sc + 512 * k, 512)[0:32, :] for k in range(6)]
            rot = lambda sh: [(i // 8) * 8 + ((i % 8) + sh) % 8 for i in range(32)]
            e.stream_shuffle(out=s[2], in_=s[0], mask=rot(4))
            e.tensor_max(out=s[1], in0=s[0], in1=s[2])
            e.stream_shuffle(out=s[2], in_=s[1], mask=rot(2))
            e.tensor_max(out=s[1], in0=s[1], in1=s[2])
            e.stream_shuffle(out=s[2], in_=s[1], mask=rot(1))
            e.tensor_max(out=s[1], in0=s[1], in1=s[2])
            e.tensor_tensor(out=s[2], in0=s[0], in1=s[1], op=ALU.is_equal)
            e.tensor_scalar(out=s[2], in0=s[2], scalar1=t_small[0:32, 5:6], scalar2=99.0,
                            op0=ALU.mult, op1=ALU.add)
            e.stream_shuffle(out=s[3], in_=s[2], mask=rot(4))
            e.tensor_tensor(out=s[2], in0=s[2], in1=s[3], op=ALU.min)
            e.stream_shuffle(out=s[3], in_=s[2], mask=rot(2))
            e.tensor_tensor(out=s[2], in0=s[2], in1=s[3], op=ALU.min)
            e.stream_shuffle(out=s[3], in_=s[2], mask=rot(1))
            e.tensor_tensor(out=s[2], in0=s[2], in1=s[3], op=ALU.min)
            e.tensor_scalar(out=s[3], in0=s[2], scalar1=t_small[0:32, 0:1], scalar2=None,
                            op0=ALU.is_equal, op1=ALU.bypass)
            e.scalar_tensor_tensor(out=s[2], in0=s[3], scalar=-1.0e30, in1=s[0],
                                   op0=ALU.mult, op1=ALU.add)
            e.stream_shuffle(out=s[5], in_=s[2], mask=rot(4))
            e.tensor_max(out=s[4], in0=s[2], in1=s[5])
            e.stream_shuffle(out=s[5], in_=s[4], mask=rot(2))
            e.tensor_max(out=s[4], in0=s[4], in1=s[5])
            e.stream_shuffle(out=s[5], in_=s[4], mask=rot(1))
            e.tensor_max(out=s[4], in0=s[4], in1=s[5])
            e.tensor_tensor(out=s[5], in0=s[2], in1=s[4], op=ALU.is_equal)
            e.tensor_scalar(out=s[5], in0=s[5], scalar1=t_small[0:32, 5:6], scalar2=99.0,
                            op0=ALU.mult, op1=ALU.add)
            e.stream_shuffle(out=s[2], in_=s[5], mask=rot(4))
            e.tensor_tensor(out=s[5], in0=s[5], in1=s[2], op=ALU.min)
            e.stream_shuffle(out=s[2], in_=s[5], mask=rot(2))
            e.tensor_tensor(out=s[5], in0=s[5], in1=s[2], op=ALU.min)
            e.stream_shuffle(out=s[2], in_=s[5], mask=rot(1))
            e.tensor_tensor(out=s[5], in0=s[5], in1=s[2], op=ALU.min)
            e.tensor_scalar(out=s[2], in0=s[5], scalar1=t_small[0:32, 0:1], scalar2=None,
                            op0=ALU.is_equal, op1=ALU.bypass)
            e.tensor_add(out=s[5], in0=s[1], in1=s[4])
            e.tensor_scalar(out=s[5], in0=s[5], scalar1=1.0e-6, scalar2=10.0,
                            op0=ALU.add, op1=ALU.mult)
            e.reciprocal(out=s[5], in_=s[5])
            e.tensor_mul(out=s[1], in0=s[3], in1=s[1])
            e.tensor_mul(out=s[4], in0=s[2], in1=s[4])
            e.tensor_add(out=s[1], in0=s[1], in1=s[4])
            e.tensor_mul(out=af(o_wfm, 2048)[0:32, 512 * r:512 * (r + 1)],
                         in0=s[1], in1=s[5])
            e.tensor_copy(out=ab(o_wbf, 2048)[0:32, 512 * r:512 * (r + 1)],
                          in_=af(o_wfm, 2048)[0:32, 512 * r:512 * (r + 1)]).then_inc(sems["swv"], 1)
        S.on("vector", f_vtop)

        for i in range(4):
            nwtp = S.inc("swTp")

            g_sple_r = S.at("sple")

            def f_twt(e, r=r, i=i, nwv=nwv, nwtp=nwtp, g_sple_r=g_sple_r):
                if i == 0:
                    W(e, "swv", nwv)
                    W(e, "sple", g_sple_r)
                if nwtp > 1:
                    W(e, "swT", nwtp - 1)
                e.transpose(p_trf[:, 0:8],
                            af(o_wfm, 2048)[0:8, 512 * r + 128 * i:512 * r + 128 * (i + 1)],
                            t_identf[0:8, 0:8]).then_inc(sems["swTp"], 1)
            S.on("tensor", f_twt)

            S.inc("swT")

            def f_vwt(e, r=r, i=i, nwtp=nwtp):
                W(e, "swTp", nwtp)
                e.tensor_copy(out=t_wtok[:, (4 * r + i) * 8:(4 * r + i) * 8 + 8],
                              in_=p_trf[:, 0:8]).then_inc(sems["swT"], 1)
            S.on("vector", f_vwt)

    SNAP_ROUNDS = S.snapshot()
    BAR_WAITS = [(nm, S.at(nm)) for nm in
                 ["sciny", "sxaw", "sx1wA", "sx1wB", "sy2", "sxfe", "swT",
                  "sdt", "srdt", "sg", "sh", "syq", "sye", "sple", "sh1",
                  "sw1sA", "sw1sB", "sw2sA", "sw2sB", "sx2A", "sx2B",
                  "sx3A", "sx3B", "sx1eA", "sx1eB", "sxaL", "sgt", "swv",
                  "sxn2e", "sbn", "sGn", "stq", "sbv", "sxnk", "srhs",
                  "sx", "sxn", "sbcB", "sbcC", "sbvd", "sx1", "sxn2",
                  "srsq3", "sxf", "sxg2"]]
    NBAR = S.inc("setup")

    def f_bar(e):
        for nm, v in BAR_WAITS:
            W(e, nm, v)
        e.sem_inc(sems["setup"], 1)
    S.on("gpsimd", f_bar)

    first_onB = {"sync": True, "vector": True, "scalar": True, "tensor": True,
                 "gpsimd": True}

    def bwait(eng, e):
        if first_onB[eng]:
            W(e, "setup", NBAR)
            first_onB[eng] = False

    scmb_guard = 0
    for hb in range(2):
        for e_ in range(E + 1):
            g_dn_prev = S.at("sdn")
            g_cmb_prev = scmb_guard
            for dffc in range(NF):
                slotu = "supsA" if dffc % 2 == 0 else "supsB"
                nups = S.at(slotu) + 16
                S.inc(slotu, 16)
                g_upr = S.at("sup")

                def f_ups(e, hb=hb, e_=e_, dffc=dffc, g_upr=g_upr, slotu=slotu):
                    bwait("gpsimd", e)
                    if g_upr >= 2:
                        W(e, "sup", g_upr - 1)
                    src = d_swupc[dffc] if e_ == 0 else d_upwc[e_ - 1, dffc]
                    e.dma_start(out=ab(o_upb + 512 * (dffc % 2), 1024)
                                .rearrange("p (c j) -> p c j", c=DC),
                                in_=src.rearrange("(c p) j -> p c j", p=128)
                                ).then_inc(sems[slotu], 16)
                S.on("gpsimd", f_ups)

                g_ha = S.at("sha") - 1 if S.at("sha") >= 1 else 0
                nup = S.inc("sup")

                def f_upm(e, hb=hb, e_=e_, dffc=dffc, nups=nups, g_ha=g_ha,
                          g_cmb_prev=g_cmb_prev, slotu=slotu):
                    bwait("tensor", e)
                    W(e, slotu, nups)
                    W(e, "sha", g_ha)
                    if dffc <= 1:
                        W(e, "scmb", g_cmb_prev)
                    wb = ab(o_upb + 512 * (dffc % 2), 1024)
                    pbase = 4 if dffc % 2 == 0 else 2
                    for tt in range(2):
                        for j in range(DC):
                            mm = e.matmul(p_mm[pbase + tt][:], wb[:, 128 * j:128 * (j + 1)],
                                          ab(o_xffm + 1024 * j, 2048)[:, 1024 * hb + 512 * tt:
                                                                      1024 * hb + 512 * (tt + 1)],
                                          start=(j == 0), stop=(j == DC - 1))
                            if tt == 1 and j == DC - 1:
                                mm.then_inc(sems["sup"], 1)
                S.on("tensor", f_upm)

                nha = S.inc("sha")

                def f_sup(e, e_=e_, dffc=dffc, nup=nup, g_dn_prev=g_dn_prev):
                    bwait("scalar", e)
                    W(e, "sup", nup)
                    W(e, "sdn", g_dn_prev)
                    bc = t_sbupc[:, dffc:dffc + 1] if e_ == 0 else \
                        t_upbc[:, (e_ - 1) * NF + dffc:(e_ - 1) * NF + dffc + 1]
                    pbase = 4 if dffc % 2 == 0 else 2
                    for tt in range(2):
                        act = e.activation(out=ab(o_hact + 512 * dffc, 1024)[:, 512 * tt:512 * (tt + 1)],
                                           in_=p_mm[pbase + tt][:], func=AF.Gelu, bias=bc)
                        if tt == 1:
                            act.then_inc(sems["sha"], 1)
                S.on("scalar", f_sup)

            nha_all = S.at("sha")

            for dh in range(2):
                for dffc in range(NF):
                    slotd = "sdnsA" if dffc % 2 == 0 else "sdnsB"
                    ndns = S.at(slotd) + 16
                    S.inc(slotd, 16)
                    g_dnr = S.at("sdn")

                    def f_dns(e, e_=e_, dffc=dffc, dh=dh, g_dnr=g_dnr, slotd=slotd):
                        if g_dnr >= 2:
                            W(e, "sdn", g_dnr - 1)
                        src = d_swdn[128 * dffc:128 * (dffc + 1), 512 * dh:512 * (dh + 1)] \
                            if e_ == 0 else \
                            d_dnw[e_ - 1, 128 * dffc:128 * (dffc + 1), 512 * dh:512 * (dh + 1)]
                        e.dma_start(out=ab(o_dnb + 512 * (dffc % 2), 1024)[:, 0:512],
                                    in_=src).then_inc(sems[slotd], 16)
                    S.on("gpsimd", f_dns)

                    ndn = S.inc("sdn")

                    def f_dnm(e, e_=e_, hb=hb, dffc=dffc, dh=dh, ndns=ndns,
                              nha_all=nha_all, scmb_g=scmb_guard, slotd=slotd):
                        W(e, slotd, ndns)
                        if dffc == 0:
                            W(e, "sha", nha_all)
                            W(e, "scmb", scmb_g)
                            for tb in range(8):
                                if e_ == 0:
                                    e.matmul(p_mm[tb][:], t_onesb[:],
                                             ab(o_vrowb, 2 * D)[0:1, D + 512 * dh:D + 512 * (dh + 1)],
                                             start=True, stop=False)
                                else:
                                    e.matmul(p_mm[tb][:],
                                             ab(o_wbf, 2048)[0:8, 1024 * hb + 128 * tb:
                                                             1024 * hb + 128 * (tb + 1)],
                                             t_downb[:, 512 * dh:512 * (dh + 1)],
                                             start=True, stop=False)
                        wb = ab(o_dnb + 512 * (dffc % 2), 1024)[:, 0:512]
                        last = (dffc == NF - 1)
                        for tb in range(8):
                            mm = e.matmul(p_mm[tb][:],
                                          ab(o_hact + 512 * dffc, 1024)[:, 128 * tb:128 * (tb + 1)],
                                          wb, start=False, stop=last)
                            if tb == 7:
                                mm.then_inc(sems["sdn"], 1)
                    S.on("tensor", f_dnm)

                ndn_all = S.at("sdn")
                ncmb = S.inc("scmb")

                def f_vcmb(e, e_=e_, hb=hb, dh=dh, ndn_all=ndn_all):
                    bwait("vector", e)
                    W(e, "sdn", ndn_all)
                    for tb in range(8):
                        dst = af(o_comb + 1024 * tb + 512 * dh, 512)
                        if e_ == 0:
                            e.tensor_scalar_mul(out=dst, in0=p_mm[tb][:], scalar1=0.1)
                        else:
                            wcol = t_wtok[:, (8 * hb + tb) * 8 + (e_ - 1):
                                          (8 * hb + tb) * 8 + e_]
                            e.scalar_tensor_tensor(out=dst, in0=p_mm[tb][:], scalar=wcol,
                                                   in1=dst, op0=ALU.mult, op1=ALU.add)
                    e.tensor_copy(out=t_col[:, 7:8], in_=t_col[:, 7:8]).then_inc(sems["scmb"], 1)
                S.on("vector", f_vcmb)
                scmb_guard = ncmb

        for tb in range(8):
            r_ = 2 * hb + tb // 4
            i_ = tb % 4
            slx = "sxarA" if tb % 2 == 0 else "sxarB"
            nxar = S.at(slx) + 16
            S.inc(slx, 16)
            g_fin = S.at("sfin") - 1 if S.at("sfin") >= 1 else 0

            def f_xar(e, r_=r_, i_=i_, tb=tb, g_fin=g_fin, slx=slx):
                bwait("sync", e)
                if tb >= 2:
                    W(e, "sfin", g_fin)
                e.dma_start(out=af(o_xar + 1024 * (tb % 2), 1024),
                            in_=d_xa[r_, 128 * i_:128 * (i_ + 1), :]).then_inc(sems[slx], 16)
            S.on("sync", f_xar)

            ncmb_all = S.at("scmb")
            nfin = S.inc("sfin")

            g_out_prev = S.at("sout")

            def f_vfin(e, tb=tb, nxar=nxar, ncmb_all=ncmb_all, slx=slx,
                       g_out_prev=g_out_prev):
                W(e, "scmb", ncmb_all)
                W(e, slx, nxar)
                W(e, "sout", g_out_prev)
                cmb = af(o_comb + 1024 * tb, 1024)
                e.reduce_max(out=t_col[:, 0:1], in_=cmb, axis=AX, apply_absolute_value=True)
                e.tensor_scalar_add(out=t_col[:, 0:1], in0=t_col[:, 0:1], scalar1=1.0e-6)
                e.reciprocal(out=t_col[:, 1:2], in_=t_col[:, 0:1])
                e.tensor_scalar_mul(out=af(o_fin, 1024), in0=cmb, scalar1=t_col[:, 1:2])
                e.tensor_add(out=af(o_fin, 1024), in0=af(o_fin, 1024),
                             in1=af(o_xar + 1024 * (tb % 2), 1024)).then_inc(sems["sfin"], 1)
            S.on("vector", f_vfin)

            S.inc("sout", 16)

            def f_out(e, r_=r_, i_=i_, nfin=nfin):
                W(e, "sfin", nfin)
                e.dma_start(out=d_out[r_, 128 * i_:128 * (i_ + 1), :],
                            in_=af(o_fin, 1024)).then_inc(sems["sout"], 16)
            S.on("sync", f_out)

    SNAP_FULL = S.snapshot()
    import os as _os
    if _os.environ.get("PRINT_SNAPS"):
        print("SNAP_XN", SNAP_XN[0], "ROUNDS", [r[0] for r in ROUND_SNAPS],
              "SNAP_ROUNDS", SNAP_ROUNDS[0], "FULL", SNAP_FULL[0])
    if stage is not None:
        snaps = {"xn": SNAP_XN, "r1": ROUND_SNAPS[1] if len(ROUND_SNAPS) > 1 else SNAP_ROUNDS,
                 "r2": ROUND_SNAPS[2] if len(ROUND_SNAPS) > 2 else SNAP_ROUNDS,
                 "r3": ROUND_SNAPS[3] if len(ROUND_SNAPS) > 3 else SNAP_ROUNDS,
                 "rounds": SNAP_ROUNDS, "full": SNAP_FULL}
        if isinstance(stage, str) and stage.startswith("cut:"):
            gcut = int(stage.split(":")[1])
            cnts = S.cnt_log[gcut - 1] if 0 < gcut <= len(S.cnt_log) else (
                dict(S.cnt) if gcut > 0 else {})
        else:
            gcut, cnts = snaps[stage]
        for k in S.q:
            S.q[k] = [(g, f) for (g, f) in S.q[k] if g < gcut]

        def f_endS(e, cnts=cnts):
            for nm, v in cnts.items():
                W(e, nm, v)
        S.on("gpsimd", f_endS)
    else:
        NOUT = S.at("sout")

        def f_end(e):
            W(e, "sout", NOUT)
        S.on("gpsimd", f_end)

    class DrainProxy:
        _PASS = {"wait_ge", "sem_inc", "drain", "nop"}

        def __init__(self, eng):
            self._e = eng

        def __getattr__(self, name):
            fn = getattr(self._e, name)
            if name in DrainProxy._PASS or not callable(fn):
                return fn

            def wrapped(*a, **k):
                self._e.drain()
                return fn(*a, **k)
            return wrapped

    @blk.sync
    def _(e):
        for _, fn in S.q["sync"]:
            fn(e)

    @blk.vector
    def _(e):
        p = DrainProxy(e)
        for _, fn in S.q["vector"]:
            fn(p)

    @blk.scalar
    def _(e):
        p = DrainProxy(e)
        for _, fn in S.q["scalar"]:
            fn(p)

    @blk.tensor
    def _(e):
        for _, fn in S.q["tensor"]:
            fn(e)

    @blk.gpsimd
    def _(e):
        p = DrainProxy(e)
        for _, fn in S.q["gpsimd"]:
            fn(p)

    ctx.close()
    return nc


_NC_CACHE = [None]


def kernel(L0_attn_x, L0_attn_norm1_w, L0_attn_norm2_w, L0_attn_A_log, L0_attn_D_param,
           L0_attn_scale, L0_attn_Wdt, L0_attn_bdt, L0_attn_WB, L0_attn_bB, L0_attn_WC,
           L0_attn_bC, L0_attn_W1, L0_attn_b1, L0_attn_W2, L0_attn_b2, L0_ffn_gate_w,
           L0_ffn_bias, L0_ffn_up_w, L0_ffn_up_b, L0_ffn_down_w, L0_ffn_down_b,
           L0_ffn_sw_up, L0_ffn_sb_up, L0_ffn_sw_down, L0_ffn_sb_down):
    x = np.asarray(L0_attn_x, dtype=np.float32)
    WdtT = np.ascontiguousarray(np.asarray(L0_attn_Wdt, np.float32).T)
    wbcT = np.ascontiguousarray(
        np.concatenate([np.asarray(L0_attn_WB, np.float32).T,
                        np.asarray(L0_attn_WC, np.float32).T], axis=1))
    w1c = np.ascontiguousarray(
        np.asarray(L0_attn_W1, np.float32).T.reshape(D, NF, 128).transpose(1, 0, 2))
    w2T = np.ascontiguousarray(np.asarray(L0_attn_W2, np.float32).T)
    swupc = np.ascontiguousarray(
        np.asarray(L0_ffn_sw_up, np.float32).reshape(D, NF, 128).transpose(1, 0, 2))
    swdn = np.ascontiguousarray(np.asarray(L0_ffn_sw_down, np.float32))
    upwc = np.ascontiguousarray(
        np.asarray(L0_ffn_up_w, np.float32).reshape(E, D, NF, 128).transpose(0, 2, 1, 3))
    dnw = np.ascontiguousarray(np.asarray(L0_ffn_down_w, np.float32))
    gatew = np.ascontiguousarray(np.asarray(L0_ffn_gate_w, np.float32))
    vecs = np.concatenate([np.asarray(L0_attn_norm1_w, np.float32),
                           np.asarray(L0_attn_norm2_w, np.float32),
                           np.asarray(L0_attn_b2, np.float32),
                           np.asarray(L0_ffn_sb_down, np.float32)])[None, :]
    b1cols = np.ascontiguousarray(np.asarray(L0_attn_b1, np.float32).reshape(NF, 128).T)
    sbupcols = np.ascontiguousarray(np.asarray(L0_ffn_sb_up, np.float32).reshape(NF, 128).T)
    upbcols = np.ascontiguousarray(
        np.asarray(L0_ffn_up_b, np.float32).reshape(E, NF, 128).transpose(2, 0, 1)
        .reshape(128, E * NF))
    downb = np.ascontiguousarray(np.asarray(L0_ffn_down_b, np.float32))
    identb = np.eye(128).astype(ml_dtypes.bfloat16)
    identf = np.eye(128, dtype=np.float32)

    if _NC_CACHE[0] is None:
        _NC_CACHE[0] = build_nc()
    nc = _NC_CACHE[0]

    in_maps = []
    for c in range(NC):
        s_, q_ = c // 4, c % 4
        x_my = np.stack([x[s_, 2048 * r + TS * q_:2048 * r + TS * (q_ + 1), :]
                         for r in range(R)])
        small = np.zeros((128, 32), np.float32)
        small[0, 9:25] = np.asarray(L0_attn_A_log, np.float32)
        small[0:32, 0] = np.arange(32, dtype=np.float32)
        small[0:8, 25] = np.asarray(L0_ffn_bias, np.float32)
        small[:, 1] = np.asarray(L0_attn_bdt, np.float32)[128 * c:128 * (c + 1)]
        small[:, 2] = np.asarray(L0_attn_D_param, np.float32)[128 * c:128 * (c + 1)]
        small[:, 3] = np.asarray(L0_attn_scale, np.float32)[128 * c:128 * (c + 1)]
        small[0:16, 4] = np.asarray(L0_attn_bB, np.float32)
        small[16:32, 4] = np.asarray(L0_attn_bC, np.float32)
        small[0:32, 5] = np.arange(32, dtype=np.float32) - 99.0
        small[:, 6] = 1.0e-5
        small[:, 7] = 1.0e-11
        small[:, 8] = 1.0e-6
        sel = np.zeros((D, 128), np.float32)
        sel[128 * c:128 * (c + 1), :] = np.eye(128, dtype=np.float32)
        selT = np.zeros((128, D), np.float32)
        selT[:, 128 * c:128 * (c + 1)] = np.eye(128, dtype=np.float32)
        in_maps.append({
            "x_my": np.ascontiguousarray(x_my),
            "wdtTk": np.ascontiguousarray(WdtT[:, 128 * c:128 * (c + 1)]),
            "wbcT": wbcT,
            "sel": sel.astype(ml_dtypes.bfloat16),
            "selT": selT.astype(ml_dtypes.bfloat16),
            "identb": identb, "identf": identf,
            "small": small, "vecs": vecs,
            "b1c": b1cols, "sbupc": sbupcols, "upbc": upbcols, "downb": downb,
            "gatew": gatew, "w1c": w1c, "w2T": w2T, "swupc": swupc, "swdn": swdn,
            "upwc": upwc, "dnw": dnw,
        })

    res = run_bass_kernel_spmd(nc, in_maps, core_ids=list(range(NC)))
    out = np.zeros((2, 8192, D), np.float32)
    for c in range(NC):
        s_, q_ = c // 4, c % 4
        om = res.results[c]["out_my"].reshape(R, TS, D)
        for r in range(R):
            out[s_, 2048 * r + TS * q_:2048 * r + TS * (q_ + 1), :] = om[r]
    return out


# version 05
# speedup vs baseline: 1.0711x; 1.0191x over previous
import numpy as np
import ml_dtypes
import concourse.bass as bass
import concourse.mybir as mybir
from concourse.bass_utils import run_bass_kernel_spmd

F32 = mybir.dt.float32
BF16 = mybir.dt.bfloat16
AF = mybir.ActivationFunctionType
ALU = mybir.AluOpType
AX = mybir.AxisListType.X

NC = 8
D, N, DFF, E = 1024, 16, 4096, 8
R, TR, TS, DC, NF, Q = 4, 4096, 512, 8, 32, 1024


class Sched:
    def __init__(self):
        self.q = {k: [] for k in ("sync", "vector", "scalar", "tensor", "gpsimd")}
        self.cnt = {}
        self.gseq = 0
        self.cnt_log = []

    def inc(self, sem, n=1):
        self.cnt[sem] = self.cnt.get(sem, 0) + n
        return self.cnt[sem]

    def at(self, sem):
        return self.cnt.get(sem, 0)

    def on(self, eng, fn):
        self.q[eng].append((self.gseq, fn))
        self.gseq += 1
        self.cnt_log.append(dict(self.cnt))

    def snapshot(self):
        return (self.gseq, dict(self.cnt))


def build_nc(stage=None):
    nc = bass.Bass("TRN2", target_bir_lowering=False)

    d_x = nc.dram_tensor("x_my", [R, TS, D], F32, kind="ExternalInput").ap()
    d_wdtTk = nc.dram_tensor("wdtTk", [D, 128], F32, kind="ExternalInput").ap()
    d_wbcT = nc.dram_tensor("wbcT", [D, 2 * N], F32, kind="ExternalInput").ap()
    d_sel = nc.dram_tensor("sel", [D, 128], BF16, kind="ExternalInput").ap()
    d_selT = nc.dram_tensor("selT", [128, D], BF16, kind="ExternalInput").ap()
    d_identb = nc.dram_tensor("identb", [128, 128], BF16, kind="ExternalInput").ap()
    d_identf = nc.dram_tensor("identf", [128, 128], F32, kind="ExternalInput").ap()
    d_small = nc.dram_tensor("small", [128, 32], F32, kind="ExternalInput").ap()
    d_vecs = nc.dram_tensor("vecs", [1, 4 * D], F32, kind="ExternalInput").ap()
    d_b1c = nc.dram_tensor("b1c", [128, NF], F32, kind="ExternalInput").ap()
    d_sbupc = nc.dram_tensor("sbupc", [128, NF], F32, kind="ExternalInput").ap()
    d_upbc = nc.dram_tensor("upbc", [128, E * NF], F32, kind="ExternalInput").ap()
    d_downb = nc.dram_tensor("downb", [E, D], F32, kind="ExternalInput").ap()
    d_gatew = nc.dram_tensor("gatew", [D, E], F32, kind="ExternalInput").ap()
    d_w1c = nc.dram_tensor("w1c", [NF, D, 128], F32, kind="ExternalInput").ap()
    d_w2T = nc.dram_tensor("w2T", [DFF, D], F32, kind="ExternalInput").ap()
    d_swupc = nc.dram_tensor("swupc", [NF, D, 128], F32, kind="ExternalInput").ap()
    d_swdn = nc.dram_tensor("swdn", [DFF, D], F32, kind="ExternalInput").ap()
    d_upwc = nc.dram_tensor("upwc", [E, NF, D, 128], F32, kind="ExternalInput").ap()
    d_dnw = nc.dram_tensor("dnw", [E, DFF, D], F32, kind="ExternalInput").ap()
    d_out = nc.dram_tensor("out_my", [R, TS, D], F32, kind="ExternalOutput").ap()

    d_xa = nc.dram_tensor("xa_stash", [R, TS, D], F32, kind="Internal").ap()
    d_bvcd = nc.dram_tensor("bvcd", [2 * N, Q], F32, kind="Internal").ap()
    d_x1s = nc.dram_tensor("x1_stash", [R, TS, D], F32, kind="Internal").ap()
    cin1 = [nc.dram_tensor(f"cin1_{r}", [D, TS], BF16, kind="Internal").ap() for r in range(R)]
    cout1 = [nc.dram_tensor(f"cout1_{r}", [NC, D, TS], BF16, kind="Internal",
                            addr_space="Shared").ap() for r in range(R)]
    cin2 = [nc.dram_tensor(f"cin2_{r}", [TR, D], BF16, kind="Internal").ap() for r in range(R)]
    cout2 = [nc.dram_tensor(f"cout2_{r}", [TS, D], BF16, kind="Internal").ap() for r in range(R)]
    RG = [list(range(NC))]

    S = Sched()
    from contextlib import ExitStack
    ctx = ExitStack()

    def sbt(name, shape, dt):
        return ctx.enter_context(nc.sbuf_tensor(name, shape, dt))

    t_identb = sbt("t_identb", [128, 128], BF16)
    t_identf = sbt("t_identf", [128, 128], F32)
    t_sel = sbt("t_sel", [128, DC, 128], BF16)
    t_selT = sbt("t_selT", [128, D], BF16)
    t_small = sbt("t_small", [128, 32], F32)
    t_n1rep = sbt("t_n1rep", [128, D], BF16)
    t_n2rep = sbt("t_n2rep", [128, D], BF16)
    t_downb = sbt("t_downb", [E, D], BF16)
    t_onesb = sbt("t_onesb", [1, 128], BF16)
    t_onesf = sbt("t_onesf", [1, 128], F32)
    t_Arow = sbt("t_Arow", [1, N], F32)
    t_Acol = sbt("t_Acol", [N, 1], F32)
    t_invA = sbt("t_invA", [2 * N, 1], F32)
    t_Arep = sbt("t_Arep", [128, N], F32)
    t_b1c = sbt("t_b1c", [128, NF], F32)
    t_sbupc = sbt("t_sbupc", [128, NF], F32)
    t_upbc = sbt("t_upbc", [128, E * NF], F32)
    t_wdtTk = sbt("t_wdtTk", [128, DC, 128], BF16)
    t_wbcT = sbt("t_wbcT", [128, DC, 2 * N], BF16)
    t_gatew = sbt("t_gatew", [128, DC, E], BF16)
    t_stats = sbt("t_stats", [128, 8 * R], F32)
    t_state = sbt("t_state", [128, 2 * N], F32)
    t_wtok = sbt("t_wtok", [128, 16 * E], F32)
    t_col = sbt("t_col", [128, 8], F32)

    AW = 48800
    t_arena = ctx.enter_context(nc.sbuf_tensor("arena", [128, AW], F32))
    arena_bf = t_arena.bitcast(BF16)

    def af(off, w):
        assert off + w <= AW, (off, w)
        return t_arena[:, off:off + w]

    def ab(off, wbf):
        assert 2 * off + wbf <= 2 * AW, (off, wbf)
        return arena_bf[:, 2 * off:2 * off + wbf]

    o_xffm = 0
    o_wfm = DC * 1024
    o_wbf = o_wfm + 2048
    o_vrow = o_wbf + 1024
    o_vrowb = o_vrow + 4096
    P0_END = o_vrowb + 1024
    o_rhs = P0_END
    o_dt = o_rhs + 4096
    o_rdt = o_dt + Q
    o_g = o_rdt + Q
    o_xnk = o_g + Q
    o_dA = o_xnk + 512
    o_bsc = o_dA + Q
    o_G = o_bsc + Q
    o_h = o_G + Q
    o_tq = o_h + Q
    o_yacc = o_tq + Q
    o_bvc = o_yacc + Q
    o_ypl = o_bvc + 1024
    P2_END = o_ypl + 512
    p13 = P2_END
    o_xs = p13
    o_wk1 = p13 + 4096
    o_wk2 = p13 + 5120
    o_xntok = p13 + 5128
    o_xnfm = p13 + 5640
    o_ysb = p13
    o_x1 = p13 + 512
    o_xa1 = p13 + 2560
    o_xn2 = p13 + 3584
    o_xn2fm = p13 + 4096
    o_h1 = p13 + 6144
    o_wbuf = p13 + 14336
    o_w2buf = p13 + 15360
    o_xf = o_xn2
    o_x1e = o_h1
    o_wkA = o_h1 + 2048
    o_sc = o_h1 + 3072
    o_xs3 = o_h1 + 6144
    P3_END = p13 + 16384
    assert P3_END <= AW, P3_END
    p4 = P0_END
    o_hact = p4
    o_upb = p4 + 16384
    o_dnb = p4 + 17408
    o_comb = p4 + 18432
    o_fin = p4 + 26624
    o_xar = p4 + 27648
    P4_END = p4 + 29696
    assert P4_END <= AW, P4_END

    p_mm = [ctx.enter_context(nc.psum_tensor(f"p_mm{i}", [128, 512], F32)) for i in range(8)]
    p_trb = p_mm[6].bitcast(BF16)
    p_trf = p_mm[7]
    p_plb = p_mm[7].bitcast(BF16)

    semnames = ["ld0", "cast0", "setup", "sx", "svar", "srsq", "sxn", "strA", "strAe",
                "scin", "sxg1", "srhs", "swdt", "sdt", "sbvp", "sbv", "sselp", "sxnk",
                "srdt", "sg", "sorB", "sorC", "sGn", "sdA", "sbn", "sh", "stq", "syq",
                "sye", "spl", "sple", "sciny", "sxg2",
                "syld", "sx2", "sx1", "srsq2", "sxn2", "strC", "sxn2e", "sx1w",
                "sw1s", "sw1", "sh1", "sw2s", "sw2", "sy2", "sx1e", "sxaw",
                "srsq3", "sxf", "strD", "sxfe", "sgtp", "sgt", "swv", "swTp", "swT",
                "sups", "sup", "sha", "sdns", "sdn", "scmb", "sxar", "sfin", "sout", "sbvd", "sbcB", "sbcC", "sw1sA", "sw1sB", "sw2sA", "sw2sB", "supsA", "supsB", "sdnsA", "sdnsB", "scin0", "scin1", "scin2", "scin3", "sx2A", "sx2B", "sx1wA", "sx1wB", "sx3A", "sx3B", "sx1eA", "sx1eB", "sxaL", "sxarA", "sxarB"]
    sems = {}
    for nm in semnames:
        sems[nm] = ctx.enter_context(nc.semaphore(nm))

    blk = ctx.enter_context(nc.Block())

    def W(e, sem, val):
        if val > 0:
            e.wait_ge(sems[sem], val)

    N_LD0 = S.inc("ld0", 16 * 7)

    def su_sync(e):
        for src, dst in [(d_identb[:], t_identb[:]), (d_identf[:], t_identf[:]),
                         (d_small[:], t_small[:]),
                         (d_vecs[:], t_arena[0:1, o_vrow:o_vrow + 4 * D]),
                         (d_b1c[:], t_b1c[:]), (d_sbupc[:], t_sbupc[:]),
                         (d_upbc[:], t_upbc[:])]:
            e.dma_start(out=dst, in_=src).then_inc(sems["ld0"], 16)
    S.on("sync", su_sync)

    N_CAST0 = S.inc("cast0", 16 * 6)

    def su_gp(e):
        for src, dst in [
            (d_sel.rearrange("(c p) j -> p c j", p=128), t_sel[:, :, :]),
            (d_selT[:], t_selT[:]),
            (d_wdtTk.rearrange("(c p) j -> p c j", p=128), t_wdtTk[:, :, :]),
            (d_wbcT.rearrange("(c p) j -> p c j", p=128), t_wbcT[:, :, :]),
            (d_gatew.rearrange("(c p) j -> p c j", p=128), t_gatew[:, :, :]),
            (d_downb[:], t_downb[:]),
        ]:
            e.dma_start(out=dst, in_=src).then_inc(sems["cast0"], 16)
    S.on("gpsimd", su_gp)

    n1 = S.inc("setup")

    def su_s(e):
        W(e, "ld0", N_LD0)
        e.activation(out=t_Arow[:], in_=t_small[0:1, 9:9 + N], func=AF.Exp).then_inc(sems["setup"], 1)
    S.on("scalar", su_s)

    n2 = S.inc("setup")

    def su_v(e, n1=n1):
        W(e, "setup", n1)
        e.tensor_scalar_mul(out=t_Arow[:], in0=t_Arow[:], scalar1=-1.0)
        e.memset(t_onesb[:], 1.0)
        e.memset(t_onesf[:], 1.0)
        e.memset(t_state[:], 0.0)
        e.tensor_copy(out=ab(o_vrowb, 2 * D)[0:1, :],
                      in_=af(o_vrow + 2 * D, 2 * D)[0:1, :]).then_inc(sems["setup"], 1)
    S.on("vector", su_v)

    tb0 = S.at("setup")
    S.inc("setup", 6)

    def su_t(e, n2=n2):
        W(e, "setup", n2)
        W(e, "cast0", N_CAST0)
        e.matmul(p_trf[:, 0:N], t_onesf[:], t_Arow[:], start=True, stop=True).then_inc(sems["setup"], 1)
        e.transpose(p_trf[0:N, 128:129], t_Arow[:], t_identf[0:1, 0:1]).then_inc(sems["setup"], 1)
        for j in range(2):
            e.matmul(p_mm[2 * j][:], t_onesf[:],
                     af(o_vrow, 4 * D)[0:1, 512 * j:512 * (j + 1)],
                     start=True, stop=True).then_inc(sems["setup"], 1)
            e.matmul(p_mm[2 * j + 1][:], t_onesf[:],
                     af(o_vrow, 4 * D)[0:1, D + 512 * j:D + 512 * (j + 1)],
                     start=True, stop=True).then_inc(sems["setup"], 1)
    S.on("tensor", su_t)

    SETUP_DONE = S.inc("setup")

    def su_v2(e, tb0=tb0):
        W(e, "setup", tb0 + 2)
        e.tensor_copy(out=t_Arep[:], in_=p_trf[:, 0:N])
        W(e, "setup", tb0 + 2)
        e.tensor_copy(out=t_Acol[:], in_=p_trf[0:N, 128:129])
        e.memset(t_invA[:], 1.0)
        e.reciprocal(out=t_invA[0:N, :], in_=t_Acol[:])
        for j in range(2):
            W(e, "setup", tb0 + 3 + 2 * j)
            e.tensor_copy(out=t_n1rep[:, 512 * j:512 * (j + 1)], in_=p_mm[2 * j][:])
            W(e, "setup", tb0 + 4 + 2 * j)
            e.tensor_copy(out=t_n2rep[:, 512 * j:512 * (j + 1)], in_=p_mm[2 * j + 1][:])
        e.tensor_copy(out=t_col[:, 7:8], in_=t_n1rep[:, 0:1]).then_inc(sems["setup"], 1)
    S.on("vector", su_v2)

    for r in range(R):
        nxn_prev = 4 * r
        nsx_base = S.at("sx")
        S.inc("sx", 64)

        def f_xld(e, r=r, nxn_prev=nxn_prev):
            W(e, "sxn", nxn_prev)
            for i in range(4):
                e.dma_start(out=af(o_xs + 1024 * i, 1024),
                            in_=d_x[r, 128 * i:128 * (i + 1), :]).then_inc(sems["sx"], 16)
        S.on("sync", f_xld)

        for i in range(4):
            sxval = nsx_base + 64
            nvar = S.inc("svar")

            def f_v(e, r=r, i=i, sxval=sxval):
                if r == 0 and i == 0:
                    W(e, "setup", SETUP_DONE)
                W(e, "sx", sxval)
                xs = af(o_xs + 1024 * i, 1024)
                st = t_stats[:, 2 * (4 * r + i):2 * (4 * r + i) + 1]
                e.tensor_mul(out=af(o_wk1, 1024), in0=xs, in1=xs)
                e.reduce_sum(out=af(o_wk2, 1), in_=xs, axis=AX)
                e.reduce_sum(out=af(o_wk2 + 1, 1), in_=af(o_wk1, 1024), axis=AX)
                e.tensor_scalar_mul(out=st, in0=af(o_wk2, 1), scalar1=1.0 / D)
                e.tensor_mul(out=af(o_wk2 + 2, 1), in0=st, in1=st)
                e.scalar_tensor_tensor(out=af(o_wk2 + 3, 1), in0=af(o_wk2 + 1, 1),
                                       scalar=1.0 / D, in1=af(o_wk2 + 2, 1),
                                       op0=ALU.mult, op1=ALU.subtract).then_inc(sems["svar"], 1)
            S.on("vector", f_v)

            nrsq = S.inc("srsq")

            def f_s(e, r=r, i=i, nvar=nvar):
                W(e, "svar", nvar)
                var = af(o_wk2 + 3, 1)
                rst = t_stats[:, 2 * (4 * r + i) + 1:2 * (4 * r + i) + 2]
                e.activation(out=t_col[:, 0:1], in_=var, func=AF.Ln, bias=t_small[:, 6:7])
                e.activation(out=rst, in_=t_col[:, 0:1], func=AF.Exp, scale=-0.5)
                e.activation(out=t_col[:, 1:2], in_=var, func=AF.Ln, scale=1.000001,
                             bias=t_small[:, 7:8])
                e.activation(out=t_col[:, 2:3], in_=t_col[:, 1:2], func=AF.Exp,
                             scale=-0.5).then_inc(sems["srsq"], 1)
            S.on("scalar", f_s)

            nxn = S.inc("sxn")

            def f_v2(e, r=r, i=i, nrsq=nrsq):
                W(e, "srsq", nrsq)
                xs = af(o_xs + 1024 * i, 1024)
                st = t_stats[:, 2 * (4 * r + i):2 * (4 * r + i) + 1]
                e.tensor_scalar(out=af(o_wk1, 1024), in0=xs, scalar1=st,
                                scalar2=t_col[:, 2:3], op0=ALU.subtract, op1=ALU.mult)
                e.tensor_mul(out=ab(o_xntok, 1024), in0=af(o_wk1, 1024),
                             in1=t_n1rep[:]).then_inc(sems["sxn"], 1)
            S.on("vector", f_v2)

            for j in range(DC):
                ktr = S.inc("strA")

                def f_t(e, j=j, nxn=nxn, ktr=ktr):
                    if j == 0:
                        W(e, "sxn", nxn)
                    if ktr > 2:
                        W(e, "strAe", ktr - 2)
                    bk = p_trb if (ktr - 1) % 2 == 0 else p_plb
                    e.transpose(bk[:, 0:128],
                                ab(o_xntok, 1024)[:, 128 * j:128 * (j + 1)],
                                t_identb[:]).then_inc(sems["strA"], 1)
                S.on("tensor", f_t)

                S.inc("strAe")

                def f_ve(e, r=r, i=i, j=j, ktr=ktr):
                    W(e, "strA", ktr)
                    if i == 0 and j == 0 and r >= 1:
                        W(e, f"scin{r - 1}", 128)
                    bk = p_trb if (ktr - 1) % 2 == 0 else p_plb
                    e.tensor_copy(out=ab(o_xnfm + 256 * j, 512)[:, 128 * i:128 * (i + 1)],
                                  in_=bk[:, 0:128]).then_inc(sems["strAe"], 1)
                S.on("vector", f_ve)

        nevac = S.at("strAe")
        cin_sem = f"scin{r}"
        S.inc(cin_sem, 16 * 8)

        def f_cin(e, r=r, nevac=nevac, cin_sem=cin_sem):
            W(e, "strAe", nevac)
            for j in range(DC):
                e.dma_start(out=cin1[r][128 * j:128 * (j + 1), :],
                            in_=ab(o_xnfm + 256 * j, 512)).then_inc(sems[cin_sem], 16)
        S.on("sync", f_cin)

        S.inc("sxg1")

        def f_xg1(e, r=r, cin_sem=cin_sem):
            W(e, cin_sem, 128)
            e.collective_compute("AllGather", ALU.bypass, replica_groups=RG,
                                 ins=[cin1[r][:]], outs=[cout1[r][:]]).then_inc(sems["sxg1"], 1)
        S.on("gpsimd", f_xg1)

    SNAP_XN = S.snapshot()
    ROUND_SNAPS = []
    sdt_guard = [0, 0]
    SY2_GUARD = 0
    for r in range(R):
        ROUND_SNAPS.append(S.snapshot())
        xg_sh1 = S.at("sh1")
        xg_sy2 = S.at("sy2")
        xg_sgt = S.at("sgt")
        xg_swT = S.at("swT")
        xg_setup = SETUP_DONE if r == 0 else 0
        for qq in range(4):
            h_half = qq // 2
            rk0 = 4 * h_half + 2 * (qq % 2)
            g_wdt = S.at("swdt")
            g_bvp = S.at("sbvp")
            g_selp = S.at("sselp")
            nrhs = S.at("srhs") + 16 * DC * 2
            S.inc("srhs", 16 * DC * 2)

            def f_rhs(e, r=r, rk0=rk0, g_wdt=g_wdt, g_bvp=g_bvp, g_selp=g_selp):
                W(e, "sxg1", r + 1)
                W(e, "swdt", g_wdt)
                W(e, "sbvp", g_bvp)
                W(e, "sselp", g_selp)
                for j in range(DC):
                    for u in range(2):
                        e.dma_start(out=ab(o_rhs + 512 * j, 1024)[:, 512 * u:512 * (u + 1)],
                                    in_=cout1[r][rk0 + u, 128 * j:128 * (j + 1), :]
                                    ).then_inc(sems["srhs"], 16)
            S.on("sync", f_rhs)

            for tt in range(2):
                g_sdt = sdt_guard[tt]
                nwdt = S.inc("swdt")

                first_q = (qq == 0 and tt == 0)

                def f_wdt(e, tt=tt, nrhs=nrhs, g_sdt=g_sdt, first_q=first_q,
                          xg_sh1=xg_sh1, xg_setup=xg_setup):
                    W(e, "srhs", nrhs)
                    W(e, "sdt", g_sdt)
                    if first_q:
                        W(e, "setup", xg_setup)
                        W(e, "sh1", xg_sh1)
                    for j in range(DC - 1):
                        e.matmul(p_mm[tt][:], t_wdtTk[:, j, :],
                                 ab(o_rhs + 512 * j, 1024)[:, 512 * tt:512 * (tt + 1)],
                                 start=(j == 0), stop=False)
                    e.matmul(p_mm[tt][:], t_wdtTk[:, DC - 1, :],
                             ab(o_rhs + 512 * (DC - 1), 1024)[:, 512 * tt:512 * (tt + 1)],
                             start=False, stop=True).then_inc(sems["swdt"], 1)
                S.on("tensor", f_wdt)

                g_sdt_all = S.at("sdt")
                g_syq_all = S.at("syq")
                ndtv = S.inc("sdt")
                sdt_guard[tt] = ndtv

                def f_vsp(e, tt=tt, nwdt=nwdt, g_sdt_all=g_sdt_all, g_syq_all=g_syq_all):
                    W(e, "swdt", nwdt)
                    if tt == 0:
                        W(e, "sdt", g_sdt_all)
                        W(e, "syq", g_syq_all)
                    e.tensor_scalar(out=af(o_tq + 512 * tt, 512), in0=p_mm[tt][:],
                                    scalar1=t_small[:, 1:2], scalar2=80.0,
                                    op0=ALU.add, op1=ALU.min).then_inc(sems["sdt"], 1)
                S.on("vector", f_vsp)

            for tt in range(2):
                g_bve = S.at("sbv")
                nbvp = S.inc("sbvp")

                def f_bv(e, tt=tt, qq=qq, nrhs=nrhs, g_bve=g_bve, xg_sgt=xg_sgt,
                         xg_sy2=xg_sy2, xg_setup=xg_setup):
                    W(e, "srhs", nrhs)
                    W(e, "sbv", g_bve)
                    if qq == 0 and tt == 0:
                        W(e, "sgt", xg_sgt)
                        W(e, "sy2", xg_sy2)
                        W(e, "setup", xg_setup)
                    for j in range(DC - 1):
                        e.matmul(p_mm[2][0:32, :], t_wbcT[:, j, :],
                                 ab(o_rhs + 512 * j, 1024)[:, 512 * tt:512 * (tt + 1)],
                                 start=(j == 0), stop=False)
                    e.matmul(p_mm[2][0:32, :], t_wbcT[:, DC - 1, :],
                             ab(o_rhs + 512 * (DC - 1), 1024)[:, 512 * tt:512 * (tt + 1)],
                             start=False, stop=True).then_inc(sems["sbvp"], 1)
                S.on("tensor", f_bv)

                S.inc("sbv")

                def f_bve(e, tt=tt, nbvp=nbvp):
                    W(e, "sbvp", nbvp)
                    dst = af(o_bvc, 1024)
                    e.tensor_scalar(out=dst[0:32, 512 * tt:512 * (tt + 1)],
                                    in0=p_mm[2][0:32, :], scalar1=t_small[0:32, 4:5],
                                    scalar2=t_invA[:], op0=ALU.add,
                                    op1=ALU.mult).then_inc(sems["sbv"], 1)
                S.on("vector", f_bve)

            g_spl = S.at("spl")
            for tt in range(2):
                g_sg = S.at("sg")
                nselp = S.inc("sselp")

                g_xnk = S.at("sxnk")

                def f_selm(e, tt=tt, qq=qq, nrhs=nrhs, g_sg=g_sg, xg_sy2=xg_sy2,
                           xg_setup=xg_setup, g_xnk=g_xnk):
                    W(e, "srhs", nrhs)
                    W(e, "sg", g_sg)
                    W(e, "sxnk", g_xnk)
                    if qq == 0 and tt == 0:
                        W(e, "sy2", xg_sy2)
                        W(e, "setup", xg_setup)
                    for j in range(DC - 1):
                        e.matmul(p_mm[3][:], t_sel[:, j, :],
                                 ab(o_rhs + 512 * j, 1024)[:, 512 * tt:512 * (tt + 1)],
                                 start=(j == 0), stop=False)
                    e.matmul(p_mm[3][:], t_sel[:, DC - 1, :],
                             ab(o_rhs + 512 * (DC - 1), 1024)[:, 512 * tt:512 * (tt + 1)],
                             start=False, stop=True).then_inc(sems["sselp"], 1)
                S.on("tensor", f_selm)

                S.inc("sxnk")

                def f_sele(e, tt=tt, nselp=nselp, g_spl=g_spl):
                    W(e, "sselp", nselp)
                    W(e, "spl", g_spl)
                    e.tensor_copy(out=ab(o_xnk, 1024)[:, 512 * tt:512 * (tt + 1)],
                                  in_=p_mm[3][:]).then_inc(sems["sxnk"], 1)
                S.on("vector", f_sele)

            ndt_now = S.at("sdt")
            g_rdt_prev = S.at("srdt") - 1 if S.at("srdt") >= 1 else 0
            nsp = S.inc("sdt")

            def f_ssp(e, ndt_now=ndt_now, g_rdt_prev=g_rdt_prev):
                W(e, "sdt", ndt_now)
                W(e, "srdt", g_rdt_prev)
                e.activation(out=af(o_dt, Q), in_=af(o_tq, Q), func=AF.Exp)
                e.activation(out=af(o_dt, Q), in_=af(o_dt, Q), func=AF.Ln,
                             bias=1.0).then_inc(sems["sdt"], 1)
            S.on("scalar", f_ssp)

            nrdt1 = S.inc("srdt")

            def f_vmax(e, nsp=nsp):
                W(e, "sdt", nsp)
                e.tensor_scalar_max(out=af(o_rdt, Q), in0=af(o_dt, Q),
                                    scalar1=1.0e-20).then_inc(sems["srdt"], 1)
            S.on("vector", f_vmax)

            nrdt2 = S.inc("srdt")

            def f_srdt(e, nrdt1=nrdt1):
                W(e, "srdt", nrdt1)
                e.activation(out=af(o_rdt, Q), in_=af(o_rdt, Q), func=AF.Ln)
                e.activation(out=af(o_rdt, Q), in_=af(o_rdt, Q), func=AF.Exp,
                             scale=-1.0).then_inc(sems["srdt"], 1)
            S.on("scalar", f_srdt)

            nxnk_now = S.at("sxnk")
            ngm = S.inc("sg")
            ng = S.inc("sg")

            def f_vg(e, nrdt2=nrdt2, nxnk_now=nxnk_now):
                W(e, "srdt", nrdt2)
                W(e, "sxnk", nxnk_now)
                e.tensor_mul(out=af(o_g, Q), in0=ab(o_xnk, 1024),
                             in1=af(o_rdt, Q)).then_inc(sems["sg"], 1)
                e.memset(af(o_yacc, Q), 0.0).then_inc(sems["sg"], 1)
            S.on("vector", f_vg)

            nbv_now = S.at("sbv")
            g_bcB_prev = S.at("sbcB")
            g_bcC_prev = S.at("sbcC")
            nbvd = S.at("sbvd") + 16
            S.inc("sbvd", 16)

            def f_bvd(e, nbv_now=nbv_now, g_bcB_prev=g_bcB_prev, g_bcC_prev=g_bcC_prev):
                W(e, "sbv", nbv_now)
                W(e, "sbcB", g_bcB_prev)
                W(e, "sbcC", g_bcC_prev)
                e.dma_start(out=d_bvcd[:], in_=af(o_bvc, 1024)[0:32, :]).then_inc(sems["sbvd"], 16)
            S.on("sync", f_bvd)

            g_sye_prev = S.at("sye")
            for n_ in range(N):
                g_Gn = S.at("sGn")
                g_bn0 = S.at("sbn")
                nbcB = S.at("sbcB") + 16
                S.inc("sbcB", 16)

                def f_bcB(e, n_=n_, nbvd=nbvd, g_Gn=g_Gn, g_bn0=g_bn0):
                    W(e, "sbvd", nbvd)
                    W(e, "sGn", g_Gn)
                    W(e, "sbn", g_bn0)
                    e.dma_start(out=af(o_G, Q),
                                in_=d_bvcd[n_:n_ + 1, :].broadcast_to([128, Q])
                                ).then_inc(sems["sbcB"], 16)
                S.on("scalar", f_bcB)

                g_h = S.at("sh")
                ndA = S.inc("sdA")

                def f_sdA(e, n_=n_, nsp=nsp, g_h=g_h):
                    W(e, "sdt", nsp)
                    W(e, "sh", g_h)
                    e.activation(out=af(o_dA, Q), in_=af(o_dt, Q), func=AF.Exp,
                                 scale=t_Arep[:, n_:n_ + 1])
                    e.activation(out=af(o_bsc, Q), in_=af(o_dA, Q), func=AF.Copy,
                                 bias=-1.0).then_inc(sems["sdA"], 1)
                S.on("scalar", f_sdA)

                g_bn = S.at("sbn")
                nGn = S.inc("sGn")

                def f_vG(e, nbcB=nbcB, ngm=ngm, g_bn=g_bn):
                    W(e, "sbcB", nbcB)
                    W(e, "sg", ngm)
                    W(e, "sbn", g_bn)
                    e.tensor_mul(out=af(o_G, Q), in0=af(o_G, Q),
                                 in1=af(o_g, Q)).then_inc(sems["sGn"], 1)
                S.on("vector", f_vG)

                g_h2 = S.at("sh")
                nbn = S.inc("sbn")

                def f_gb(e, nGn=nGn, ndA=ndA, g_h2=g_h2):
                    W(e, "sGn", nGn)
                    W(e, "sdA", ndA)
                    W(e, "sh", g_h2)
                    e.tensor_mul(out=af(o_bsc, Q), in0=af(o_bsc, Q),
                                 in1=af(o_G, Q)).then_inc(sems["sbn"], 1)
                S.on("gpsimd", f_gb)

                g_tq2 = S.at("stq")
                nh = S.inc("sh")
                scol = 16 * h_half + n_

                def f_vscan(e, nbn=nbn, ndA=ndA, scol=scol, g_tq2=g_tq2):
                    W(e, "sbn", nbn)
                    W(e, "sdA", ndA)
                    W(e, "stq", g_tq2)
                    e.tensor_tensor_scan(out=af(o_h, Q), data0=af(o_dA, Q),
                                         data1=af(o_bsc, Q),
                                         initial=t_state[:, scol:scol + 1],
                                         op0=ALU.mult, op1=ALU.add)
                    e.tensor_copy(out=t_state[:, scol:scol + 1],
                                  in_=af(o_h + Q - 1, 1)).then_inc(sems["sh"], 1)
                S.on("vector", f_vscan)

                g_tq3 = S.at("stq")
                g_yq0 = S.at("syq")
                nbcC = S.at("sbcC") + 16
                S.inc("sbcC", 16)

                def f_bcC(e, n_=n_, nbvd=nbvd, g_tq3=g_tq3, g_yq0=g_yq0,
                          g_sye_prev=g_sye_prev):
                    W(e, "sbvd", nbvd)
                    W(e, "stq", g_tq3)
                    W(e, "syq", g_yq0)
                    if n_ == 0:
                        W(e, "sye", g_sye_prev)
                    e.dma_start(out=af(o_tq, Q),
                                in_=d_bvcd[N + n_:N + n_ + 1, :].broadcast_to([128, Q])
                                ).then_inc(sems["sbcC"], 16)
                S.on("scalar", f_bcC)

                g_yq = S.at("syq")
                ntq = S.inc("stq")

                def f_vt(e, nh=nh, nbcC=nbcC, g_yq=g_yq):
                    W(e, "sh", nh)
                    W(e, "sbcC", nbcC)
                    W(e, "syq", g_yq)
                    e.tensor_mul(out=af(o_tq, Q), in0=af(o_tq, Q),
                                 in1=af(o_h, Q)).then_inc(sems["stq"], 1)
                S.on("vector", f_vt)

                nyq = S.inc("syq")

                def f_gy(e, ntq=ntq, ng=ng):
                    W(e, "stq", ntq)
                    W(e, "sg", ng)
                    e.tensor_add(out=af(o_yacc, Q), in0=af(o_yacc, Q),
                                 in1=af(o_tq, Q)).then_inc(sems["syq"], 1)
                S.on("gpsimd", f_gy)

            nyq_all = S.at("syq")
            nye1 = S.inc("sye")
            nye = S.inc("sye")

            def f_vep(e, nyq_all=nyq_all):
                W(e, "syq", nyq_all)
                e.tensor_scalar_mul(out=af(o_tq, Q), in0=ab(o_xnk, 1024),
                                    scalar1=t_small[:, 2:3])
                e.tensor_add(out=af(o_yacc, Q), in0=af(o_yacc, Q), in1=af(o_tq, Q))
                e.tensor_scalar_mul(out=af(o_yacc, Q), in0=af(o_yacc, Q),
                                    scalar1=t_small[:, 3:4]).then_inc(sems["sye"], 1)
                e.tensor_copy(out=ab(o_xnk, 1024), in_=af(o_yacc, Q)).then_inc(sems["sye"], 1)
            S.on("vector", f_vep)

            for tb in range(8):
                tbh = (qq % 2) * 8 + tb
                owner = 4 * h_half + tbh // 4
                rowbase = 512 * owner + 128 * (tbh % 4)
                g_ple = S.at("sple")
                npl = S.inc("spl")

                def f_tpl(e, tb=tb, qq=qq, nye=nye, g_ple=g_ple,
                          xg_sy2=xg_sy2, xg_setup=xg_setup, xg_swT=xg_swT):
                    W(e, "sye", nye)
                    W(e, "sple", g_ple)
                    if qq == 0 and tb == 0:
                        W(e, "sy2", xg_sy2)
                        W(e, "swT", xg_swT)
                        W(e, "setup", xg_setup)
                    e.matmul(p_mm[6][:], ab(o_xnk, 1024)[:, 128 * tb:128 * (tb + 1)],
                             t_selT[:, 0:512], start=True, stop=True)
                    e.matmul(p_mm[7][:], ab(o_xnk, 1024)[:, 128 * tb:128 * (tb + 1)],
                             t_selT[:, 512:1024], start=True, stop=True).then_inc(sems["spl"], 1)
                S.on("tensor", f_tpl)

                g_cy = S.at("sciny")
                nple = S.inc("sple")

                def f_vpl(e, npl=npl, g_cy=g_cy):
                    W(e, "spl", npl)
                    W(e, "sciny", g_cy)
                    e.tensor_copy(out=ab(o_ypl, 1024)[:, 0:512], in_=p_mm[6][:])
                    e.tensor_copy(out=ab(o_ypl, 1024)[:, 512:1024],
                                  in_=p_mm[7][:]).then_inc(sems["sple"], 1)
                S.on("vector", f_vpl)

                nple2 = S.at("sple")
                S.inc("sciny", 16)
                firstpl = (qq == 0 and tb == 0)

                def f_dpl(e, r=r, rowbase=rowbase, nple2=nple2, firstpl=firstpl):
                    W(e, "sple", nple2)
                    if firstpl and r >= 1:
                        W(e, "sxg2", r)
                    e.dma_start(out=cin2[r][rowbase:rowbase + 128, :],
                                in_=ab(o_ypl, 1024)).then_inc(sems["sciny"], 16)
                S.on("sync", f_dpl)

        nciny_all = S.at("sciny")
        S.inc("sxg2")

        def f_xg2(e, r=r, nciny_all=nciny_all):
            W(e, "sciny", nciny_all)
            e.collective_compute("ReduceScatter", ALU.add, replica_groups=RG,
                                 ins=[cin2[r][:]], outs=[cout2[r][:]]).then_inc(sems["sxg2"], 1)
        S.on("gpsimd", f_xg2)

        for i in range(4):
            g_x1 = S.at("sx1")
            nyld = S.at("syld") + 16
            S.inc("syld", 16)

            def f_yld(e, r=r, i=i, g_x1=g_x1):
                W(e, "sxg2", r + 1)
                W(e, "sx1", g_x1)
                e.dma_start(out=ab(o_ysb, 1024),
                            in_=cout2[r][128 * i:128 * (i + 1), :]).then_inc(sems["syld"], 16)
            S.on("sync", f_yld)

            sl2 = "sx2A" if i % 2 == 0 else "sx2B"
            g_x1b = S.at("sx1") - 1 if S.at("sx1") >= 1 else 0
            nx2 = S.at(sl2) + 16
            S.inc(sl2, 16)

            def f_xrl(e, r=r, i=i, g_x1b=g_x1b, sl2=sl2):
                W(e, "sx1", g_x1b)
                e.dma_start(out=af(o_xs3 + 1024 * (i % 2), 1024),
                            in_=d_x[r, 128 * i:128 * (i + 1), :]).then_inc(sems[sl2], 16)
            S.on("sync", f_xrl)

            slw = "sx1wA" if i % 2 == 0 else "sx1wB"
            g_x1w = S.at(slw)
            nx1 = S.inc("sx1")

            def f_vx1(e, r=r, i=i, nyld=nyld, nx2=nx2, g_x1w=g_x1w, sl2=sl2, slw=slw):
                W(e, "syld", nyld)
                W(e, sl2, nx2)
                W(e, slw, g_x1w)
                st = t_stats[:, 2 * (4 * r + i):2 * (4 * r + i) + 2]
                x1s = af(o_x1 + 1024 * (i % 2), 1024)
                e.tensor_scalar(out=x1s, in0=af(o_xs3 + 1024 * (i % 2), 1024),
                                scalar1=st[:, 0:1], scalar2=st[:, 1:2],
                                op0=ALU.subtract, op1=ALU.mult)
                e.tensor_add(out=x1s, in0=x1s, in1=ab(o_ysb, 1024))
                e.tensor_mul(out=af(o_wk2, 1024) if False else af(o_xa1, 1024), in0=x1s, in1=x1s)
                e.reduce_sum(out=t_col[:, 3:4], in_=af(o_xa1, 1024), axis=AX).then_inc(sems["sx1"], 1)
            S.on("vector", f_vx1)

            nrsq2 = S.inc("srsq2")

            def f_srs2(e, nx1=nx1):
                W(e, "sx1", nx1)
                e.activation(out=t_col[:, 4:5], in_=t_col[:, 3:4], func=AF.Ln,
                             scale=1.0 / D, bias=t_small[:, 8:9])
                e.activation(out=t_col[:, 4:5], in_=t_col[:, 4:5], func=AF.Exp,
                             scale=-0.5).then_inc(sems["srsq2"], 1)
            S.on("scalar", f_srs2)

            nxn2 = S.inc("sxn2")

            def f_vxn2(e, i=i, nrsq2=nrsq2):
                W(e, "srsq2", nrsq2)
                x1s = af(o_x1 + 1024 * (i % 2), 1024)
                e.tensor_scalar_mul(out=af(o_xa1, 1024), in0=x1s, scalar1=t_col[:, 4:5])
                e.tensor_mul(out=ab(o_xn2, 1024), in0=af(o_xa1, 1024),
                             in1=t_n2rep[:]).then_inc(sems["sxn2"], 1)
            S.on("vector", f_vxn2)

            S.inc(slw, 16)

            def f_x1w(e, r=r, i=i, nxn2=nxn2, slw=slw):
                W(e, "sxn2", nxn2)
                e.dma_start(out=d_x1s[r, 128 * i:128 * (i + 1), :],
                            in_=af(o_x1 + 1024 * (i % 2), 1024)).then_inc(sems[slw], 16)
            S.on("sync", f_x1w)

            for j in range(DC):
                ktr = S.inc("strC")
                g_strAe = S.at("strAe")
                g_sple_r2 = S.at("sple")

                def f_t2(e, j=j, nxn2=nxn2, ktr=ktr, xg_sy2=xg_sy2, g_strAe=g_strAe,
                         g_sple_r2=g_sple_r2):
                    if j == 0:
                        W(e, "sxn2", nxn2)
                        if ktr == 1:
                            W(e, "strAe", g_strAe)
                    if ktr > 2:
                        W(e, "sxn2e", ktr - 2)
                    if ktr <= 8:
                        W(e, "sy2", xg_sy2)
                        W(e, "sple", g_sple_r2)
                    bk = p_trb if (ktr - 1) % 2 == 0 else p_plb
                    e.transpose(bk[:, 0:128],
                                ab(o_xn2, 1024)[:, 128 * j:128 * (j + 1)],
                                t_identb[:]).then_inc(sems["strC"], 1)
                S.on("tensor", f_t2)

                S.inc("sxn2e")

                def f_ve2(e, i=i, j=j, ktr=ktr):
                    W(e, "strC", ktr)
                    bk = p_trb if (ktr - 1) % 2 == 0 else p_plb
                    e.tensor_copy(out=ab(o_xn2fm + 256 * j, 512)[:, 128 * i:128 * (i + 1)],
                                  in_=bk[:, 0:128]).then_inc(sems["sxn2e"], 1)
                S.on("vector", f_ve2)

        nxn2e_all = S.at("sxn2e")
        for dffc in range(NF):
            slot_sem = "sw1sA" if dffc % 2 == 0 else "sw1sB"
            nw1s = S.at(slot_sem) + 16
            S.inc(slot_sem, 16)
            g_w1r = S.at("sw1") - 1 if S.at("sw1") >= 1 else 0

            def f_w1s(e, r=r, dffc=dffc, g_w1r=g_w1r, slot_sem=slot_sem):
                W(e, "sw1", g_w1r)
                e.dma_start(out=ab(o_wbuf + 512 * (dffc % 2), 1024)
                            .rearrange("p (c j) -> p c j", c=DC),
                            in_=d_w1c[dffc].rearrange("(c p) j -> p c j", p=128)
                            ).then_inc(sems[slot_sem], 16)
            S.on("gpsimd", f_w1s)

            g_h1 = S.at("sh1") - 3 if S.at("sh1") >= 3 else 0
            g_sdtA = S.at("sdt")
            nw1 = S.inc("sw1")

            g_bvA = S.at("sbv")
            g_xnkA = S.at("sxnk")

            def f_w1m(e, dffc=dffc, nw1s=nw1s, nxn2e_all=nxn2e_all, g_h1=g_h1,
                      g_sdtA=g_sdtA, slot_sem=slot_sem, g_bvA=g_bvA, g_xnkA=g_xnkA):
                W(e, slot_sem, nw1s)
                if dffc == 0:
                    W(e, "sxn2e", nxn2e_all)
                    W(e, "sdt", g_sdtA)
                    W(e, "sbv", g_bvA)
                    W(e, "sxnk", g_xnkA)
                W(e, "sh1", g_h1)
                wb = ab(o_wbuf + 512 * (dffc % 2), 1024)
                bk = p_mm[dffc % 4]
                for j in range(DC - 1):
                    e.matmul(bk[:], wb[:, 128 * j:128 * (j + 1)],
                             ab(o_xn2fm + 256 * j, 512), start=(j == 0), stop=False)
                e.matmul(bk[:], wb[:, 128 * (DC - 1):128 * DC],
                         ab(o_xn2fm + 256 * (DC - 1), 512),
                         start=False, stop=True).then_inc(sems["sw1"], 1)
            S.on("tensor", f_w1m)

            g_sy2W1 = S.at("sy2")
            nh1 = S.inc("sh1")

            def f_sh1(e, dffc=dffc, nw1=nw1, g_sy2W1=g_sy2W1):
                W(e, "sw1", nw1)
                if dffc == 0:
                    W(e, "sy2", g_sy2W1)
                e.activation(out=ab(o_h1 + 256 * dffc, 512), in_=p_mm[dffc % 4][:],
                             func=AF.Gelu_apprx_tanh,
                             bias=t_b1c[:, dffc:dffc + 1]).then_inc(sems["sh1"], 1)
            S.on("scalar", f_sh1)

        nh1_all = S.at("sh1")
        g_ple_all = S.at("sple")
        g_sbv_all = S.at("sbv")
        g_sxnk_all = S.at("sxnk")
        g_sGn_all = S.at("sGn")
        g_stq_all = S.at("stq")
        g_sy2_prev = SY2_GUARD
        for dffc in range(NF):
            slot2 = "sw2sA" if dffc % 2 == 0 else "sw2sB"
            nw2s = S.at(slot2) + 16
            S.inc(slot2, 16)
            g_w2r = S.at("sw2")

            def f_w2s(e, dffc=dffc, g_w2r=g_w2r, slot2=slot2):
                if g_w2r >= 2:
                    W(e, "sw2", g_w2r - 1)
                e.dma_start(out=ab(o_w2buf + 512 * (dffc % 2), 1024),
                            in_=d_w2T[128 * dffc:128 * (dffc + 1), :]).then_inc(sems[slot2], 16)
            S.on("gpsimd", f_w2s)

            nw2 = S.inc("sw2")

            def f_w2m(e, r=r, dffc=dffc, nw2s=nw2s, nh1_all=nh1_all,
                      g_ple_all=g_ple_all, g_sbv_all=g_sbv_all, g_sxnk_all=g_sxnk_all,
                      g_sGn_all=g_sGn_all, g_stq_all=g_stq_all, g_sy2_prev=g_sy2_prev,
                      slot2=slot2):
                W(e, slot2, nw2s)
                if dffc == 0:
                    W(e, "sh1", nh1_all)
                    W(e, "sple", g_ple_all)
                    W(e, "sbv", g_sbv_all)
                    W(e, "sxnk", g_sxnk_all)
                    W(e, "sGn", g_sGn_all)
                    W(e, "stq", g_stq_all)
                    W(e, "sy2", g_sy2_prev)
                    for tb in range(4):
                        for dh in range(2):
                            e.matmul(p_mm[2 * tb + dh][:], t_onesb[:],
                                     ab(o_vrowb, 2 * D)[0:1, 512 * dh:512 * (dh + 1)],
                                     start=True, stop=False)
                wb = ab(o_w2buf + 512 * (dffc % 2), 1024)
                last = (dffc == NF - 1)
                for tb in range(4):
                    for dh in range(2):
                        mm = e.matmul(p_mm[2 * tb + dh][:],
                                      ab(o_h1 + 256 * dffc, 512)[:, 128 * tb:128 * (tb + 1)],
                                      wb[:, 512 * dh:512 * (dh + 1)],
                                      start=False, stop=last)
                        if tb == 3 and dh == 1:
                            mm.then_inc(sems["sw2"], 1)
            S.on("tensor", f_w2m)
        nw2_all = S.at("sw2")

        nx1_all = S.at("sx1")
        ny2_list = []
        for tb in range(4):
            sl3 = "sx3A" if tb % 2 == 0 else "sx3B"
            sle = "sx1eA" if tb % 2 == 0 else "sx1eB"
            g_y2a = S.at("sy2") - 2 if tb >= 2 else 0
            nx2b = S.at(sl3) + 16
            S.inc(sl3, 16)
            nx1e = S.at(sle) + 16
            S.inc(sle, 16)

            g_w1A = S.at("sx1wA")
            g_w1B = S.at("sx1wB")

            def f_rl(e, r=r, tb=tb, g_y2a=g_y2a, nx1_all=nx1_all, nw2_all=nw2_all,
                     sl3=sl3, sle=sle, g_w1A=g_w1A, g_w1B=g_w1B):
                W(e, "sx1", nx1_all)
                W(e, "sy2", g_y2a)
                W(e, "sw2", nw2_all)
                W(e, "sx1wA", g_w1A)
                W(e, "sx1wB", g_w1B)
                e.dma_start(out=af(o_xs3 + 1024 * (tb % 2), 1024),
                            in_=d_x[r, 128 * tb:128 * (tb + 1), :]).then_inc(sems[sl3], 16)
                e.dma_start(out=af(o_x1e + 1024 * (tb % 2), 1024),
                            in_=d_x1s[r, 128 * tb:128 * (tb + 1), :]).then_inc(sems[sle], 16)
            S.on("sync", f_rl)

            g_xaw = S.at("sxaw")
            ny2 = S.inc("sy2")
            ny2_list.append(ny2)

            def f_vy2(e, tb=tb, nw2_all=nw2_all, nx2b=nx2b, nx1e=nx1e, g_xaw=g_xaw,
                      sl3=sl3, sle=sle):
                W(e, "sw2", nw2_all)
                W(e, sl3, nx2b)
                W(e, sle, nx1e)
                W(e, "sxaw", g_xaw)
                for dh in range(2):
                    e.tensor_add(out=af(o_xa1 + 512 * dh, 512),
                                 in0=p_mm[2 * tb + dh][:],
                                 in1=af(o_x1e + 1024 * (tb % 2) + 512 * dh, 512))
                e.tensor_add(out=af(o_xa1, 1024), in0=af(o_xa1, 1024),
                             in1=af(o_xs3 + 1024 * (tb % 2), 1024)).then_inc(sems["sy2"], 1)
            S.on("vector", f_vy2)
            SY2_GUARD = ny2

            S.inc("sxaw", 16)

            def f_xaw(e, r=r, tb=tb, ny2=ny2):
                W(e, "sy2", ny2)
                e.dma_start(out=d_xa[r, 128 * tb:128 * (tb + 1), :],
                            in_=af(o_xa1, 1024)).then_inc(sems["sxaw"], 16)
            S.on("sync", f_xaw)

        ny2_all_r = S.at("sy2")
        for tb in range(4):
            nxar2 = S.at("sxaL") + 16
            S.inc("sxaL", 16)
            g_xf_prev = S.at("sxf")

            g_xaw_all = S.at("sxaw")

            def f_xarl(e, r=r, tb=tb, g_xf_prev=g_xf_prev, g_xaw_all=g_xaw_all):
                W(e, "sxf", g_xf_prev)
                W(e, "sxaw", g_xaw_all)
                e.dma_start(out=af(o_xa1, 1024),
                            in_=d_xa[r, 128 * tb:128 * (tb + 1), :]).then_inc(sems["sxaL"], 16)
            S.on("sync", f_xarl)

            nv3 = S.inc("svar")

            def f_vln1(e, nxar2=nxar2):
                W(e, "sxaL", nxar2)
                e.tensor_mul(out=af(o_wkA, 1024), in0=af(o_xa1, 1024), in1=af(o_xa1, 1024))
                e.reduce_sum(out=t_col[:, 5:6], in_=af(o_xa1, 1024), axis=AX)
                e.reduce_sum(out=t_col[:, 6:7], in_=af(o_wkA, 1024), axis=AX)
                e.tensor_scalar_mul(out=t_col[:, 5:6], in0=t_col[:, 5:6], scalar1=1.0 / D)
                e.tensor_mul(out=t_col[:, 2:3], in0=t_col[:, 5:6], in1=t_col[:, 5:6])
                e.scalar_tensor_tensor(out=t_col[:, 3:4], in0=t_col[:, 6:7],
                                       scalar=1.0 / D, in1=t_col[:, 2:3],
                                       op0=ALU.mult, op1=ALU.subtract).then_inc(sems["svar"], 1)
            S.on("vector", f_vln1)

            nrsq3 = S.inc("srsq3")

            def f_sln1(e, nv3=nv3):
                W(e, "svar", nv3)
                e.activation(out=t_col[:, 4:5], in_=t_col[:, 3:4], func=AF.Ln,
                             bias=t_small[:, 6:7])
                e.activation(out=t_col[:, 4:5], in_=t_col[:, 4:5], func=AF.Exp,
                             scale=-0.5).then_inc(sems["srsq3"], 1)
            S.on("scalar", f_sln1)

            g_strC = S.at("strC")
            nxf = S.inc("sxf")

            def f_vxf(e, nrsq3=nrsq3, g_strC=g_strC):
                W(e, "srsq3", nrsq3)
                W(e, "strC", g_strC)
                e.tensor_scalar(out=ab(o_xf, 1024), in0=af(o_xa1, 1024),
                                scalar1=t_col[:, 5:6], scalar2=t_col[:, 4:5],
                                op0=ALU.subtract, op1=ALU.mult).then_inc(sems["sxf"], 1)
            S.on("vector", f_vxf)

            for j in range(DC):
                ktr = S.inc("strD")

                def f_t3(e, j=j, nxf=nxf, ktr=ktr, ny2_all_r=ny2_all_r):
                    if j == 0:
                        W(e, "sxf", nxf)
                        W(e, "sy2", ny2_all_r)
                    if ktr > 2:
                        W(e, "sxfe", ktr - 2)
                    bk = p_trb if (ktr - 1) % 2 == 0 else p_plb
                    e.transpose(bk[:, 0:128],
                                ab(o_xf, 1024)[:, 128 * j:128 * (j + 1)],
                                t_identb[:]).then_inc(sems["strD"], 1)
                S.on("tensor", f_t3)

                S.inc("sxfe")

                def f_ve3(e, r=r, tb=tb, j=j, ktr=ktr):
                    W(e, "strD", ktr)
                    bk = p_trb if (ktr - 1) % 2 == 0 else p_plb
                    e.tensor_copy(out=ab(o_xffm + 1024 * j, 2048)[:, 512 * r + 128 * tb:
                                                                  512 * r + 128 * (tb + 1)],
                                  in_=bk[:, 0:128]).then_inc(sems["sxfe"], 1)
                S.on("vector", f_ve3)

        nxfe_all = S.at("sxfe")
        ngtp = S.inc("sgtp")

        def f_gt(e, r=r, nxfe_all=nxfe_all):
            W(e, "sxfe", nxfe_all)
            for j in range(DC - 1):
                e.matmul(p_mm[2][0:8, :], t_gatew[:, j, :],
                         ab(o_xffm + 1024 * j, 2048)[:, 512 * r:512 * (r + 1)],
                         start=(j == 0), stop=False)
            e.matmul(p_mm[2][0:8, :], t_gatew[:, DC - 1, :],
                     ab(o_xffm + 1024 * (DC - 1), 2048)[:, 512 * r:512 * (r + 1)],
                     start=False, stop=True).then_inc(sems["sgtp"], 1)
        S.on("tensor", f_gt)

        ngt = S.inc("sgt")

        def f_sgt(e, ngtp=ngtp):
            W(e, "sgtp", ngtp)
            e.activation(out=af(o_sc, 512)[0:8, :], in_=p_mm[2][0:8, :], func=AF.Sigmoid,
                         bias=t_small[0:8, 25:26]).then_inc(sems["sgt"], 1)
        S.on("scalar", f_sgt)

        nwv = S.inc("swv")

        def f_vtop(e, r=r, ngt=ngt):
            W(e, "sgt", ngt)
            s = [af(o_sc + 512 * k, 512)[0:32, :] for k in range(6)]
            rot = lambda sh: [(i // 8) * 8 + ((i % 8) + sh) % 8 for i in range(32)]
            e.stream_shuffle(out=s[2], in_=s[0], mask=rot(4))
            e.tensor_max(out=s[1], in0=s[0], in1=s[2])
            e.stream_shuffle(out=s[2], in_=s[1], mask=rot(2))
            e.tensor_max(out=s[1], in0=s[1], in1=s[2])
            e.stream_shuffle(out=s[2], in_=s[1], mask=rot(1))
            e.tensor_max(out=s[1], in0=s[1], in1=s[2])
            e.tensor_tensor(out=s[2], in0=s[0], in1=s[1], op=ALU.is_equal)
            e.tensor_scalar(out=s[2], in0=s[2], scalar1=t_small[0:32, 5:6], scalar2=99.0,
                            op0=ALU.mult, op1=ALU.add)
            e.stream_shuffle(out=s[3], in_=s[2], mask=rot(4))
            e.tensor_tensor(out=s[2], in0=s[2], in1=s[3], op=ALU.min)
            e.stream_shuffle(out=s[3], in_=s[2], mask=rot(2))
            e.tensor_tensor(out=s[2], in0=s[2], in1=s[3], op=ALU.min)
            e.stream_shuffle(out=s[3], in_=s[2], mask=rot(1))
            e.tensor_tensor(out=s[2], in0=s[2], in1=s[3], op=ALU.min)
            e.tensor_scalar(out=s[3], in0=s[2], scalar1=t_small[0:32, 0:1], scalar2=None,
                            op0=ALU.is_equal, op1=ALU.bypass)
            e.scalar_tensor_tensor(out=s[2], in0=s[3], scalar=-1.0e30, in1=s[0],
                                   op0=ALU.mult, op1=ALU.add)
            e.stream_shuffle(out=s[5], in_=s[2], mask=rot(4))
            e.tensor_max(out=s[4], in0=s[2], in1=s[5])
            e.stream_shuffle(out=s[5], in_=s[4], mask=rot(2))
            e.tensor_max(out=s[4], in0=s[4], in1=s[5])
            e.stream_shuffle(out=s[5], in_=s[4], mask=rot(1))
            e.tensor_max(out=s[4], in0=s[4], in1=s[5])
            e.tensor_tensor(out=s[5], in0=s[2], in1=s[4], op=ALU.is_equal)
            e.tensor_scalar(out=s[5], in0=s[5], scalar1=t_small[0:32, 5:6], scalar2=99.0,
                            op0=ALU.mult, op1=ALU.add)
            e.stream_shuffle(out=s[2], in_=s[5], mask=rot(4))
            e.tensor_tensor(out=s[5], in0=s[5], in1=s[2], op=ALU.min)
            e.stream_shuffle(out=s[2], in_=s[5], mask=rot(2))
            e.tensor_tensor(out=s[5], in0=s[5], in1=s[2], op=ALU.min)
            e.stream_shuffle(out=s[2], in_=s[5], mask=rot(1))
            e.tensor_tensor(out=s[5], in0=s[5], in1=s[2], op=ALU.min)
            e.tensor_scalar(out=s[2], in0=s[5], scalar1=t_small[0:32, 0:1], scalar2=None,
                            op0=ALU.is_equal, op1=ALU.bypass)
            e.tensor_add(out=s[5], in0=s[1], in1=s[4])
            e.tensor_scalar(out=s[5], in0=s[5], scalar1=1.0e-6, scalar2=10.0,
                            op0=ALU.add, op1=ALU.mult)
            e.reciprocal(out=s[5], in_=s[5])
            e.tensor_mul(out=s[1], in0=s[3], in1=s[1])
            e.tensor_mul(out=s[4], in0=s[2], in1=s[4])
            e.tensor_add(out=s[1], in0=s[1], in1=s[4])
            e.tensor_mul(out=af(o_wfm, 2048)[0:32, 512 * r:512 * (r + 1)],
                         in0=s[1], in1=s[5])
            e.tensor_copy(out=ab(o_wbf, 2048)[0:32, 512 * r:512 * (r + 1)],
                          in_=af(o_wfm, 2048)[0:32, 512 * r:512 * (r + 1)]).then_inc(sems["swv"], 1)
        S.on("vector", f_vtop)

        for i in range(4):
            nwtp = S.inc("swTp")

            g_sple_r = S.at("sple")

            def f_twt(e, r=r, i=i, nwv=nwv, nwtp=nwtp, g_sple_r=g_sple_r):
                if i == 0:
                    W(e, "swv", nwv)
                    W(e, "sple", g_sple_r)
                if nwtp > 1:
                    W(e, "swT", nwtp - 1)
                e.transpose(p_trf[:, 0:8],
                            af(o_wfm, 2048)[0:8, 512 * r + 128 * i:512 * r + 128 * (i + 1)],
                            t_identf[0:8, 0:8]).then_inc(sems["swTp"], 1)
            S.on("tensor", f_twt)

            S.inc("swT")

            def f_vwt(e, r=r, i=i, nwtp=nwtp):
                W(e, "swTp", nwtp)
                e.tensor_copy(out=t_wtok[:, (4 * r + i) * 8:(4 * r + i) * 8 + 8],
                              in_=p_trf[:, 0:8]).then_inc(sems["swT"], 1)
            S.on("vector", f_vwt)

    SNAP_ROUNDS = S.snapshot()
    BAR_WAITS = [(nm, S.at(nm)) for nm in
                 ["sciny", "sxaw", "sx1wA", "sx1wB", "sy2", "sxfe", "swT",
                  "sdt", "srdt", "sg", "sh", "syq", "sye", "sple", "sh1",
                  "sw1sA", "sw1sB", "sw2sA", "sw2sB", "sx2A", "sx2B",
                  "sx3A", "sx3B", "sx1eA", "sx1eB", "sxaL", "sgt", "swv",
                  "sxn2e", "sbn", "sGn", "stq", "sbv", "sxnk", "srhs",
                  "sx", "sxn", "sbcB", "sbcC", "sbvd", "sx1", "sxn2",
                  "srsq3", "sxf", "sxg2"]]
    NBAR = S.inc("setup")

    def f_bar(e):
        for nm, v in BAR_WAITS:
            W(e, nm, v)
        e.sem_inc(sems["setup"], 1)
    S.on("gpsimd", f_bar)

    first_onB = {"sync": True, "vector": True, "scalar": True, "tensor": True,
                 "gpsimd": True}

    def bwait(eng, e):
        if first_onB[eng]:
            W(e, "setup", NBAR)
            first_onB[eng] = False

    scmb_guard = 0
    for hb in range(2):
        for e_ in range(E + 1):
            g_dn_prev = S.at("sdn")
            g_cmb_prev = scmb_guard
            for dffc in range(NF):
                slotu = "supsA" if dffc % 2 == 0 else "supsB"
                nups = S.at(slotu) + 16
                S.inc(slotu, 16)
                g_upr = S.at("sup")

                def f_ups(e, hb=hb, e_=e_, dffc=dffc, g_upr=g_upr, slotu=slotu):
                    bwait("gpsimd", e)
                    if g_upr >= 2:
                        W(e, "sup", g_upr - 1)
                    src = d_swupc[dffc] if e_ == 0 else d_upwc[e_ - 1, dffc]
                    e.dma_start(out=ab(o_upb + 512 * (dffc % 2), 1024)
                                .rearrange("p (c j) -> p c j", c=DC),
                                in_=src.rearrange("(c p) j -> p c j", p=128)
                                ).then_inc(sems[slotu], 16)
                S.on("gpsimd", f_ups)

                g_ha = S.at("sha") - 1 if S.at("sha") >= 1 else 0
                nup = S.inc("sup")

                def f_upm(e, hb=hb, e_=e_, dffc=dffc, nups=nups, g_ha=g_ha,
                          g_cmb_prev=g_cmb_prev, slotu=slotu):
                    bwait("tensor", e)
                    W(e, slotu, nups)
                    W(e, "sha", g_ha)
                    if dffc <= 1:
                        W(e, "scmb", g_cmb_prev)
                    wb = ab(o_upb + 512 * (dffc % 2), 1024)
                    pbase = 4 if dffc % 2 == 0 else 2
                    for tt in range(2):
                        for j in range(DC):
                            mm = e.matmul(p_mm[pbase + tt][:], wb[:, 128 * j:128 * (j + 1)],
                                          ab(o_xffm + 1024 * j, 2048)[:, 1024 * hb + 512 * tt:
                                                                      1024 * hb + 512 * (tt + 1)],
                                          start=(j == 0), stop=(j == DC - 1))
                            if tt == 1 and j == DC - 1:
                                mm.then_inc(sems["sup"], 1)
                S.on("tensor", f_upm)

                nha = S.inc("sha")

                def f_sup(e, e_=e_, dffc=dffc, nup=nup, g_dn_prev=g_dn_prev):
                    bwait("scalar", e)
                    W(e, "sup", nup)
                    W(e, "sdn", g_dn_prev)
                    bc = t_sbupc[:, dffc:dffc + 1] if e_ == 0 else \
                        t_upbc[:, (e_ - 1) * NF + dffc:(e_ - 1) * NF + dffc + 1]
                    pbase = 4 if dffc % 2 == 0 else 2
                    for tt in range(2):
                        act = e.activation(out=ab(o_hact + 512 * dffc, 1024)[:, 512 * tt:512 * (tt + 1)],
                                           in_=p_mm[pbase + tt][:], func=AF.Gelu, bias=bc)
                        if tt == 1:
                            act.then_inc(sems["sha"], 1)
                S.on("scalar", f_sup)

            nha_all = S.at("sha")

            for dh in range(2):
                for dffc in range(NF):
                    slotd = "sdnsA" if dffc % 2 == 0 else "sdnsB"
                    ndns = S.at(slotd) + 16
                    S.inc(slotd, 16)
                    g_dnr = S.at("sdn")

                    def f_dns(e, e_=e_, dffc=dffc, dh=dh, g_dnr=g_dnr, slotd=slotd):
                        if g_dnr >= 2:
                            W(e, "sdn", g_dnr - 1)
                        src = d_swdn[128 * dffc:128 * (dffc + 1), 512 * dh:512 * (dh + 1)] \
                            if e_ == 0 else \
                            d_dnw[e_ - 1, 128 * dffc:128 * (dffc + 1), 512 * dh:512 * (dh + 1)]
                        e.dma_start(out=ab(o_dnb + 512 * (dffc % 2), 1024)[:, 0:512],
                                    in_=src).then_inc(sems[slotd], 16)
                    S.on("gpsimd", f_dns)

                    ndn = S.inc("sdn")

                    def f_dnm(e, e_=e_, hb=hb, dffc=dffc, dh=dh, ndns=ndns,
                              nha_all=nha_all, scmb_g=scmb_guard, slotd=slotd):
                        W(e, slotd, ndns)
                        if dffc == 0:
                            W(e, "sha", nha_all)
                            W(e, "scmb", scmb_g)
                            for tb in range(8):
                                if e_ == 0:
                                    e.matmul(p_mm[tb][:], t_onesb[:],
                                             ab(o_vrowb, 2 * D)[0:1, D + 512 * dh:D + 512 * (dh + 1)],
                                             start=True, stop=False)
                                else:
                                    e.matmul(p_mm[tb][:],
                                             ab(o_wbf, 2048)[0:8, 1024 * hb + 128 * tb:
                                                             1024 * hb + 128 * (tb + 1)],
                                             t_downb[:, 512 * dh:512 * (dh + 1)],
                                             start=True, stop=False)
                        wb = ab(o_dnb + 512 * (dffc % 2), 1024)[:, 0:512]
                        last = (dffc == NF - 1)
                        for tb in range(8):
                            mm = e.matmul(p_mm[tb][:],
                                          ab(o_hact + 512 * dffc, 1024)[:, 128 * tb:128 * (tb + 1)],
                                          wb, start=False, stop=last)
                            if tb == 7:
                                mm.then_inc(sems["sdn"], 1)
                    S.on("tensor", f_dnm)

                ndn_all = S.at("sdn")
                ncmb = S.inc("scmb")

                def f_vcmb(e, e_=e_, hb=hb, dh=dh, ndn_all=ndn_all):
                    bwait("vector", e)
                    W(e, "sdn", ndn_all)
                    for tb in range(8):
                        dst = af(o_comb + 1024 * tb + 512 * dh, 512)
                        if e_ == 0:
                            e.tensor_scalar_mul(out=dst, in0=p_mm[tb][:], scalar1=0.1)
                        else:
                            wcol = t_wtok[:, (8 * hb + tb) * 8 + (e_ - 1):
                                          (8 * hb + tb) * 8 + e_]
                            e.scalar_tensor_tensor(out=dst, in0=p_mm[tb][:], scalar=wcol,
                                                   in1=dst, op0=ALU.mult, op1=ALU.add)
                    e.tensor_copy(out=t_col[:, 7:8], in_=t_col[:, 7:8]).then_inc(sems["scmb"], 1)
                S.on("vector", f_vcmb)
                scmb_guard = ncmb

        for tb in range(8):
            r_ = 2 * hb + tb // 4
            i_ = tb % 4
            slx = "sxarA" if tb % 2 == 0 else "sxarB"
            nxar = S.at(slx) + 16
            S.inc(slx, 16)
            g_fin = S.at("sfin") - 1 if S.at("sfin") >= 1 else 0

            def f_xar(e, r_=r_, i_=i_, tb=tb, g_fin=g_fin, slx=slx):
                bwait("sync", e)
                if tb >= 2:
                    W(e, "sfin", g_fin)
                e.dma_start(out=af(o_xar + 1024 * (tb % 2), 1024),
                            in_=d_xa[r_, 128 * i_:128 * (i_ + 1), :]).then_inc(sems[slx], 16)
            S.on("sync", f_xar)

            ncmb_all = S.at("scmb")
            nfin = S.inc("sfin")

            g_out_prev = S.at("sout")

            def f_vfin(e, tb=tb, nxar=nxar, ncmb_all=ncmb_all, slx=slx,
                       g_out_prev=g_out_prev):
                W(e, "scmb", ncmb_all)
                W(e, slx, nxar)
                W(e, "sout", g_out_prev)
                cmb = af(o_comb + 1024 * tb, 1024)
                e.reduce_max(out=t_col[:, 0:1], in_=cmb, axis=AX, apply_absolute_value=True)
                e.tensor_scalar_add(out=t_col[:, 0:1], in0=t_col[:, 0:1], scalar1=1.0e-6)
                e.reciprocal(out=t_col[:, 1:2], in_=t_col[:, 0:1])
                e.tensor_scalar_mul(out=af(o_fin, 1024), in0=cmb, scalar1=t_col[:, 1:2])
                e.tensor_add(out=af(o_fin, 1024), in0=af(o_fin, 1024),
                             in1=af(o_xar + 1024 * (tb % 2), 1024)).then_inc(sems["sfin"], 1)
            S.on("vector", f_vfin)

            S.inc("sout", 16)

            def f_out(e, r_=r_, i_=i_, nfin=nfin):
                W(e, "sfin", nfin)
                e.dma_start(out=d_out[r_, 128 * i_:128 * (i_ + 1), :],
                            in_=af(o_fin, 1024)).then_inc(sems["sout"], 16)
            S.on("sync", f_out)

    SNAP_FULL = S.snapshot()
    import os as _os
    if _os.environ.get("PRINT_SNAPS"):
        print("SNAP_XN", SNAP_XN[0], "ROUNDS", [r[0] for r in ROUND_SNAPS],
              "SNAP_ROUNDS", SNAP_ROUNDS[0], "FULL", SNAP_FULL[0])
    if stage is not None:
        snaps = {"xn": SNAP_XN, "r1": ROUND_SNAPS[1] if len(ROUND_SNAPS) > 1 else SNAP_ROUNDS,
                 "r2": ROUND_SNAPS[2] if len(ROUND_SNAPS) > 2 else SNAP_ROUNDS,
                 "r3": ROUND_SNAPS[3] if len(ROUND_SNAPS) > 3 else SNAP_ROUNDS,
                 "rounds": SNAP_ROUNDS, "full": SNAP_FULL}
        if isinstance(stage, str) and stage.startswith("cut:"):
            gcut = int(stage.split(":")[1])
            cnts = S.cnt_log[gcut - 1] if 0 < gcut <= len(S.cnt_log) else (
                dict(S.cnt) if gcut > 0 else {})
        else:
            gcut, cnts = snaps[stage]
        for k in S.q:
            S.q[k] = [(g, f) for (g, f) in S.q[k] if g < gcut]

        def f_endS(e, cnts=cnts):
            for nm, v in cnts.items():
                W(e, nm, v)
        S.on("gpsimd", f_endS)
    else:
        NOUT = S.at("sout")

        def f_end(e):
            W(e, "sout", NOUT)
        S.on("gpsimd", f_end)

    class DrainProxy:
        _PASS = {"wait_ge", "sem_inc", "drain", "nop"}

        def __init__(self, eng):
            self._e = eng

        def __getattr__(self, name):
            fn = getattr(self._e, name)
            if name in DrainProxy._PASS or not callable(fn):
                return fn

            def wrapped(*a, **k):
                self._e.drain()
                return fn(*a, **k)
            return wrapped

    @blk.sync
    def _(e):
        for _, fn in S.q["sync"]:
            fn(e)

    @blk.vector
    def _(e):
        p = DrainProxy(e)
        for _, fn in S.q["vector"]:
            fn(p)

    @blk.scalar
    def _(e):
        p = DrainProxy(e)
        for _, fn in S.q["scalar"]:
            fn(p)

    @blk.tensor
    def _(e):
        for _, fn in S.q["tensor"]:
            fn(e)

    @blk.gpsimd
    def _(e):
        p = DrainProxy(e)
        for _, fn in S.q["gpsimd"]:
            fn(p)

    ctx.close()
    return nc


_NC_CACHE = [None]


def kernel(L0_attn_x, L0_attn_norm1_w, L0_attn_norm2_w, L0_attn_A_log, L0_attn_D_param,
           L0_attn_scale, L0_attn_Wdt, L0_attn_bdt, L0_attn_WB, L0_attn_bB, L0_attn_WC,
           L0_attn_bC, L0_attn_W1, L0_attn_b1, L0_attn_W2, L0_attn_b2, L0_ffn_gate_w,
           L0_ffn_bias, L0_ffn_up_w, L0_ffn_up_b, L0_ffn_down_w, L0_ffn_down_b,
           L0_ffn_sw_up, L0_ffn_sb_up, L0_ffn_sw_down, L0_ffn_sb_down):
    x = np.asarray(L0_attn_x, dtype=np.float32)
    WdtT = np.ascontiguousarray(np.asarray(L0_attn_Wdt, np.float32).T)
    wbcT = np.ascontiguousarray(
        np.concatenate([np.asarray(L0_attn_WB, np.float32).T,
                        np.asarray(L0_attn_WC, np.float32).T], axis=1))
    w1c = np.ascontiguousarray(
        np.asarray(L0_attn_W1, np.float32).T.reshape(D, NF, 128).transpose(1, 0, 2))
    w2T = np.ascontiguousarray(np.asarray(L0_attn_W2, np.float32).T)
    swupc = np.ascontiguousarray(
        np.asarray(L0_ffn_sw_up, np.float32).reshape(D, NF, 128).transpose(1, 0, 2))
    swdn = np.ascontiguousarray(np.asarray(L0_ffn_sw_down, np.float32))
    upwc = np.ascontiguousarray(
        np.asarray(L0_ffn_up_w, np.float32).reshape(E, D, NF, 128).transpose(0, 2, 1, 3))
    dnw = np.ascontiguousarray(np.asarray(L0_ffn_down_w, np.float32))
    gatew = np.ascontiguousarray(np.asarray(L0_ffn_gate_w, np.float32))
    vecs = np.concatenate([np.asarray(L0_attn_norm1_w, np.float32),
                           np.asarray(L0_attn_norm2_w, np.float32),
                           np.asarray(L0_attn_b2, np.float32),
                           np.asarray(L0_ffn_sb_down, np.float32)])[None, :]
    b1cols = np.ascontiguousarray(np.asarray(L0_attn_b1, np.float32).reshape(NF, 128).T)
    sbupcols = np.ascontiguousarray(np.asarray(L0_ffn_sb_up, np.float32).reshape(NF, 128).T)
    upbcols = np.ascontiguousarray(
        np.asarray(L0_ffn_up_b, np.float32).reshape(E, NF, 128).transpose(2, 0, 1)
        .reshape(128, E * NF))
    downb = np.ascontiguousarray(np.asarray(L0_ffn_down_b, np.float32))
    identb = np.eye(128).astype(ml_dtypes.bfloat16)
    identf = np.eye(128, dtype=np.float32)

    if _NC_CACHE[0] is None:
        _NC_CACHE[0] = build_nc()
    nc = _NC_CACHE[0]

    in_maps = []
    for c in range(NC):
        s_, q_ = c // 4, c % 4
        x_my = np.stack([x[s_, 2048 * r + TS * q_:2048 * r + TS * (q_ + 1), :]
                         for r in range(R)])
        small = np.zeros((128, 32), np.float32)
        small[0, 9:25] = np.asarray(L0_attn_A_log, np.float32)
        small[0:32, 0] = np.arange(32, dtype=np.float32)
        small[0:8, 25] = np.asarray(L0_ffn_bias, np.float32)
        small[:, 1] = np.asarray(L0_attn_bdt, np.float32)[128 * c:128 * (c + 1)]
        small[:, 2] = np.asarray(L0_attn_D_param, np.float32)[128 * c:128 * (c + 1)]
        small[:, 3] = np.asarray(L0_attn_scale, np.float32)[128 * c:128 * (c + 1)]
        small[0:16, 4] = np.asarray(L0_attn_bB, np.float32)
        small[16:32, 4] = np.asarray(L0_attn_bC, np.float32)
        small[0:32, 5] = np.arange(32, dtype=np.float32) - 99.0
        small[:, 6] = 1.0e-5
        small[:, 7] = 1.0e-11
        small[:, 8] = 1.0e-6
        sel = np.zeros((D, 128), np.float32)
        sel[128 * c:128 * (c + 1), :] = np.eye(128, dtype=np.float32)
        selT = np.zeros((128, D), np.float32)
        selT[:, 128 * c:128 * (c + 1)] = np.eye(128, dtype=np.float32)
        in_maps.append({
            "x_my": np.ascontiguousarray(x_my),
            "wdtTk": np.ascontiguousarray(WdtT[:, 128 * c:128 * (c + 1)]),
            "wbcT": wbcT,
            "sel": sel.astype(ml_dtypes.bfloat16),
            "selT": selT.astype(ml_dtypes.bfloat16),
            "identb": identb, "identf": identf,
            "small": small, "vecs": vecs,
            "b1c": b1cols, "sbupc": sbupcols, "upbc": upbcols, "downb": downb,
            "gatew": gatew, "w1c": w1c, "w2T": w2T, "swupc": swupc, "swdn": swdn,
            "upwc": upwc, "dnw": dnw,
        })

    res = run_bass_kernel_spmd(nc, in_maps, core_ids=list(range(NC)))
    out = np.zeros((2, 8192, D), np.float32)
    for c in range(NC):
        s_, q_ = c // 4, c % 4
        om = res.results[c]["out_my"].reshape(R, TS, D)
        for r in range(R):
            out[s_, 2048 * r + TS * q_:2048 * r + TS * (q_ + 1), :] = om[r]
    return out


# version 06
# speedup vs baseline: 1.0720x; 1.0009x over previous
import numpy as np
import ml_dtypes
import concourse.bass as bass
import concourse.mybir as mybir
from concourse.bass_utils import run_bass_kernel_spmd

F32 = mybir.dt.float32
BF16 = mybir.dt.bfloat16
AF = mybir.ActivationFunctionType
ALU = mybir.AluOpType
AX = mybir.AxisListType.X

NC = 8
D, N, DFF, E = 1024, 16, 4096, 8
R, TR, TS, DC, NF, Q = 4, 4096, 512, 8, 32, 1024


class Sched:
    def __init__(self):
        self.q = {k: [] for k in ("sync", "vector", "scalar", "tensor", "gpsimd")}
        self.cnt = {}
        self.gseq = 0
        self.cnt_log = []

    def inc(self, sem, n=1):
        self.cnt[sem] = self.cnt.get(sem, 0) + n
        return self.cnt[sem]

    def at(self, sem):
        return self.cnt.get(sem, 0)

    def on(self, eng, fn):
        self.q[eng].append((self.gseq, fn))
        self.gseq += 1
        self.cnt_log.append(dict(self.cnt))

    def snapshot(self):
        return (self.gseq, dict(self.cnt))


def build_nc(stage=None):
    nc = bass.Bass("TRN2", target_bir_lowering=False)

    d_x = nc.dram_tensor("x_my", [R, TS, D], F32, kind="ExternalInput").ap()
    d_wdtTk = nc.dram_tensor("wdtTk", [D, 128], F32, kind="ExternalInput").ap()
    d_wbcT = nc.dram_tensor("wbcT", [D, 2 * N], F32, kind="ExternalInput").ap()
    d_sel = nc.dram_tensor("sel", [D, 128], BF16, kind="ExternalInput").ap()
    d_selT = nc.dram_tensor("selT", [128, D], BF16, kind="ExternalInput").ap()
    d_identb = nc.dram_tensor("identb", [128, 128], BF16, kind="ExternalInput").ap()
    d_identf = nc.dram_tensor("identf", [128, 128], F32, kind="ExternalInput").ap()
    d_small = nc.dram_tensor("small", [128, 32], F32, kind="ExternalInput").ap()
    d_vecs = nc.dram_tensor("vecs", [1, 4 * D], F32, kind="ExternalInput").ap()
    d_b1c = nc.dram_tensor("b1c", [128, NF], F32, kind="ExternalInput").ap()
    d_sbupc = nc.dram_tensor("sbupc", [128, NF], F32, kind="ExternalInput").ap()
    d_upbc = nc.dram_tensor("upbc", [128, E * NF], F32, kind="ExternalInput").ap()
    d_downb = nc.dram_tensor("downb", [E, D], F32, kind="ExternalInput").ap()
    d_gatew = nc.dram_tensor("gatew", [D, E], F32, kind="ExternalInput").ap()
    d_w1c = nc.dram_tensor("w1c", [NF, D, 128], F32, kind="ExternalInput").ap()
    d_w2T = nc.dram_tensor("w2T", [DFF, D], F32, kind="ExternalInput").ap()
    d_swupc = nc.dram_tensor("swupc", [NF, D, 128], F32, kind="ExternalInput").ap()
    d_swdn = nc.dram_tensor("swdn", [DFF, D], F32, kind="ExternalInput").ap()
    d_upwc = nc.dram_tensor("upwc", [E, NF, D, 128], F32, kind="ExternalInput").ap()
    d_dnw = nc.dram_tensor("dnw", [E, DFF, D], F32, kind="ExternalInput").ap()
    d_out = nc.dram_tensor("out_my", [R, TS, D], F32, kind="ExternalOutput").ap()

    d_xa = nc.dram_tensor("xa_stash", [R, TS, D], F32, kind="Internal").ap()
    d_bvcd = nc.dram_tensor("bvcd", [2 * N, Q], F32, kind="Internal").ap()
    d_x1s = nc.dram_tensor("x1_stash", [R, TS, D], F32, kind="Internal").ap()
    cin1 = [nc.dram_tensor(f"cin1_{r}", [D, TS], BF16, kind="Internal").ap() for r in range(R)]
    cout1 = [nc.dram_tensor(f"cout1_{r}", [NC, D, TS], BF16, kind="Internal").ap() for r in range(R)]
    cin2 = [nc.dram_tensor(f"cin2_{r}", [TR, D], BF16, kind="Internal").ap() for r in range(R)]
    cout2 = [nc.dram_tensor(f"cout2_{r}", [TS, D], BF16, kind="Internal").ap() for r in range(R)]
    RG = [list(range(NC))]

    S = Sched()
    from contextlib import ExitStack
    ctx = ExitStack()

    def sbt(name, shape, dt):
        return ctx.enter_context(nc.sbuf_tensor(name, shape, dt))

    t_identb = sbt("t_identb", [128, 128], BF16)
    t_identf = sbt("t_identf", [128, 128], F32)
    t_sel = sbt("t_sel", [128, DC, 128], BF16)
    t_selT = sbt("t_selT", [128, D], BF16)
    t_small = sbt("t_small", [128, 32], F32)
    t_n1rep = sbt("t_n1rep", [128, D], BF16)
    t_n2rep = sbt("t_n2rep", [128, D], BF16)
    t_downb = sbt("t_downb", [E, D], BF16)
    t_onesb = sbt("t_onesb", [1, 128], BF16)
    t_onesf = sbt("t_onesf", [1, 128], F32)
    t_Arow = sbt("t_Arow", [1, N], F32)
    t_Acol = sbt("t_Acol", [N, 1], F32)
    t_invA = sbt("t_invA", [2 * N, 1], F32)
    t_Arep = sbt("t_Arep", [128, N], F32)
    t_b1c = sbt("t_b1c", [128, NF], F32)
    t_sbupc = sbt("t_sbupc", [128, NF], F32)
    t_upbc = sbt("t_upbc", [128, E * NF], F32)
    t_wdtTk = sbt("t_wdtTk", [128, DC, 128], BF16)
    t_wbcT = sbt("t_wbcT", [128, DC, 2 * N], BF16)
    t_gatew = sbt("t_gatew", [128, DC, E], BF16)
    t_stats = sbt("t_stats", [128, 8 * R], F32)
    t_state = sbt("t_state", [128, 2 * N], F32)
    t_wtok = sbt("t_wtok", [128, 16 * E], F32)
    t_col = sbt("t_col", [128, 8], F32)

    AW = 48800
    t_arena = ctx.enter_context(nc.sbuf_tensor("arena", [128, AW], F32))
    arena_bf = t_arena.bitcast(BF16)

    def af(off, w):
        assert off + w <= AW, (off, w)
        return t_arena[:, off:off + w]

    def ab(off, wbf):
        assert 2 * off + wbf <= 2 * AW, (off, wbf)
        return arena_bf[:, 2 * off:2 * off + wbf]

    o_xffm = 0
    o_wfm = DC * 1024
    o_wbf = o_wfm + 2048
    o_vrow = o_wbf + 1024
    o_vrowb = o_vrow + 4096
    P0_END = o_vrowb + 1024
    o_rhs = P0_END
    o_dt = o_rhs + 4096
    o_rdt = o_dt + Q
    o_g = o_rdt + Q
    o_xnk = o_g + Q
    o_dA = o_xnk + 512
    o_bsc = o_dA + Q
    o_G = o_bsc + Q
    o_h = o_G + Q
    o_tq = o_h + Q
    o_yacc = o_tq + Q
    o_bvc = o_yacc + Q
    o_ypl = o_bvc + 1024
    P2_END = o_ypl + 512
    p13 = P2_END
    o_xs = p13
    o_wk1 = p13 + 4096
    o_wk2 = p13 + 5120
    o_xntok = p13 + 5128
    o_xnfm = p13 + 5640
    o_ysb = p13
    o_x1 = p13 + 512
    o_xa1 = p13 + 2560
    o_xn2 = p13 + 3584
    o_xn2fm = p13 + 4096
    o_h1 = p13 + 6144
    o_wbuf = p13 + 14336
    o_w2buf = p13 + 15360
    o_xf = o_xn2
    o_x1e = o_h1
    o_wkA = o_h1 + 2048
    o_sc = o_h1 + 3072
    o_xs3 = o_h1 + 6144
    P3_END = p13 + 16384
    assert P3_END <= AW, P3_END
    p4 = P0_END
    o_hact = p4
    o_upb = p4 + 16384
    o_dnb = p4 + 17408
    o_comb = p4 + 18432
    o_fin = p4 + 26624
    o_xar = p4 + 27648
    P4_END = p4 + 29696
    assert P4_END <= AW, P4_END

    p_mm = [ctx.enter_context(nc.psum_tensor(f"p_mm{i}", [128, 512], F32)) for i in range(8)]
    p_trb = p_mm[6].bitcast(BF16)
    p_trf = p_mm[7]
    p_plb = p_mm[7].bitcast(BF16)

    semnames = ["ld0", "cast0", "setup", "sx", "svar", "srsq", "sxn", "strA", "strAe",
                "scin", "sxg1", "srhs", "swdt", "sdt", "sbvp", "sbv", "sselp", "sxnk",
                "srdt", "sg", "sorB", "sorC", "sGn", "sdA", "sbn", "sh", "stq", "syq",
                "sye", "spl", "sple", "sciny", "sxg2",
                "syld", "sx2", "sx1", "srsq2", "sxn2", "strC", "sxn2e", "sx1w",
                "sw1s", "sw1", "sh1", "sw2s", "sw2", "sy2", "sx1e", "sxaw",
                "srsq3", "sxf", "strD", "sxfe", "sgtp", "sgt", "swv", "swTp", "swT",
                "sups", "sup", "sha", "sdns", "sdn", "scmb", "sxar", "sfin", "sout", "sbvd", "sbcB", "sbcC", "sw1sA", "sw1sB", "sw2sA", "sw2sB", "supsA", "supsB", "sdnsA", "sdnsB", "scin0", "scin1", "scin2", "scin3", "sx2A", "sx2B", "sx1wA", "sx1wB", "sx3A", "sx3B", "sx1eA", "sx1eB", "sxaL", "sxarA", "sxarB"]
    sems = {}
    for nm in semnames:
        sems[nm] = ctx.enter_context(nc.semaphore(nm))

    blk = ctx.enter_context(nc.Block())

    def W(e, sem, val):
        if val > 0:
            e.wait_ge(sems[sem], val)

    N_LD0 = S.inc("ld0", 16 * 7)

    def su_sync(e):
        for src, dst in [(d_identb[:], t_identb[:]), (d_identf[:], t_identf[:]),
                         (d_small[:], t_small[:]),
                         (d_vecs[:], t_arena[0:1, o_vrow:o_vrow + 4 * D]),
                         (d_b1c[:], t_b1c[:]), (d_sbupc[:], t_sbupc[:]),
                         (d_upbc[:], t_upbc[:])]:
            e.dma_start(out=dst, in_=src).then_inc(sems["ld0"], 16)
    S.on("sync", su_sync)

    N_CAST0 = S.inc("cast0", 16 * 6)

    def su_gp(e):
        for src, dst in [
            (d_sel.rearrange("(c p) j -> p c j", p=128), t_sel[:, :, :]),
            (d_selT[:], t_selT[:]),
            (d_wdtTk.rearrange("(c p) j -> p c j", p=128), t_wdtTk[:, :, :]),
            (d_wbcT.rearrange("(c p) j -> p c j", p=128), t_wbcT[:, :, :]),
            (d_gatew.rearrange("(c p) j -> p c j", p=128), t_gatew[:, :, :]),
            (d_downb[:], t_downb[:]),
        ]:
            e.dma_start(out=dst, in_=src).then_inc(sems["cast0"], 16)
    S.on("gpsimd", su_gp)

    n1 = S.inc("setup")

    def su_s(e):
        W(e, "ld0", N_LD0)
        e.activation(out=t_Arow[:], in_=t_small[0:1, 9:9 + N], func=AF.Exp).then_inc(sems["setup"], 1)
    S.on("scalar", su_s)

    n2 = S.inc("setup")

    def su_v(e, n1=n1):
        W(e, "setup", n1)
        e.tensor_scalar_mul(out=t_Arow[:], in0=t_Arow[:], scalar1=-1.0)
        e.memset(t_onesb[:], 1.0)
        e.memset(t_onesf[:], 1.0)
        e.memset(t_state[:], 0.0)
        e.tensor_copy(out=ab(o_vrowb, 2 * D)[0:1, :],
                      in_=af(o_vrow + 2 * D, 2 * D)[0:1, :]).then_inc(sems["setup"], 1)
    S.on("vector", su_v)

    tb0 = S.at("setup")
    S.inc("setup", 6)

    def su_t(e, n2=n2):
        W(e, "setup", n2)
        W(e, "cast0", N_CAST0)
        e.matmul(p_trf[:, 0:N], t_onesf[:], t_Arow[:], start=True, stop=True).then_inc(sems["setup"], 1)
        e.transpose(p_trf[0:N, 128:129], t_Arow[:], t_identf[0:1, 0:1]).then_inc(sems["setup"], 1)
        for j in range(2):
            e.matmul(p_mm[2 * j][:], t_onesf[:],
                     af(o_vrow, 4 * D)[0:1, 512 * j:512 * (j + 1)],
                     start=True, stop=True).then_inc(sems["setup"], 1)
            e.matmul(p_mm[2 * j + 1][:], t_onesf[:],
                     af(o_vrow, 4 * D)[0:1, D + 512 * j:D + 512 * (j + 1)],
                     start=True, stop=True).then_inc(sems["setup"], 1)
    S.on("tensor", su_t)

    SETUP_DONE = S.inc("setup")

    def su_v2(e, tb0=tb0):
        W(e, "setup", tb0 + 2)
        e.tensor_copy(out=t_Arep[:], in_=p_trf[:, 0:N])
        W(e, "setup", tb0 + 2)
        e.tensor_copy(out=t_Acol[:], in_=p_trf[0:N, 128:129])
        e.memset(t_invA[:], 1.0)
        e.reciprocal(out=t_invA[0:N, :], in_=t_Acol[:])
        for j in range(2):
            W(e, "setup", tb0 + 3 + 2 * j)
            e.tensor_copy(out=t_n1rep[:, 512 * j:512 * (j + 1)], in_=p_mm[2 * j][:])
            W(e, "setup", tb0 + 4 + 2 * j)
            e.tensor_copy(out=t_n2rep[:, 512 * j:512 * (j + 1)], in_=p_mm[2 * j + 1][:])
        e.tensor_copy(out=t_col[:, 7:8], in_=t_n1rep[:, 0:1]).then_inc(sems["setup"], 1)
    S.on("vector", su_v2)

    for r in range(R):
        nxn_prev = 4 * r
        nsx_base = S.at("sx")
        S.inc("sx", 64)

        def f_xld(e, r=r, nxn_prev=nxn_prev):
            W(e, "sxn", nxn_prev)
            for i in range(4):
                e.dma_start(out=af(o_xs + 1024 * i, 1024),
                            in_=d_x[r, 128 * i:128 * (i + 1), :]).then_inc(sems["sx"], 16)
        S.on("sync", f_xld)

        for i in range(4):
            sxval = nsx_base + 64
            nvar = S.inc("svar")

            def f_v(e, r=r, i=i, sxval=sxval):
                if r == 0 and i == 0:
                    W(e, "setup", SETUP_DONE)
                W(e, "sx", sxval)
                xs = af(o_xs + 1024 * i, 1024)
                st = t_stats[:, 2 * (4 * r + i):2 * (4 * r + i) + 1]
                e.tensor_mul(out=af(o_wk1, 1024), in0=xs, in1=xs)
                e.reduce_sum(out=af(o_wk2, 1), in_=xs, axis=AX)
                e.reduce_sum(out=af(o_wk2 + 1, 1), in_=af(o_wk1, 1024), axis=AX)
                e.tensor_scalar_mul(out=st, in0=af(o_wk2, 1), scalar1=1.0 / D)
                e.tensor_mul(out=af(o_wk2 + 2, 1), in0=st, in1=st)
                e.scalar_tensor_tensor(out=af(o_wk2 + 3, 1), in0=af(o_wk2 + 1, 1),
                                       scalar=1.0 / D, in1=af(o_wk2 + 2, 1),
                                       op0=ALU.mult, op1=ALU.subtract).then_inc(sems["svar"], 1)
            S.on("vector", f_v)

            nrsq = S.inc("srsq")

            def f_s(e, r=r, i=i, nvar=nvar):
                W(e, "svar", nvar)
                var = af(o_wk2 + 3, 1)
                rst = t_stats[:, 2 * (4 * r + i) + 1:2 * (4 * r + i) + 2]
                e.activation(out=t_col[:, 0:1], in_=var, func=AF.Ln, bias=t_small[:, 6:7])
                e.activation(out=rst, in_=t_col[:, 0:1], func=AF.Exp, scale=-0.5)
                e.activation(out=t_col[:, 1:2], in_=var, func=AF.Ln, scale=1.000001,
                             bias=t_small[:, 7:8])
                e.activation(out=t_col[:, 2:3], in_=t_col[:, 1:2], func=AF.Exp,
                             scale=-0.5).then_inc(sems["srsq"], 1)
            S.on("scalar", f_s)

            nxn = S.inc("sxn")

            def f_v2(e, r=r, i=i, nrsq=nrsq):
                W(e, "srsq", nrsq)
                xs = af(o_xs + 1024 * i, 1024)
                st = t_stats[:, 2 * (4 * r + i):2 * (4 * r + i) + 1]
                e.tensor_scalar(out=af(o_wk1, 1024), in0=xs, scalar1=st,
                                scalar2=t_col[:, 2:3], op0=ALU.subtract, op1=ALU.mult)
                e.tensor_mul(out=ab(o_xntok, 1024), in0=af(o_wk1, 1024),
                             in1=t_n1rep[:]).then_inc(sems["sxn"], 1)
            S.on("vector", f_v2)

            for j in range(DC):
                ktr = S.inc("strA")

                def f_t(e, j=j, nxn=nxn, ktr=ktr):
                    if j == 0:
                        W(e, "sxn", nxn)
                    if ktr > 2:
                        W(e, "strAe", ktr - 2)
                    bk = p_trb if (ktr - 1) % 2 == 0 else p_plb
                    e.transpose(bk[:, 0:128],
                                ab(o_xntok, 1024)[:, 128 * j:128 * (j + 1)],
                                t_identb[:]).then_inc(sems["strA"], 1)
                S.on("tensor", f_t)

                S.inc("strAe")

                def f_ve(e, r=r, i=i, j=j, ktr=ktr):
                    W(e, "strA", ktr)
                    if i == 0 and j == 0 and r >= 1:
                        W(e, f"scin{r - 1}", 128)
                    bk = p_trb if (ktr - 1) % 2 == 0 else p_plb
                    e.tensor_copy(out=ab(o_xnfm + 256 * j, 512)[:, 128 * i:128 * (i + 1)],
                                  in_=bk[:, 0:128]).then_inc(sems["strAe"], 1)
                S.on("vector", f_ve)

        nevac = S.at("strAe")
        cin_sem = f"scin{r}"
        S.inc(cin_sem, 16 * 8)

        def f_cin(e, r=r, nevac=nevac, cin_sem=cin_sem):
            W(e, "strAe", nevac)
            for j in range(DC):
                e.dma_start(out=cin1[r][128 * j:128 * (j + 1), :],
                            in_=ab(o_xnfm + 256 * j, 512)).then_inc(sems[cin_sem], 16)
        S.on("sync", f_cin)

        S.inc("sxg1")

        def f_xg1(e, r=r, cin_sem=cin_sem):
            W(e, cin_sem, 128)
            e.collective_compute("AllGather", ALU.bypass, replica_groups=RG,
                                 ins=[cin1[r][:]], outs=[cout1[r][:]]).then_inc(sems["sxg1"], 1)
        S.on("gpsimd", f_xg1)

    SNAP_XN = S.snapshot()
    ROUND_SNAPS = []
    sdt_guard = [0, 0]
    SY2_GUARD = 0
    for r in range(R):
        ROUND_SNAPS.append(S.snapshot())
        xg_sh1 = S.at("sh1")
        xg_sy2 = S.at("sy2")
        xg_sgt = S.at("sgt")
        xg_swT = S.at("swT")
        xg_setup = SETUP_DONE if r == 0 else 0
        for qq in range(4):
            h_half = qq // 2
            rk0 = 4 * h_half + 2 * (qq % 2)
            g_wdt = S.at("swdt")
            g_bvp = S.at("sbvp")
            g_selp = S.at("sselp")
            nrhs = S.at("srhs") + 16 * DC * 2
            S.inc("srhs", 16 * DC * 2)

            def f_rhs(e, r=r, rk0=rk0, g_wdt=g_wdt, g_bvp=g_bvp, g_selp=g_selp):
                W(e, "sxg1", r + 1)
                W(e, "swdt", g_wdt)
                W(e, "sbvp", g_bvp)
                W(e, "sselp", g_selp)
                for j in range(DC):
                    for u in range(2):
                        e.dma_start(out=ab(o_rhs + 512 * j, 1024)[:, 512 * u:512 * (u + 1)],
                                    in_=cout1[r][rk0 + u, 128 * j:128 * (j + 1), :]
                                    ).then_inc(sems["srhs"], 16)
            S.on("sync", f_rhs)

            for tt in range(2):
                g_sdt = sdt_guard[tt]
                nwdt = S.inc("swdt")

                first_q = (qq == 0 and tt == 0)

                def f_wdt(e, tt=tt, nrhs=nrhs, g_sdt=g_sdt, first_q=first_q,
                          xg_sh1=xg_sh1, xg_setup=xg_setup):
                    W(e, "srhs", nrhs)
                    W(e, "sdt", g_sdt)
                    if first_q:
                        W(e, "setup", xg_setup)
                        W(e, "sh1", xg_sh1)
                    for j in range(DC - 1):
                        e.matmul(p_mm[tt][:], t_wdtTk[:, j, :],
                                 ab(o_rhs + 512 * j, 1024)[:, 512 * tt:512 * (tt + 1)],
                                 start=(j == 0), stop=False)
                    e.matmul(p_mm[tt][:], t_wdtTk[:, DC - 1, :],
                             ab(o_rhs + 512 * (DC - 1), 1024)[:, 512 * tt:512 * (tt + 1)],
                             start=False, stop=True).then_inc(sems["swdt"], 1)
                S.on("tensor", f_wdt)

                g_sdt_all = S.at("sdt")
                g_syq_all = S.at("syq")
                ndtv = S.inc("sdt")
                sdt_guard[tt] = ndtv

                def f_vsp(e, tt=tt, nwdt=nwdt, g_sdt_all=g_sdt_all, g_syq_all=g_syq_all):
                    W(e, "swdt", nwdt)
                    if tt == 0:
                        W(e, "sdt", g_sdt_all)
                        W(e, "syq", g_syq_all)
                    e.tensor_scalar(out=af(o_tq + 512 * tt, 512), in0=p_mm[tt][:],
                                    scalar1=t_small[:, 1:2], scalar2=80.0,
                                    op0=ALU.add, op1=ALU.min).then_inc(sems["sdt"], 1)
                S.on("vector", f_vsp)

            for tt in range(2):
                g_bve = S.at("sbv")
                nbvp = S.inc("sbvp")

                def f_bv(e, tt=tt, qq=qq, nrhs=nrhs, g_bve=g_bve, xg_sgt=xg_sgt,
                         xg_sy2=xg_sy2, xg_setup=xg_setup):
                    W(e, "srhs", nrhs)
                    W(e, "sbv", g_bve)
                    if qq == 0 and tt == 0:
                        W(e, "sgt", xg_sgt)
                        W(e, "sy2", xg_sy2)
                        W(e, "setup", xg_setup)
                    for j in range(DC - 1):
                        e.matmul(p_mm[2][0:32, :], t_wbcT[:, j, :],
                                 ab(o_rhs + 512 * j, 1024)[:, 512 * tt:512 * (tt + 1)],
                                 start=(j == 0), stop=False)
                    e.matmul(p_mm[2][0:32, :], t_wbcT[:, DC - 1, :],
                             ab(o_rhs + 512 * (DC - 1), 1024)[:, 512 * tt:512 * (tt + 1)],
                             start=False, stop=True).then_inc(sems["sbvp"], 1)
                S.on("tensor", f_bv)

                S.inc("sbv")

                def f_bve(e, tt=tt, nbvp=nbvp):
                    W(e, "sbvp", nbvp)
                    dst = af(o_bvc, 1024)
                    e.tensor_scalar(out=dst[0:32, 512 * tt:512 * (tt + 1)],
                                    in0=p_mm[2][0:32, :], scalar1=t_small[0:32, 4:5],
                                    scalar2=t_invA[:], op0=ALU.add,
                                    op1=ALU.mult).then_inc(sems["sbv"], 1)
                S.on("vector", f_bve)

            g_spl = S.at("spl")
            for tt in range(2):
                g_sg = S.at("sg")
                nselp = S.inc("sselp")

                g_xnk = S.at("sxnk")

                def f_selm(e, tt=tt, qq=qq, nrhs=nrhs, g_sg=g_sg, xg_sy2=xg_sy2,
                           xg_setup=xg_setup, g_xnk=g_xnk):
                    W(e, "srhs", nrhs)
                    W(e, "sg", g_sg)
                    W(e, "sxnk", g_xnk)
                    if qq == 0 and tt == 0:
                        W(e, "sy2", xg_sy2)
                        W(e, "setup", xg_setup)
                    for j in range(DC - 1):
                        e.matmul(p_mm[3][:], t_sel[:, j, :],
                                 ab(o_rhs + 512 * j, 1024)[:, 512 * tt:512 * (tt + 1)],
                                 start=(j == 0), stop=False)
                    e.matmul(p_mm[3][:], t_sel[:, DC - 1, :],
                             ab(o_rhs + 512 * (DC - 1), 1024)[:, 512 * tt:512 * (tt + 1)],
                             start=False, stop=True).then_inc(sems["sselp"], 1)
                S.on("tensor", f_selm)

                S.inc("sxnk")

                def f_sele(e, tt=tt, nselp=nselp, g_spl=g_spl):
                    W(e, "sselp", nselp)
                    W(e, "spl", g_spl)
                    e.tensor_copy(out=ab(o_xnk, 1024)[:, 512 * tt:512 * (tt + 1)],
                                  in_=p_mm[3][:]).then_inc(sems["sxnk"], 1)
                S.on("vector", f_sele)

            ndt_now = S.at("sdt")
            g_rdt_prev = S.at("srdt") - 1 if S.at("srdt") >= 1 else 0
            nsp = S.inc("sdt")

            def f_ssp(e, ndt_now=ndt_now, g_rdt_prev=g_rdt_prev):
                W(e, "sdt", ndt_now)
                W(e, "srdt", g_rdt_prev)
                e.activation(out=af(o_dt, Q), in_=af(o_tq, Q), func=AF.Exp)
                e.activation(out=af(o_dt, Q), in_=af(o_dt, Q), func=AF.Ln,
                             bias=1.0).then_inc(sems["sdt"], 1)
            S.on("scalar", f_ssp)

            nrdt1 = S.inc("srdt")

            def f_vmax(e, nsp=nsp):
                W(e, "sdt", nsp)
                e.tensor_scalar_max(out=af(o_rdt, Q), in0=af(o_dt, Q),
                                    scalar1=1.0e-20).then_inc(sems["srdt"], 1)
            S.on("vector", f_vmax)

            nrdt2 = S.inc("srdt")

            def f_srdt(e, nrdt1=nrdt1):
                W(e, "srdt", nrdt1)
                e.activation(out=af(o_rdt, Q), in_=af(o_rdt, Q), func=AF.Ln)
                e.activation(out=af(o_rdt, Q), in_=af(o_rdt, Q), func=AF.Exp,
                             scale=-1.0).then_inc(sems["srdt"], 1)
            S.on("scalar", f_srdt)

            nxnk_now = S.at("sxnk")
            ngm = S.inc("sg")
            ng = S.inc("sg")

            def f_vg(e, nrdt2=nrdt2, nxnk_now=nxnk_now):
                W(e, "srdt", nrdt2)
                W(e, "sxnk", nxnk_now)
                e.tensor_mul(out=af(o_g, Q), in0=ab(o_xnk, 1024),
                             in1=af(o_rdt, Q)).then_inc(sems["sg"], 1)
                e.memset(af(o_yacc, Q), 0.0).then_inc(sems["sg"], 1)
            S.on("vector", f_vg)

            nbv_now = S.at("sbv")
            g_bcB_prev = S.at("sbcB")
            g_bcC_prev = S.at("sbcC")
            nbvd = S.at("sbvd") + 16
            S.inc("sbvd", 16)

            def f_bvd(e, nbv_now=nbv_now, g_bcB_prev=g_bcB_prev, g_bcC_prev=g_bcC_prev):
                W(e, "sbv", nbv_now)
                W(e, "sbcB", g_bcB_prev)
                W(e, "sbcC", g_bcC_prev)
                e.dma_start(out=d_bvcd[:], in_=af(o_bvc, 1024)[0:32, :]).then_inc(sems["sbvd"], 16)
            S.on("sync", f_bvd)

            g_sye_prev = S.at("sye")
            for n_ in range(N):
                g_Gn = S.at("sGn")
                g_bn0 = S.at("sbn")
                nbcB = S.at("sbcB") + 16
                S.inc("sbcB", 16)

                def f_bcB(e, n_=n_, nbvd=nbvd, g_Gn=g_Gn, g_bn0=g_bn0):
                    W(e, "sbvd", nbvd)
                    W(e, "sGn", g_Gn)
                    W(e, "sbn", g_bn0)
                    e.dma_start(out=af(o_G, Q),
                                in_=d_bvcd[n_:n_ + 1, :].broadcast_to([128, Q])
                                ).then_inc(sems["sbcB"], 16)
                S.on("scalar", f_bcB)

                g_h = S.at("sh")
                ndA = S.inc("sdA")

                def f_sdA(e, n_=n_, nsp=nsp, g_h=g_h):
                    W(e, "sdt", nsp)
                    W(e, "sh", g_h)
                    e.activation(out=af(o_dA, Q), in_=af(o_dt, Q), func=AF.Exp,
                                 scale=t_Arep[:, n_:n_ + 1])
                    e.activation(out=af(o_bsc, Q), in_=af(o_dA, Q), func=AF.Copy,
                                 bias=-1.0).then_inc(sems["sdA"], 1)
                S.on("scalar", f_sdA)

                g_bn = S.at("sbn")
                nGn = S.inc("sGn")

                def f_vG(e, nbcB=nbcB, ngm=ngm, g_bn=g_bn):
                    W(e, "sbcB", nbcB)
                    W(e, "sg", ngm)
                    W(e, "sbn", g_bn)
                    e.tensor_mul(out=af(o_G, Q), in0=af(o_G, Q),
                                 in1=af(o_g, Q)).then_inc(sems["sGn"], 1)
                S.on("vector", f_vG)

                g_h2 = S.at("sh")
                nbn = S.inc("sbn")

                def f_gb(e, nGn=nGn, ndA=ndA, g_h2=g_h2):
                    W(e, "sGn", nGn)
                    W(e, "sdA", ndA)
                    W(e, "sh", g_h2)
                    e.tensor_mul(out=af(o_bsc, Q), in0=af(o_bsc, Q),
                                 in1=af(o_G, Q)).then_inc(sems["sbn"], 1)
                S.on("gpsimd", f_gb)

                g_tq2 = S.at("stq")
                nh = S.inc("sh")
                scol = 16 * h_half + n_

                def f_vscan(e, nbn=nbn, ndA=ndA, scol=scol, g_tq2=g_tq2):
                    W(e, "sbn", nbn)
                    W(e, "sdA", ndA)
                    W(e, "stq", g_tq2)
                    e.tensor_tensor_scan(out=af(o_h, Q), data0=af(o_dA, Q),
                                         data1=af(o_bsc, Q),
                                         initial=t_state[:, scol:scol + 1],
                                         op0=ALU.mult, op1=ALU.add)
                    e.tensor_copy(out=t_state[:, scol:scol + 1],
                                  in_=af(o_h + Q - 1, 1)).then_inc(sems["sh"], 1)
                S.on("vector", f_vscan)

                g_tq3 = S.at("stq")
                g_yq0 = S.at("syq")
                nbcC = S.at("sbcC") + 16
                S.inc("sbcC", 16)

                def f_bcC(e, n_=n_, nbvd=nbvd, g_tq3=g_tq3, g_yq0=g_yq0,
                          g_sye_prev=g_sye_prev):
                    W(e, "sbvd", nbvd)
                    W(e, "stq", g_tq3)
                    W(e, "syq", g_yq0)
                    if n_ == 0:
                        W(e, "sye", g_sye_prev)
                    e.dma_start(out=af(o_tq, Q),
                                in_=d_bvcd[N + n_:N + n_ + 1, :].broadcast_to([128, Q])
                                ).then_inc(sems["sbcC"], 16)
                S.on("scalar", f_bcC)

                g_yq = S.at("syq")
                ntq = S.inc("stq")

                def f_vt(e, nh=nh, nbcC=nbcC, g_yq=g_yq):
                    W(e, "sh", nh)
                    W(e, "sbcC", nbcC)
                    W(e, "syq", g_yq)
                    e.tensor_mul(out=af(o_tq, Q), in0=af(o_tq, Q),
                                 in1=af(o_h, Q)).then_inc(sems["stq"], 1)
                S.on("vector", f_vt)

                nyq = S.inc("syq")

                def f_gy(e, ntq=ntq, ng=ng):
                    W(e, "stq", ntq)
                    W(e, "sg", ng)
                    e.tensor_add(out=af(o_yacc, Q), in0=af(o_yacc, Q),
                                 in1=af(o_tq, Q)).then_inc(sems["syq"], 1)
                S.on("gpsimd", f_gy)

            nyq_all = S.at("syq")
            nye1 = S.inc("sye")
            nye = S.inc("sye")

            def f_vep(e, nyq_all=nyq_all):
                W(e, "syq", nyq_all)
                e.tensor_scalar_mul(out=af(o_tq, Q), in0=ab(o_xnk, 1024),
                                    scalar1=t_small[:, 2:3])
                e.tensor_add(out=af(o_yacc, Q), in0=af(o_yacc, Q), in1=af(o_tq, Q))
                e.tensor_scalar_mul(out=af(o_yacc, Q), in0=af(o_yacc, Q),
                                    scalar1=t_small[:, 3:4]).then_inc(sems["sye"], 1)
                e.tensor_copy(out=ab(o_xnk, 1024), in_=af(o_yacc, Q)).then_inc(sems["sye"], 1)
            S.on("vector", f_vep)

            for tb in range(8):
                tbh = (qq % 2) * 8 + tb
                owner = 4 * h_half + tbh // 4
                rowbase = 512 * owner + 128 * (tbh % 4)
                g_ple = S.at("sple")
                npl = S.inc("spl")

                def f_tpl(e, tb=tb, qq=qq, nye=nye, g_ple=g_ple,
                          xg_sy2=xg_sy2, xg_setup=xg_setup, xg_swT=xg_swT):
                    W(e, "sye", nye)
                    W(e, "sple", g_ple)
                    if qq == 0 and tb == 0:
                        W(e, "sy2", xg_sy2)
                        W(e, "swT", xg_swT)
                        W(e, "setup", xg_setup)
                    e.matmul(p_mm[6][:], ab(o_xnk, 1024)[:, 128 * tb:128 * (tb + 1)],
                             t_selT[:, 0:512], start=True, stop=True)
                    e.matmul(p_mm[7][:], ab(o_xnk, 1024)[:, 128 * tb:128 * (tb + 1)],
                             t_selT[:, 512:1024], start=True, stop=True).then_inc(sems["spl"], 1)
                S.on("tensor", f_tpl)

                g_cy = S.at("sciny")
                nple = S.inc("sple")

                def f_vpl(e, npl=npl, g_cy=g_cy):
                    W(e, "spl", npl)
                    W(e, "sciny", g_cy)
                    e.tensor_copy(out=ab(o_ypl, 1024)[:, 0:512], in_=p_mm[6][:])
                    e.tensor_copy(out=ab(o_ypl, 1024)[:, 512:1024],
                                  in_=p_mm[7][:]).then_inc(sems["sple"], 1)
                S.on("vector", f_vpl)

                nple2 = S.at("sple")
                S.inc("sciny", 16)
                firstpl = (qq == 0 and tb == 0)

                def f_dpl(e, r=r, rowbase=rowbase, nple2=nple2, firstpl=firstpl):
                    W(e, "sple", nple2)
                    if firstpl and r >= 1:
                        W(e, "sxg2", r)
                    e.dma_start(out=cin2[r][rowbase:rowbase + 128, :],
                                in_=ab(o_ypl, 1024)).then_inc(sems["sciny"], 16)
                S.on("sync", f_dpl)

        nciny_all = S.at("sciny")
        S.inc("sxg2")

        def f_xg2(e, r=r, nciny_all=nciny_all):
            W(e, "sciny", nciny_all)
            e.collective_compute("ReduceScatter", ALU.add, replica_groups=RG,
                                 ins=[cin2[r][:]], outs=[cout2[r][:]]).then_inc(sems["sxg2"], 1)
        S.on("gpsimd", f_xg2)

        for i in range(4):
            g_x1 = S.at("sx1")
            nyld = S.at("syld") + 16
            S.inc("syld", 16)

            def f_yld(e, r=r, i=i, g_x1=g_x1):
                W(e, "sxg2", r + 1)
                W(e, "sx1", g_x1)
                e.dma_start(out=ab(o_ysb, 1024),
                            in_=cout2[r][128 * i:128 * (i + 1), :]).then_inc(sems["syld"], 16)
            S.on("sync", f_yld)

            sl2 = "sx2A" if i % 2 == 0 else "sx2B"
            g_x1b = S.at("sx1") - 1 if S.at("sx1") >= 1 else 0
            nx2 = S.at(sl2) + 16
            S.inc(sl2, 16)

            def f_xrl(e, r=r, i=i, g_x1b=g_x1b, sl2=sl2):
                W(e, "sx1", g_x1b)
                e.dma_start(out=af(o_xs3 + 1024 * (i % 2), 1024),
                            in_=d_x[r, 128 * i:128 * (i + 1), :]).then_inc(sems[sl2], 16)
            S.on("sync", f_xrl)

            slw = "sx1wA" if i % 2 == 0 else "sx1wB"
            g_x1w = S.at(slw)
            nx1 = S.inc("sx1")

            def f_vx1(e, r=r, i=i, nyld=nyld, nx2=nx2, g_x1w=g_x1w, sl2=sl2, slw=slw):
                W(e, "syld", nyld)
                W(e, sl2, nx2)
                W(e, slw, g_x1w)
                st = t_stats[:, 2 * (4 * r + i):2 * (4 * r + i) + 2]
                x1s = af(o_x1 + 1024 * (i % 2), 1024)
                e.tensor_scalar(out=x1s, in0=af(o_xs3 + 1024 * (i % 2), 1024),
                                scalar1=st[:, 0:1], scalar2=st[:, 1:2],
                                op0=ALU.subtract, op1=ALU.mult)
                e.tensor_add(out=x1s, in0=x1s, in1=ab(o_ysb, 1024))
                e.tensor_mul(out=af(o_wk2, 1024) if False else af(o_xa1, 1024), in0=x1s, in1=x1s)
                e.reduce_sum(out=t_col[:, 3:4], in_=af(o_xa1, 1024), axis=AX).then_inc(sems["sx1"], 1)
            S.on("vector", f_vx1)

            nrsq2 = S.inc("srsq2")

            def f_srs2(e, nx1=nx1):
                W(e, "sx1", nx1)
                e.activation(out=t_col[:, 4:5], in_=t_col[:, 3:4], func=AF.Ln,
                             scale=1.0 / D, bias=t_small[:, 8:9])
                e.activation(out=t_col[:, 4:5], in_=t_col[:, 4:5], func=AF.Exp,
                             scale=-0.5).then_inc(sems["srsq2"], 1)
            S.on("scalar", f_srs2)

            nxn2 = S.inc("sxn2")

            def f_vxn2(e, i=i, nrsq2=nrsq2):
                W(e, "srsq2", nrsq2)
                x1s = af(o_x1 + 1024 * (i % 2), 1024)
                e.tensor_scalar_mul(out=af(o_xa1, 1024), in0=x1s, scalar1=t_col[:, 4:5])
                e.tensor_mul(out=ab(o_xn2, 1024), in0=af(o_xa1, 1024),
                             in1=t_n2rep[:]).then_inc(sems["sxn2"], 1)
            S.on("vector", f_vxn2)

            S.inc(slw, 16)

            def f_x1w(e, r=r, i=i, nxn2=nxn2, slw=slw):
                W(e, "sxn2", nxn2)
                e.dma_start(out=d_x1s[r, 128 * i:128 * (i + 1), :],
                            in_=af(o_x1 + 1024 * (i % 2), 1024)).then_inc(sems[slw], 16)
            S.on("sync", f_x1w)

            for j in range(DC):
                ktr = S.inc("strC")
                g_strAe = S.at("strAe")
                g_sple_r2 = S.at("sple")

                def f_t2(e, j=j, nxn2=nxn2, ktr=ktr, xg_sy2=xg_sy2, g_strAe=g_strAe,
                         g_sple_r2=g_sple_r2):
                    if j == 0:
                        W(e, "sxn2", nxn2)
                        if ktr == 1:
                            W(e, "strAe", g_strAe)
                    if ktr > 2:
                        W(e, "sxn2e", ktr - 2)
                    if ktr <= 8:
                        W(e, "sy2", xg_sy2)
                        W(e, "sple", g_sple_r2)
                    bk = p_trb if (ktr - 1) % 2 == 0 else p_plb
                    e.transpose(bk[:, 0:128],
                                ab(o_xn2, 1024)[:, 128 * j:128 * (j + 1)],
                                t_identb[:]).then_inc(sems["strC"], 1)
                S.on("tensor", f_t2)

                S.inc("sxn2e")

                def f_ve2(e, i=i, j=j, ktr=ktr):
                    W(e, "strC", ktr)
                    bk = p_trb if (ktr - 1) % 2 == 0 else p_plb
                    e.tensor_copy(out=ab(o_xn2fm + 256 * j, 512)[:, 128 * i:128 * (i + 1)],
                                  in_=bk[:, 0:128]).then_inc(sems["sxn2e"], 1)
                S.on("vector", f_ve2)

        nxn2e_all = S.at("sxn2e")
        for dffc in range(NF):
            slot_sem = "sw1sA" if dffc % 2 == 0 else "sw1sB"
            nw1s = S.at(slot_sem) + 16
            S.inc(slot_sem, 16)
            g_w1r = S.at("sw1") - 1 if S.at("sw1") >= 1 else 0

            def f_w1s(e, r=r, dffc=dffc, g_w1r=g_w1r, slot_sem=slot_sem):
                W(e, "sw1", g_w1r)
                e.dma_start(out=ab(o_wbuf + 512 * (dffc % 2), 1024)
                            .rearrange("p (c j) -> p c j", c=DC),
                            in_=d_w1c[dffc].rearrange("(c p) j -> p c j", p=128)
                            ).then_inc(sems[slot_sem], 16)
            S.on("gpsimd", f_w1s)

            g_h1 = S.at("sh1") - 3 if S.at("sh1") >= 3 else 0
            g_sdtA = S.at("sdt")
            nw1 = S.inc("sw1")

            g_bvA = S.at("sbv")
            g_xnkA = S.at("sxnk")

            def f_w1m(e, dffc=dffc, nw1s=nw1s, nxn2e_all=nxn2e_all, g_h1=g_h1,
                      g_sdtA=g_sdtA, slot_sem=slot_sem, g_bvA=g_bvA, g_xnkA=g_xnkA):
                W(e, slot_sem, nw1s)
                if dffc == 0:
                    W(e, "sxn2e", nxn2e_all)
                    W(e, "sdt", g_sdtA)
                    W(e, "sbv", g_bvA)
                    W(e, "sxnk", g_xnkA)
                W(e, "sh1", g_h1)
                wb = ab(o_wbuf + 512 * (dffc % 2), 1024)
                bk = p_mm[dffc % 4]
                for j in range(DC - 1):
                    e.matmul(bk[:], wb[:, 128 * j:128 * (j + 1)],
                             ab(o_xn2fm + 256 * j, 512), start=(j == 0), stop=False)
                e.matmul(bk[:], wb[:, 128 * (DC - 1):128 * DC],
                         ab(o_xn2fm + 256 * (DC - 1), 512),
                         start=False, stop=True).then_inc(sems["sw1"], 1)
            S.on("tensor", f_w1m)

            g_sy2W1 = S.at("sy2")
            nh1 = S.inc("sh1")

            def f_sh1(e, dffc=dffc, nw1=nw1, g_sy2W1=g_sy2W1):
                W(e, "sw1", nw1)
                if dffc == 0:
                    W(e, "sy2", g_sy2W1)
                e.activation(out=ab(o_h1 + 256 * dffc, 512), in_=p_mm[dffc % 4][:],
                             func=AF.Gelu_apprx_tanh,
                             bias=t_b1c[:, dffc:dffc + 1]).then_inc(sems["sh1"], 1)
            S.on("scalar", f_sh1)

        nh1_all = S.at("sh1")
        g_ple_all = S.at("sple")
        g_sbv_all = S.at("sbv")
        g_sxnk_all = S.at("sxnk")
        g_sGn_all = S.at("sGn")
        g_stq_all = S.at("stq")
        g_sy2_prev = SY2_GUARD
        for dffc in range(NF):
            slot2 = "sw2sA" if dffc % 2 == 0 else "sw2sB"
            nw2s = S.at(slot2) + 16
            S.inc(slot2, 16)
            g_w2r = S.at("sw2")

            def f_w2s(e, dffc=dffc, g_w2r=g_w2r, slot2=slot2):
                if g_w2r >= 2:
                    W(e, "sw2", g_w2r - 1)
                e.dma_start(out=ab(o_w2buf + 512 * (dffc % 2), 1024),
                            in_=d_w2T[128 * dffc:128 * (dffc + 1), :]).then_inc(sems[slot2], 16)
            S.on("gpsimd", f_w2s)

            nw2 = S.inc("sw2")

            def f_w2m(e, r=r, dffc=dffc, nw2s=nw2s, nh1_all=nh1_all,
                      g_ple_all=g_ple_all, g_sbv_all=g_sbv_all, g_sxnk_all=g_sxnk_all,
                      g_sGn_all=g_sGn_all, g_stq_all=g_stq_all, g_sy2_prev=g_sy2_prev,
                      slot2=slot2):
                W(e, slot2, nw2s)
                if dffc == 0:
                    W(e, "sh1", nh1_all)
                    W(e, "sple", g_ple_all)
                    W(e, "sbv", g_sbv_all)
                    W(e, "sxnk", g_sxnk_all)
                    W(e, "sGn", g_sGn_all)
                    W(e, "stq", g_stq_all)
                    W(e, "sy2", g_sy2_prev)
                    for tb in range(4):
                        for dh in range(2):
                            e.matmul(p_mm[2 * tb + dh][:], t_onesb[:],
                                     ab(o_vrowb, 2 * D)[0:1, 512 * dh:512 * (dh + 1)],
                                     start=True, stop=False)
                wb = ab(o_w2buf + 512 * (dffc % 2), 1024)
                last = (dffc == NF - 1)
                for tb in range(4):
                    for dh in range(2):
                        mm = e.matmul(p_mm[2 * tb + dh][:],
                                      ab(o_h1 + 256 * dffc, 512)[:, 128 * tb:128 * (tb + 1)],
                                      wb[:, 512 * dh:512 * (dh + 1)],
                                      start=False, stop=last)
                        if tb == 3 and dh == 1:
                            mm.then_inc(sems["sw2"], 1)
            S.on("tensor", f_w2m)
        nw2_all = S.at("sw2")

        nx1_all = S.at("sx1")
        ny2_list = []
        for tb in range(4):
            sl3 = "sx3A" if tb % 2 == 0 else "sx3B"
            sle = "sx1eA" if tb % 2 == 0 else "sx1eB"
            g_y2a = S.at("sy2") - 2 if tb >= 2 else 0
            nx2b = S.at(sl3) + 16
            S.inc(sl3, 16)
            nx1e = S.at(sle) + 16
            S.inc(sle, 16)

            g_w1A = S.at("sx1wA")
            g_w1B = S.at("sx1wB")

            def f_rl(e, r=r, tb=tb, g_y2a=g_y2a, nx1_all=nx1_all, nw2_all=nw2_all,
                     sl3=sl3, sle=sle, g_w1A=g_w1A, g_w1B=g_w1B):
                W(e, "sx1", nx1_all)
                W(e, "sy2", g_y2a)
                W(e, "sw2", nw2_all)
                W(e, "sx1wA", g_w1A)
                W(e, "sx1wB", g_w1B)
                e.dma_start(out=af(o_xs3 + 1024 * (tb % 2), 1024),
                            in_=d_x[r, 128 * tb:128 * (tb + 1), :]).then_inc(sems[sl3], 16)
                e.dma_start(out=af(o_x1e + 1024 * (tb % 2), 1024),
                            in_=d_x1s[r, 128 * tb:128 * (tb + 1), :]).then_inc(sems[sle], 16)
            S.on("sync", f_rl)

            g_xaw = S.at("sxaw")
            ny2 = S.inc("sy2")
            ny2_list.append(ny2)

            def f_vy2(e, tb=tb, nw2_all=nw2_all, nx2b=nx2b, nx1e=nx1e, g_xaw=g_xaw,
                      sl3=sl3, sle=sle):
                W(e, "sw2", nw2_all)
                W(e, sl3, nx2b)
                W(e, sle, nx1e)
                W(e, "sxaw", g_xaw)
                for dh in range(2):
                    e.tensor_add(out=af(o_xa1 + 512 * dh, 512),
                                 in0=p_mm[2 * tb + dh][:],
                                 in1=af(o_x1e + 1024 * (tb % 2) + 512 * dh, 512))
                e.tensor_add(out=af(o_xa1, 1024), in0=af(o_xa1, 1024),
                             in1=af(o_xs3 + 1024 * (tb % 2), 1024)).then_inc(sems["sy2"], 1)
            S.on("vector", f_vy2)
            SY2_GUARD = ny2

            S.inc("sxaw", 16)

            def f_xaw(e, r=r, tb=tb, ny2=ny2):
                W(e, "sy2", ny2)
                e.dma_start(out=d_xa[r, 128 * tb:128 * (tb + 1), :],
                            in_=af(o_xa1, 1024)).then_inc(sems["sxaw"], 16)
            S.on("sync", f_xaw)

        ny2_all_r = S.at("sy2")
        for tb in range(4):
            nxar2 = S.at("sxaL") + 16
            S.inc("sxaL", 16)
            g_xf_prev = S.at("sxf")

            g_xaw_all = S.at("sxaw")

            def f_xarl(e, r=r, tb=tb, g_xf_prev=g_xf_prev, g_xaw_all=g_xaw_all):
                W(e, "sxf", g_xf_prev)
                W(e, "sxaw", g_xaw_all)
                e.dma_start(out=af(o_xa1, 1024),
                            in_=d_xa[r, 128 * tb:128 * (tb + 1), :]).then_inc(sems["sxaL"], 16)
            S.on("sync", f_xarl)

            nv3 = S.inc("svar")

            def f_vln1(e, nxar2=nxar2):
                W(e, "sxaL", nxar2)
                e.tensor_mul(out=af(o_wkA, 1024), in0=af(o_xa1, 1024), in1=af(o_xa1, 1024))
                e.reduce_sum(out=t_col[:, 5:6], in_=af(o_xa1, 1024), axis=AX)
                e.reduce_sum(out=t_col[:, 6:7], in_=af(o_wkA, 1024), axis=AX)
                e.tensor_scalar_mul(out=t_col[:, 5:6], in0=t_col[:, 5:6], scalar1=1.0 / D)
                e.tensor_mul(out=t_col[:, 2:3], in0=t_col[:, 5:6], in1=t_col[:, 5:6])
                e.scalar_tensor_tensor(out=t_col[:, 3:4], in0=t_col[:, 6:7],
                                       scalar=1.0 / D, in1=t_col[:, 2:3],
                                       op0=ALU.mult, op1=ALU.subtract).then_inc(sems["svar"], 1)
            S.on("vector", f_vln1)

            nrsq3 = S.inc("srsq3")

            def f_sln1(e, nv3=nv3):
                W(e, "svar", nv3)
                e.activation(out=t_col[:, 4:5], in_=t_col[:, 3:4], func=AF.Ln,
                             bias=t_small[:, 6:7])
                e.activation(out=t_col[:, 4:5], in_=t_col[:, 4:5], func=AF.Exp,
                             scale=-0.5).then_inc(sems["srsq3"], 1)
            S.on("scalar", f_sln1)

            g_strC = S.at("strC")
            nxf = S.inc("sxf")

            def f_vxf(e, nrsq3=nrsq3, g_strC=g_strC):
                W(e, "srsq3", nrsq3)
                W(e, "strC", g_strC)
                e.tensor_scalar(out=ab(o_xf, 1024), in0=af(o_xa1, 1024),
                                scalar1=t_col[:, 5:6], scalar2=t_col[:, 4:5],
                                op0=ALU.subtract, op1=ALU.mult).then_inc(sems["sxf"], 1)
            S.on("vector", f_vxf)

            for j in range(DC):
                ktr = S.inc("strD")

                def f_t3(e, j=j, nxf=nxf, ktr=ktr, ny2_all_r=ny2_all_r):
                    if j == 0:
                        W(e, "sxf", nxf)
                        W(e, "sy2", ny2_all_r)
                    if ktr > 2:
                        W(e, "sxfe", ktr - 2)
                    bk = p_trb if (ktr - 1) % 2 == 0 else p_plb
                    e.transpose(bk[:, 0:128],
                                ab(o_xf, 1024)[:, 128 * j:128 * (j + 1)],
                                t_identb[:]).then_inc(sems["strD"], 1)
                S.on("tensor", f_t3)

                S.inc("sxfe")

                def f_ve3(e, r=r, tb=tb, j=j, ktr=ktr):
                    W(e, "strD", ktr)
                    bk = p_trb if (ktr - 1) % 2 == 0 else p_plb
                    e.tensor_copy(out=ab(o_xffm + 1024 * j, 2048)[:, 512 * r + 128 * tb:
                                                                  512 * r + 128 * (tb + 1)],
                                  in_=bk[:, 0:128]).then_inc(sems["sxfe"], 1)
                S.on("vector", f_ve3)

        nxfe_all = S.at("sxfe")
        ngtp = S.inc("sgtp")

        def f_gt(e, r=r, nxfe_all=nxfe_all):
            W(e, "sxfe", nxfe_all)
            for j in range(DC - 1):
                e.matmul(p_mm[2][0:8, :], t_gatew[:, j, :],
                         ab(o_xffm + 1024 * j, 2048)[:, 512 * r:512 * (r + 1)],
                         start=(j == 0), stop=False)
            e.matmul(p_mm[2][0:8, :], t_gatew[:, DC - 1, :],
                     ab(o_xffm + 1024 * (DC - 1), 2048)[:, 512 * r:512 * (r + 1)],
                     start=False, stop=True).then_inc(sems["sgtp"], 1)
        S.on("tensor", f_gt)

        ngt = S.inc("sgt")

        def f_sgt(e, ngtp=ngtp):
            W(e, "sgtp", ngtp)
            e.activation(out=af(o_sc, 512)[0:8, :], in_=p_mm[2][0:8, :], func=AF.Sigmoid,
                         bias=t_small[0:8, 25:26]).then_inc(sems["sgt"], 1)
        S.on("scalar", f_sgt)

        nwv = S.inc("swv")

        def f_vtop(e, r=r, ngt=ngt):
            W(e, "sgt", ngt)
            s = [af(o_sc + 512 * k, 512)[0:32, :] for k in range(6)]
            rot = lambda sh: [(i // 8) * 8 + ((i % 8) + sh) % 8 for i in range(32)]
            e.stream_shuffle(out=s[2], in_=s[0], mask=rot(4))
            e.tensor_max(out=s[1], in0=s[0], in1=s[2])
            e.stream_shuffle(out=s[2], in_=s[1], mask=rot(2))
            e.tensor_max(out=s[1], in0=s[1], in1=s[2])
            e.stream_shuffle(out=s[2], in_=s[1], mask=rot(1))
            e.tensor_max(out=s[1], in0=s[1], in1=s[2])
            e.tensor_tensor(out=s[2], in0=s[0], in1=s[1], op=ALU.is_equal)
            e.tensor_scalar(out=s[2], in0=s[2], scalar1=t_small[0:32, 5:6], scalar2=99.0,
                            op0=ALU.mult, op1=ALU.add)
            e.stream_shuffle(out=s[3], in_=s[2], mask=rot(4))
            e.tensor_tensor(out=s[2], in0=s[2], in1=s[3], op=ALU.min)
            e.stream_shuffle(out=s[3], in_=s[2], mask=rot(2))
            e.tensor_tensor(out=s[2], in0=s[2], in1=s[3], op=ALU.min)
            e.stream_shuffle(out=s[3], in_=s[2], mask=rot(1))
            e.tensor_tensor(out=s[2], in0=s[2], in1=s[3], op=ALU.min)
            e.tensor_scalar(out=s[3], in0=s[2], scalar1=t_small[0:32, 0:1], scalar2=None,
                            op0=ALU.is_equal, op1=ALU.bypass)
            e.scalar_tensor_tensor(out=s[2], in0=s[3], scalar=-1.0e30, in1=s[0],
                                   op0=ALU.mult, op1=ALU.add)
            e.stream_shuffle(out=s[5], in_=s[2], mask=rot(4))
            e.tensor_max(out=s[4], in0=s[2], in1=s[5])
            e.stream_shuffle(out=s[5], in_=s[4], mask=rot(2))
            e.tensor_max(out=s[4], in0=s[4], in1=s[5])
            e.stream_shuffle(out=s[5], in_=s[4], mask=rot(1))
            e.tensor_max(out=s[4], in0=s[4], in1=s[5])
            e.tensor_tensor(out=s[5], in0=s[2], in1=s[4], op=ALU.is_equal)
            e.tensor_scalar(out=s[5], in0=s[5], scalar1=t_small[0:32, 5:6], scalar2=99.0,
                            op0=ALU.mult, op1=ALU.add)
            e.stream_shuffle(out=s[2], in_=s[5], mask=rot(4))
            e.tensor_tensor(out=s[5], in0=s[5], in1=s[2], op=ALU.min)
            e.stream_shuffle(out=s[2], in_=s[5], mask=rot(2))
            e.tensor_tensor(out=s[5], in0=s[5], in1=s[2], op=ALU.min)
            e.stream_shuffle(out=s[2], in_=s[5], mask=rot(1))
            e.tensor_tensor(out=s[5], in0=s[5], in1=s[2], op=ALU.min)
            e.tensor_scalar(out=s[2], in0=s[5], scalar1=t_small[0:32, 0:1], scalar2=None,
                            op0=ALU.is_equal, op1=ALU.bypass)
            e.tensor_add(out=s[5], in0=s[1], in1=s[4])
            e.tensor_scalar(out=s[5], in0=s[5], scalar1=1.0e-6, scalar2=10.0,
                            op0=ALU.add, op1=ALU.mult)
            e.reciprocal(out=s[5], in_=s[5])
            e.tensor_mul(out=s[1], in0=s[3], in1=s[1])
            e.tensor_mul(out=s[4], in0=s[2], in1=s[4])
            e.tensor_add(out=s[1], in0=s[1], in1=s[4])
            e.tensor_mul(out=af(o_wfm, 2048)[0:32, 512 * r:512 * (r + 1)],
                         in0=s[1], in1=s[5])
            e.tensor_copy(out=ab(o_wbf, 2048)[0:32, 512 * r:512 * (r + 1)],
                          in_=af(o_wfm, 2048)[0:32, 512 * r:512 * (r + 1)]).then_inc(sems["swv"], 1)
        S.on("vector", f_vtop)

        for i in range(4):
            nwtp = S.inc("swTp")

            g_sple_r = S.at("sple")

            def f_twt(e, r=r, i=i, nwv=nwv, nwtp=nwtp, g_sple_r=g_sple_r):
                if i == 0:
                    W(e, "swv", nwv)
                    W(e, "sple", g_sple_r)
                if nwtp > 1:
                    W(e, "swT", nwtp - 1)
                e.transpose(p_trf[:, 0:8],
                            af(o_wfm, 2048)[0:8, 512 * r + 128 * i:512 * r + 128 * (i + 1)],
                            t_identf[0:8, 0:8]).then_inc(sems["swTp"], 1)
            S.on("tensor", f_twt)

            S.inc("swT")

            def f_vwt(e, r=r, i=i, nwtp=nwtp):
                W(e, "swTp", nwtp)
                e.tensor_copy(out=t_wtok[:, (4 * r + i) * 8:(4 * r + i) * 8 + 8],
                              in_=p_trf[:, 0:8]).then_inc(sems["swT"], 1)
            S.on("vector", f_vwt)

    SNAP_ROUNDS = S.snapshot()
    BAR_WAITS = [(nm, S.at(nm)) for nm in
                 ["sciny", "sxaw", "sx1wA", "sx1wB", "sy2", "sxfe", "swT",
                  "sdt", "srdt", "sg", "sh", "syq", "sye", "sple", "sh1",
                  "sw1sA", "sw1sB", "sw2sA", "sw2sB", "sx2A", "sx2B",
                  "sx3A", "sx3B", "sx1eA", "sx1eB", "sxaL", "sgt", "swv",
                  "sxn2e", "sbn", "sGn", "stq", "sbv", "sxnk", "srhs",
                  "sx", "sxn", "sbcB", "sbcC", "sbvd", "sx1", "sxn2",
                  "srsq3", "sxf", "sxg2"]]
    NBAR = S.inc("setup")

    def f_bar(e):
        for nm, v in BAR_WAITS:
            W(e, nm, v)
        e.sem_inc(sems["setup"], 1)
    S.on("gpsimd", f_bar)

    first_onB = {"sync": True, "vector": True, "scalar": True, "tensor": True,
                 "gpsimd": True}

    def bwait(eng, e):
        if first_onB[eng]:
            W(e, "setup", NBAR)
            first_onB[eng] = False

    scmb_guard = 0
    for hb in range(2):
        for e_ in range(E + 1):
            g_dn_prev = S.at("sdn")
            g_cmb_prev = scmb_guard
            for dffc in range(NF):
                slotu = "supsA" if dffc % 2 == 0 else "supsB"
                nups = S.at(slotu) + 16
                S.inc(slotu, 16)
                g_upr = S.at("sup")

                def f_ups(e, hb=hb, e_=e_, dffc=dffc, g_upr=g_upr, slotu=slotu):
                    bwait("gpsimd", e)
                    if g_upr >= 2:
                        W(e, "sup", g_upr - 1)
                    src = d_swupc[dffc] if e_ == 0 else d_upwc[e_ - 1, dffc]
                    e.dma_start(out=ab(o_upb + 512 * (dffc % 2), 1024)
                                .rearrange("p (c j) -> p c j", c=DC),
                                in_=src.rearrange("(c p) j -> p c j", p=128)
                                ).then_inc(sems[slotu], 16)
                S.on("gpsimd", f_ups)

                g_ha = S.at("sha") - 1 if S.at("sha") >= 1 else 0
                nup = S.inc("sup")

                def f_upm(e, hb=hb, e_=e_, dffc=dffc, nups=nups, g_ha=g_ha,
                          g_cmb_prev=g_cmb_prev, slotu=slotu):
                    bwait("tensor", e)
                    W(e, slotu, nups)
                    W(e, "sha", g_ha)
                    if dffc <= 1:
                        W(e, "scmb", g_cmb_prev)
                    wb = ab(o_upb + 512 * (dffc % 2), 1024)
                    pbase = 4 if dffc % 2 == 0 else 2
                    for tt in range(2):
                        for j in range(DC):
                            mm = e.matmul(p_mm[pbase + tt][:], wb[:, 128 * j:128 * (j + 1)],
                                          ab(o_xffm + 1024 * j, 2048)[:, 1024 * hb + 512 * tt:
                                                                      1024 * hb + 512 * (tt + 1)],
                                          start=(j == 0), stop=(j == DC - 1))
                            if tt == 1 and j == DC - 1:
                                mm.then_inc(sems["sup"], 1)
                S.on("tensor", f_upm)

                nha = S.inc("sha")

                def f_sup(e, e_=e_, dffc=dffc, nup=nup, g_dn_prev=g_dn_prev):
                    bwait("scalar", e)
                    W(e, "sup", nup)
                    W(e, "sdn", g_dn_prev)
                    bc = t_sbupc[:, dffc:dffc + 1] if e_ == 0 else \
                        t_upbc[:, (e_ - 1) * NF + dffc:(e_ - 1) * NF + dffc + 1]
                    pbase = 4 if dffc % 2 == 0 else 2
                    for tt in range(2):
                        act = e.activation(out=ab(o_hact + 512 * dffc, 1024)[:, 512 * tt:512 * (tt + 1)],
                                           in_=p_mm[pbase + tt][:], func=AF.Gelu, bias=bc)
                        if tt == 1:
                            act.then_inc(sems["sha"], 1)
                S.on("scalar", f_sup)

            nha_all = S.at("sha")

            for dh in range(2):
                for dffc in range(NF):
                    slotd = "sdnsA" if dffc % 2 == 0 else "sdnsB"
                    ndns = S.at(slotd) + 16
                    S.inc(slotd, 16)
                    g_dnr = S.at("sdn")

                    def f_dns(e, e_=e_, dffc=dffc, dh=dh, g_dnr=g_dnr, slotd=slotd):
                        if g_dnr >= 2:
                            W(e, "sdn", g_dnr - 1)
                        src = d_swdn[128 * dffc:128 * (dffc + 1), 512 * dh:512 * (dh + 1)] \
                            if e_ == 0 else \
                            d_dnw[e_ - 1, 128 * dffc:128 * (dffc + 1), 512 * dh:512 * (dh + 1)]
                        e.dma_start(out=ab(o_dnb + 512 * (dffc % 2), 1024)[:, 0:512],
                                    in_=src).then_inc(sems[slotd], 16)
                    S.on("gpsimd", f_dns)

                    ndn = S.inc("sdn")

                    def f_dnm(e, e_=e_, hb=hb, dffc=dffc, dh=dh, ndns=ndns,
                              nha_all=nha_all, scmb_g=scmb_guard, slotd=slotd):
                        W(e, slotd, ndns)
                        if dffc == 0:
                            W(e, "sha", nha_all)
                            W(e, "scmb", scmb_g)
                            for tb in range(8):
                                if e_ == 0:
                                    e.matmul(p_mm[tb][:], t_onesb[:],
                                             ab(o_vrowb, 2 * D)[0:1, D + 512 * dh:D + 512 * (dh + 1)],
                                             start=True, stop=False)
                                else:
                                    e.matmul(p_mm[tb][:],
                                             ab(o_wbf, 2048)[0:8, 1024 * hb + 128 * tb:
                                                             1024 * hb + 128 * (tb + 1)],
                                             t_downb[:, 512 * dh:512 * (dh + 1)],
                                             start=True, stop=False)
                        wb = ab(o_dnb + 512 * (dffc % 2), 1024)[:, 0:512]
                        last = (dffc == NF - 1)
                        for tb in range(8):
                            mm = e.matmul(p_mm[tb][:],
                                          ab(o_hact + 512 * dffc, 1024)[:, 128 * tb:128 * (tb + 1)],
                                          wb, start=False, stop=last)
                            if tb == 7:
                                mm.then_inc(sems["sdn"], 1)
                    S.on("tensor", f_dnm)

                ndn_all = S.at("sdn")
                ncmb = S.inc("scmb")

                def f_vcmb(e, e_=e_, hb=hb, dh=dh, ndn_all=ndn_all):
                    bwait("vector", e)
                    W(e, "sdn", ndn_all)
                    for tb in range(8):
                        dst = af(o_comb + 1024 * tb + 512 * dh, 512)
                        if e_ == 0:
                            e.tensor_scalar_mul(out=dst, in0=p_mm[tb][:], scalar1=0.1)
                        else:
                            wcol = t_wtok[:, (8 * hb + tb) * 8 + (e_ - 1):
                                          (8 * hb + tb) * 8 + e_]
                            e.scalar_tensor_tensor(out=dst, in0=p_mm[tb][:], scalar=wcol,
                                                   in1=dst, op0=ALU.mult, op1=ALU.add)
                    e.tensor_copy(out=t_col[:, 7:8], in_=t_col[:, 7:8]).then_inc(sems["scmb"], 1)
                S.on("vector", f_vcmb)
                scmb_guard = ncmb

        for tb in range(8):
            r_ = 2 * hb + tb // 4
            i_ = tb % 4
            slx = "sxarA" if tb % 2 == 0 else "sxarB"
            nxar = S.at(slx) + 16
            S.inc(slx, 16)
            g_fin = S.at("sfin") - 1 if S.at("sfin") >= 1 else 0

            def f_xar(e, r_=r_, i_=i_, tb=tb, g_fin=g_fin, slx=slx):
                bwait("sync", e)
                if tb >= 2:
                    W(e, "sfin", g_fin)
                e.dma_start(out=af(o_xar + 1024 * (tb % 2), 1024),
                            in_=d_xa[r_, 128 * i_:128 * (i_ + 1), :]).then_inc(sems[slx], 16)
            S.on("sync", f_xar)

            ncmb_all = S.at("scmb")
            nfin = S.inc("sfin")

            g_out_prev = S.at("sout")

            def f_vfin(e, tb=tb, nxar=nxar, ncmb_all=ncmb_all, slx=slx,
                       g_out_prev=g_out_prev):
                W(e, "scmb", ncmb_all)
                W(e, slx, nxar)
                W(e, "sout", g_out_prev)
                cmb = af(o_comb + 1024 * tb, 1024)
                e.reduce_max(out=t_col[:, 0:1], in_=cmb, axis=AX, apply_absolute_value=True)
                e.tensor_scalar_add(out=t_col[:, 0:1], in0=t_col[:, 0:1], scalar1=1.0e-6)
                e.reciprocal(out=t_col[:, 1:2], in_=t_col[:, 0:1])
                e.tensor_scalar_mul(out=af(o_fin, 1024), in0=cmb, scalar1=t_col[:, 1:2])
                e.tensor_add(out=af(o_fin, 1024), in0=af(o_fin, 1024),
                             in1=af(o_xar + 1024 * (tb % 2), 1024)).then_inc(sems["sfin"], 1)
            S.on("vector", f_vfin)

            S.inc("sout", 16)

            def f_out(e, r_=r_, i_=i_, nfin=nfin):
                W(e, "sfin", nfin)
                e.dma_start(out=d_out[r_, 128 * i_:128 * (i_ + 1), :],
                            in_=af(o_fin, 1024)).then_inc(sems["sout"], 16)
            S.on("sync", f_out)

    SNAP_FULL = S.snapshot()
    import os as _os
    if _os.environ.get("PRINT_SNAPS"):
        print("SNAP_XN", SNAP_XN[0], "ROUNDS", [r[0] for r in ROUND_SNAPS],
              "SNAP_ROUNDS", SNAP_ROUNDS[0], "FULL", SNAP_FULL[0])
    if stage is not None:
        snaps = {"xn": SNAP_XN, "r1": ROUND_SNAPS[1] if len(ROUND_SNAPS) > 1 else SNAP_ROUNDS,
                 "r2": ROUND_SNAPS[2] if len(ROUND_SNAPS) > 2 else SNAP_ROUNDS,
                 "r3": ROUND_SNAPS[3] if len(ROUND_SNAPS) > 3 else SNAP_ROUNDS,
                 "rounds": SNAP_ROUNDS, "full": SNAP_FULL}
        if isinstance(stage, str) and stage.startswith("cut:"):
            gcut = int(stage.split(":")[1])
            cnts = S.cnt_log[gcut - 1] if 0 < gcut <= len(S.cnt_log) else (
                dict(S.cnt) if gcut > 0 else {})
        else:
            gcut, cnts = snaps[stage]
        for k in S.q:
            S.q[k] = [(g, f) for (g, f) in S.q[k] if g < gcut]

        def f_endS(e, cnts=cnts):
            for nm, v in cnts.items():
                W(e, nm, v)
        S.on("gpsimd", f_endS)
    else:
        NOUT = S.at("sout")

        def f_end(e):
            W(e, "sout", NOUT)
        S.on("gpsimd", f_end)

    class DrainProxy:
        _PASS = {"wait_ge", "sem_inc", "drain", "nop"}

        def __init__(self, eng):
            self._e = eng

        def __getattr__(self, name):
            fn = getattr(self._e, name)
            if name in DrainProxy._PASS or not callable(fn):
                return fn

            def wrapped(*a, **k):
                self._e.drain()
                return fn(*a, **k)
            return wrapped

    @blk.sync
    def _(e):
        for _, fn in S.q["sync"]:
            fn(e)

    @blk.vector
    def _(e):
        p = DrainProxy(e)
        for _, fn in S.q["vector"]:
            fn(p)

    @blk.scalar
    def _(e):
        p = DrainProxy(e)
        for _, fn in S.q["scalar"]:
            fn(p)

    @blk.tensor
    def _(e):
        for _, fn in S.q["tensor"]:
            fn(e)

    @blk.gpsimd
    def _(e):
        p = DrainProxy(e)
        for _, fn in S.q["gpsimd"]:
            fn(p)

    ctx.close()
    return nc


_NC_CACHE = [None]


def kernel(L0_attn_x, L0_attn_norm1_w, L0_attn_norm2_w, L0_attn_A_log, L0_attn_D_param,
           L0_attn_scale, L0_attn_Wdt, L0_attn_bdt, L0_attn_WB, L0_attn_bB, L0_attn_WC,
           L0_attn_bC, L0_attn_W1, L0_attn_b1, L0_attn_W2, L0_attn_b2, L0_ffn_gate_w,
           L0_ffn_bias, L0_ffn_up_w, L0_ffn_up_b, L0_ffn_down_w, L0_ffn_down_b,
           L0_ffn_sw_up, L0_ffn_sb_up, L0_ffn_sw_down, L0_ffn_sb_down):
    x = np.asarray(L0_attn_x, dtype=np.float32)
    WdtT = np.ascontiguousarray(np.asarray(L0_attn_Wdt, np.float32).T)
    wbcT = np.ascontiguousarray(
        np.concatenate([np.asarray(L0_attn_WB, np.float32).T,
                        np.asarray(L0_attn_WC, np.float32).T], axis=1))
    w1c = np.ascontiguousarray(
        np.asarray(L0_attn_W1, np.float32).T.reshape(D, NF, 128).transpose(1, 0, 2))
    w2T = np.ascontiguousarray(np.asarray(L0_attn_W2, np.float32).T)
    swupc = np.ascontiguousarray(
        np.asarray(L0_ffn_sw_up, np.float32).reshape(D, NF, 128).transpose(1, 0, 2))
    swdn = np.ascontiguousarray(np.asarray(L0_ffn_sw_down, np.float32))
    upwc = np.ascontiguousarray(
        np.asarray(L0_ffn_up_w, np.float32).reshape(E, D, NF, 128).transpose(0, 2, 1, 3))
    dnw = np.ascontiguousarray(np.asarray(L0_ffn_down_w, np.float32))
    gatew = np.ascontiguousarray(np.asarray(L0_ffn_gate_w, np.float32))
    vecs = np.concatenate([np.asarray(L0_attn_norm1_w, np.float32),
                           np.asarray(L0_attn_norm2_w, np.float32),
                           np.asarray(L0_attn_b2, np.float32),
                           np.asarray(L0_ffn_sb_down, np.float32)])[None, :]
    b1cols = np.ascontiguousarray(np.asarray(L0_attn_b1, np.float32).reshape(NF, 128).T)
    sbupcols = np.ascontiguousarray(np.asarray(L0_ffn_sb_up, np.float32).reshape(NF, 128).T)
    upbcols = np.ascontiguousarray(
        np.asarray(L0_ffn_up_b, np.float32).reshape(E, NF, 128).transpose(2, 0, 1)
        .reshape(128, E * NF))
    downb = np.ascontiguousarray(np.asarray(L0_ffn_down_b, np.float32))
    identb = np.eye(128).astype(ml_dtypes.bfloat16)
    identf = np.eye(128, dtype=np.float32)

    if _NC_CACHE[0] is None:
        _NC_CACHE[0] = build_nc()
    nc = _NC_CACHE[0]

    in_maps = []
    for c in range(NC):
        s_, q_ = c // 4, c % 4
        x_my = np.stack([x[s_, 2048 * r + TS * q_:2048 * r + TS * (q_ + 1), :]
                         for r in range(R)])
        small = np.zeros((128, 32), np.float32)
        small[0, 9:25] = np.asarray(L0_attn_A_log, np.float32)
        small[0:32, 0] = np.arange(32, dtype=np.float32)
        small[0:8, 25] = np.asarray(L0_ffn_bias, np.float32)
        small[:, 1] = np.asarray(L0_attn_bdt, np.float32)[128 * c:128 * (c + 1)]
        small[:, 2] = np.asarray(L0_attn_D_param, np.float32)[128 * c:128 * (c + 1)]
        small[:, 3] = np.asarray(L0_attn_scale, np.float32)[128 * c:128 * (c + 1)]
        small[0:16, 4] = np.asarray(L0_attn_bB, np.float32)
        small[16:32, 4] = np.asarray(L0_attn_bC, np.float32)
        small[0:32, 5] = np.arange(32, dtype=np.float32) - 99.0
        small[:, 6] = 1.0e-5
        small[:, 7] = 1.0e-11
        small[:, 8] = 1.0e-6
        sel = np.zeros((D, 128), np.float32)
        sel[128 * c:128 * (c + 1), :] = np.eye(128, dtype=np.float32)
        selT = np.zeros((128, D), np.float32)
        selT[:, 128 * c:128 * (c + 1)] = np.eye(128, dtype=np.float32)
        in_maps.append({
            "x_my": np.ascontiguousarray(x_my),
            "wdtTk": np.ascontiguousarray(WdtT[:, 128 * c:128 * (c + 1)]),
            "wbcT": wbcT,
            "sel": sel.astype(ml_dtypes.bfloat16),
            "selT": selT.astype(ml_dtypes.bfloat16),
            "identb": identb, "identf": identf,
            "small": small, "vecs": vecs,
            "b1c": b1cols, "sbupc": sbupcols, "upbc": upbcols, "downb": downb,
            "gatew": gatew, "w1c": w1c, "w2T": w2T, "swupc": swupc, "swdn": swdn,
            "upwc": upwc, "dnw": dnw,
        })

    res = run_bass_kernel_spmd(nc, in_maps, core_ids=list(range(NC)))
    out = np.zeros((2, 8192, D), np.float32)
    for c in range(NC):
        s_, q_ = c // 4, c % 4
        om = res.results[c]["out_my"].reshape(R, TS, D)
        for r in range(R):
            out[s_, 2048 * r + TS * q_:2048 * r + TS * (q_ + 1), :] = om[r]
    return out
